# version 15
# baseline (speedup 1.0000x reference)
import math
import numpy as np
import ml_dtypes
import concourse.bass as bass
import concourse.mybir as mybir
from concourse.bass_utils import run_bass_kernel_spmd

F32 = mybir.dt.float32
BF16 = mybir.dt.bfloat16
AF = mybir.ActivationFunctionType
ALU = mybir.AluOpType
AX = mybir.AxisListType

NC = 8
D = 4096
T = 4096
B = 2
NT = B * T
TLOC = NT // NC
HD = 128
NQH = 4
DQ = NQH * HD
H = 11008
HPAD = 11264
HBAND = HPAD // NC
SCALE = 1.0 / math.sqrt(HD)
EPS = 1e-5
MVAL = -1e30
TCH = 512

NF4 = np.array([-1.0, -0.6961928009986877, -0.5250730514526367,
                -0.39491748809814453, -0.28444138169288635,
                -0.18477343022823334, -0.09105003625154495, 0.0,
                0.07958029955625534, 0.16093020141124725,
                0.24611230194568634, 0.33791524171829224,
                0.44070982933044434, 0.5626170039176941,
                0.7229568362236023, 1.0], dtype=np.float32)
NF4_B = ((NF4[:-1] + NF4[1:]) * np.float32(0.5)).astype(np.float32)
NF4_DH = ((NF4[1:] - NF4[:-1]) * np.float32(0.5)).astype(np.float32)
NF4_K0 = float(np.float32(NF4[0] + np.float64(NF4_DH.astype(np.float64).sum())))


class Prog:
    ENGS = ("sync", "gpsimd", "tensor", "vector", "scalar")

    def __init__(self, nc):
        self.nc = nc
        self.q = {e: [] for e in self.ENGS}
        self.sems = {}

    def _sem(self, name):
        if name not in self.sems:
            self.sems[name] = [self.nc.alloc_semaphore(name), 0]
        return self.sems[name]

    def val(self, name):
        return self._sem(name)[1]

    def step(self, eng, build, waits=(), sig=None, inc=1):
        tgt, s = None, None
        if sig is not None:
            s = self._sem(sig)
            s[1] += inc
            tgt = s[1]
        wl = tuple((self._sem(n)[0], v) for (n, v) in waits if v > 0)

        def f(e, wl=wl, build=build, s=s, inc=inc):
            for (hh, v) in wl:
                e.wait_ge(hh, v)
            ins = build(e)
            if s is not None:
                ins.then_inc(s[0], inc)
        self.q[eng].append(f)
        return tgt

    def fstep(self, eng, build, waits=(), sig=None, inc=1):
        s = sig if sig is not None else ("F_" + eng)
        v = self.step(eng, build, waits, s, inc)
        self.wait(eng, [(s, v)])
        return v

    def wait(self, eng, waits):
        wl = tuple((self._sem(n)[0], v) for (n, v) in waits if v > 0)

        def f(e, wl=wl):
            for (hh, v) in wl:
                e.wait_ge(hh, v)
        self.q[eng].append(f)

    def dma(self, eng, out, in_, waits=(), sig="dma"):
        return self.step(eng, lambda e, o=out, i=in_: e.dma_start(out=o, in_=i),
                         waits=waits, sig=sig, inc=16)


class Arena:
    def __init__(self, nc, base, top, tag):
        self.nc, self.base, self.top, self.off = nc, base, top, base
        self.tag, self.n = tag, 0

    def alloc(self, shape, dtype):
        nb = int(np.prod(shape[1:])) * mybir.dt.size(dtype)
        off = (self.off + 31) & ~31
        assert off + nb <= self.top, f"sbuf overflow {off}+{nb}>{self.top}"
        self.off = off + nb
        self.n += 1
        return self.nc.alloc_sbuf_tensor_at(
            f"{self.tag}{self.n}", list(shape), dtype, offset=off).ap()

    def mark(self):
        return self.off

    def reset(self, m):
        self.off = m


def _rope_tables():
    base = 500000.0
    inv = 1.0 / base ** (np.arange(0, HD, 2, dtype=np.float32) / HD)
    factor, low_f, high_f, old_ctx = 32.0, 1.0, 4.0, 8192
    low_wl, high_wl = old_ctx / low_f, old_ctx / high_f
    wl = 2 * math.pi / inv
    inv_l = np.where(wl > low_wl, inv / factor, inv)
    smooth = (old_ctx / wl - low_f) / (high_f - low_f)
    smoothed = (1 - smooth) * inv_l / factor + smooth * inv_l
    med = ~(wl < high_wl) & ~(wl > low_wl)
    freq = np.where(med, smoothed, inv_l).astype(np.float32)
    ang = np.arange(T, dtype=np.float32)[:, None] * freq[None, :]
    emb = np.concatenate([ang, ang], axis=1)
    cos = np.cos(emb).astype(np.float32)
    sinS = np.sin(emb).astype(np.float32)
    sinS[:, :HD // 2] *= -1.0
    return cos, sinS


DEBUG = False


def build_program():
    nc = bass.Bass("TRN2", target_bir_lowering=False)
    p = Prog(nc)

    def din(name, shape, dt=F32):
        return nc.dram_tensor(name, list(shape), dt, kind="ExternalInput").ap()

    x_in = din("x_c", [TLOC, D])
    wqt_in = din("wqt_c", [D, DQ])
    wkt_in = din("wkt_c", [D, HD])
    wvt_in = din("wvt_c", [D, HD])
    wot_in = din("wot_c", [DQ, D])
    w1_in = din("w1_c", [HBAND, D])
    w2_in = din("w2_c", [HBAND, D])
    w3_in = din("w3_c", [D, HBAND])
    cos_in = din("cos_t", [T, HD])
    sin_in = din("sinS_t", [T, HD])
    mask_in = din("mask_t", [4, 128, 512])
    id_in = din("ident_t", [128, 128], BF16)
    nf4b_in = din("nf4b_t", [128, 15])

    out_d = nc.dram_tensor("out_c", [TLOC, D], F32, kind="ExternalOutput").ap()

    def dint(name, shape, dt):
        return nc.dram_tensor(name, list(shape), dt, kind="Internal").ap()

    b_ln1 = dint("b_ln1", [128, TLOC // 128, 32, 128], BF16)
    g_ln1 = dint("g_ln1", [NC, 128, TLOC // 128, 32, 128], BF16)
    b_w1 = dint("b_w1", [128, HBAND // 128, 32, 128], BF16)
    b_w2 = dint("b_w2", [128, HBAND // 128, 32, 128], BF16)
    b_w3 = dint("b_w3", [8, 128, HBAND // 128, 512], BF16)
    g_w1 = dint("g_w1", [NC, 128, HBAND // 128, 32, 128], BF16)
    g_w2 = dint("g_w2", [NC, 128, HBAND // 128, 32, 128], BF16)
    g_w3 = dint("g_w3", [NC, 8, 128, HBAND // 128, 512], BF16)
    b_wo = dint("b_wo", [DQ, D], BF16)
    g_wo = dint("g_wo", [NC, DQ, D], BF16)
    b_oT = dint("b_oT", [DQ, NT], BF16)
    g_oT = dint("g_oT", [NC, DQ, NT], BF16)
    d_dq = dint("d_dq", [76, 128, 2048], BF16)
    d_rr = dint("d_rr", [64, 512], F32)
    s_y = dint("s_y", [TLOC, D], F32)
    s_x1 = dint("s_x1", [TLOC, D], F32)
    s_ln2 = dint("s_ln2", [D, TLOC], BF16)
    s_y2 = dint("s_y2", [TLOC, D], F32)

    PS = nc.alloc_psum_tensor("PS", [128, 4096], F32).ap()
    PSB = PS.bitcast(BF16)

    def bank(i):
        return PS[:, i * 512:(i + 1) * 512]

    def bankb(i):
        return PSB[:, i * 1024:i * 1024 + 128]

    SB_TOP = 222 * 1024
    pers = Arena(nc, 24 * 1024, 34 * 1024, "pe")
    ar = Arena(nc, 34 * 1024, SB_TOP, "ar")

    ident = pers.alloc([128, 128], BF16)
    ones_b = pers.alloc([128, 1], BF16)
    ones_f = pers.alloc([1, 128], F32)
    masks = pers.alloc([128, 4, 512], F32)
    small = pers.alloc([128, 16], F32)
    sm2 = pers.alloc([128, 16], F32)
    nrm = pers.alloc([1, 16], F32)
    bias_sb = pers.alloc([128, 1], F32)
    mean8 = pers.alloc([128, 8], F32)
    msq8 = pers.alloc([128, 8], F32)
    stm = pers.alloc([128, 8], F32)
    str_ = pers.alloc([128, 8], F32)
    st6 = pers.alloc([128, 48], F32)
    nf4b = pers.alloc([128, 15], F32)
    dh_sb = pers.alloc([128, 15], BF16)

    _pid = {}

    def pid_of(e):
        if id(e) not in _pid:
            _pid[id(e)] = bass.ts(e.partition_id(), TLOC)
        return _pid[id(e)]

    p.q["sync"].append(lambda e: pid_of(e))

    p.dma("sync", ident, id_in, sig="cst")
    p.dma("sync", masks, mask_in.transpose([1, 0, 2]), sig="cst")
    p.dma("sync", nf4b, nf4b_in, sig="cst")
    p.step("gpsimd", lambda e: e.memset(ones_b, 1.0), sig="cst")
    p.step("gpsimd", lambda e: e.memset(ones_f, 1.0), sig="cst")
    for j in range(15):
        p.step("gpsimd", lambda e, j=j: e.memset(
            dh_sb[:, j:j + 1], float(NF4_DH[j])), sig="cst")
    CST = p.val("cst")

    m0 = ar.mark()
    q_sb = ar.alloc([128, 4 * NT], BF16)
    k_sb = ar.alloc([128, NT], BF16)
    v_sb = ar.alloc([128, NT], BF16)
    attb = [ar.alloc([128, 512], BF16) for _ in range(4)]
    otst = [ar.alloc([128, 512], BF16) for _ in range(2)]
    rs_sm = ar.alloc([1, 512], F32)
    rbb = [ar.alloc([128, 512], F32) for _ in range(2)]
    sqb = ar.alloc([128, 512], BF16)
    mSh = ar.mark()
    arD = Arena(nc, mSh, SB_TOP, "dq")
    CW = 2048
    wbuf = [arD.alloc([128, CW], F32) for _ in range(2)]
    ubuf = arD.alloc([128, CW], F32)
    plane = [arD.alloc([128, CW], BF16) for _ in range(2)]
    accb = [arD.alloc([128, CW], F32) for _ in range(2)]
    dqb = [arD.alloc([128, CW], BF16) for _ in range(2)]
    tstage = [arD.alloc([128, 16 * 128], BF16) for _ in range(2)]
    amax = arD.alloc([128, 32], F32)
    rcp = arD.alloc([128, 32], F32)
    cstb = arD.alloc([128, D], BF16)

    xb_ln = [ar.alloc([128, D], F32) for _ in range(2)]
    lnb = ar.alloc([128, D], BF16)
    tstage_l = ar.alloc([128, 32 * 128], BF16)

    def ln_prep(xsrc):
        for ci6 in range(8):
            p.fstep("vector", lambda e, x=xsrc, ci6=ci6: e.bn_stats(
                st6[:, ci6 * 6:(ci6 + 1) * 6],
                x[:, ci6 * 512:(ci6 + 1) * 512]))
        p.fstep("vector", lambda e: e.bn_aggr(small[:, 0:2], st6))
        sv = p.step("vector", lambda e: e.tensor_scalar_add(
            sm2[:, 0:1], small[:, 1:2], EPS), sig="ln_v")
        av = p.step("scalar", lambda e: e.activation(
            out=sm2[:, 1:2], in_=sm2[:, 0:1], func=AF.Sqrt),
            waits=[("ln_v", sv)], sig="ln_a")
        p.wait("vector", [("ln_a", av)])
        p.fstep("vector", lambda e: e.reciprocal(sm2[:, 5:6], sm2[:, 1:2]))
        p.fstep("vector", lambda e: e.tensor_scalar_mul(
            sm2[:, 4:5], small[:, 0:1], -1.0))

    for i in range(TLOC // 128):
        x_t = xb_ln[i % 2]
        lw = [] if i < 2 else [("ln_v", p.val("ln_v") - 1)]
        lv = p.dma("sync", x_t, x_in[i * 128:(i + 1) * 128, :], waits=lw,
                   sig="x_ld")
        p.wait("vector", [("x_ld", lv)])
        ln_prep(x_t)
        lnv = p.step("vector", lambda e, x=x_t: e.tensor_scalar(
            out=lnb, in0=x, scalar1=sm2[:, 4:5], scalar2=sm2[:, 5:6],
            op0=ALU.add, op1=ALU.mult),
            waits=[("l1_t", p.val("l1_t"))], sig="ln_v")
        for k in range(32):
            bnk = bankb(k % 2)
            tw = [("ln_v", lnv)]
            if k >= 2:
                tw.append(("l1_e", p.val("l1_e") - 1))
            else:
                tw.append(("ln1_d", p.val("ln1_d")))
            tv = p.step("tensor", lambda e, k=k, bnk=bnk: e.transpose(
                bnk, lnb[:, k * 128:(k + 1) * 128], ident),
                waits=tw, sig="l1_t")
            p.step("vector", lambda e, k=k, bnk=bnk: e.tensor_copy(
                tstage_l[:, k * 128:(k + 1) * 128], bnk),
                waits=[("l1_t", tv)], sig="l1_e")
        p.dma("sync", b_ln1[:, i, :, :],
              tstage_l.rearrange("p (k r) -> p k r", r=128),
              waits=[("l1_e", p.val("l1_e"))], sig="ln1_d")

    agln1 = p.step("gpsimd", lambda e: e.collective_compute(
        "AllGather", ALU.bypass, replica_groups=[list(range(NC))],
        ins=[b_ln1.rearrange("p a b c -> (p a b) c")],
        outs=[g_ln1.rearrange("n p a b c -> (n p a b) c")]),
        waits=[("ln1_d", p.val("ln1_d"))], sig="cc")
    LND = p.val("ln1_d")

    ar.reset(mSh)
    wq_sb = ar.alloc([128, 32 * DQ], BF16)
    wkv_sb = ar.alloc([128, 32 * 256], BF16)
    lnch = [ar.alloc([128, 32 * 128], BF16) for _ in range(2)]
    costl = [ar.alloc([128, 128], F32) for _ in range(2)]
    sintl = [ar.alloc([128, 128], F32) for _ in range(2)]
    ropeA = ar.alloc([128, 640], F32)
    ropeB = ar.alloc([128, 640], F32)
    qkr = ar.alloc([128, 640], BF16)

    wsb_w = [("ln1_d", LND)]
    p.dma("gpsimd", wq_sb.rearrange("p (k c) -> p k c", c=DQ),
          wqt_in.rearrange("(k p) c -> p k c", p=128), waits=wsb_w, sig="w_sb")
    p.dma("gpsimd", wkv_sb.rearrange("p (k c) -> p k c", c=256)[:, :, 0:128],
          wkt_in.rearrange("(k p) c -> p k c", p=128), sig="w_sb")
    wsb = p.dma("gpsimd", wkv_sb.rearrange("p (k c) -> p k c", c=256)
                [:, :, 128:256],
                wvt_in.rearrange("(k p) c -> p k c", p=128), sig="w_sb")
    p.wait("tensor", [("w_sb", wsb), ("cc", agln1)])
    p.wait("sync", [("cc", agln1)])
    p.wait("vector", [("w_sb", wsb)])

    for tt in range(NT // 128):
        bandi, tl = tt // 8, (tt % 8) * 128
        lc = lnch[tt % 2]
        lw = [("q_mm2", p.val("q_mm2") - 1)] if tt >= 2 else []
        p.dma("sync", lc.rearrange("p (k c) -> p k c", c=128),
              g_ln1[bandi][:, (tt % 8), :, :], waits=lw, sig="q_ld")
        ct, st_ = costl[tt % 2], sintl[tt % 2]
        pos = (tt % 32) * 128
        rw = [("q_rp", p.val("q_rp") - 2)] if tt >= 2 else []
        p.dma("sync", ct, cos_in[pos:pos + 128, :], waits=rw, sig="q_ld")
        cv = p.dma("sync", st_, sin_in[pos:pos + 128, :], sig="q_ld")
        tw = [("q_ld", cv)]
        if tt >= 1:
            tw.append(("q_rp", p.val("q_rp")))
        p.wait("tensor", tw)
        for Dc in range(32):
            p.step("tensor", lambda e, Dc=Dc, lc=lc: e.matmul(
                bank(4), lc[:, Dc * 128:(Dc + 1) * 128],
                wq_sb[:, Dc * DQ:(Dc + 1) * DQ],
                start=(Dc == 0), stop=(Dc == 31)))
            p.step("tensor", lambda e, Dc=Dc, lc=lc: e.matmul(
                PS[:, 5 * 512:5 * 512 + 256], lc[:, Dc * 128:(Dc + 1) * 128],
                wkv_sb[:, Dc * 256:(Dc + 1) * 256],
                start=(Dc == 0), stop=(Dc == 31)),
                sig="q_mm2" if Dc == 31 else None)
        mmv = p.val("q_mm2")
        qk_ps = PS[:, 4 * 512:4 * 512 + 640]
        p.step("vector", lambda e, ct=ct: e.tensor_tensor(
            out=ropeA.rearrange("p (h c) -> p h c", c=128),
            in0=qk_ps.rearrange("p (h c) -> p h c", c=128),
            in1=ct.unsqueeze(1).broadcast_to([128, 5, 128]), op=ALU.mult),
            waits=[("q_mm2", mmv)])
        qk4 = qk_ps.rearrange("p (h s c) -> p h s c", s=2, c=64)
        rB4 = ropeB.rearrange("p (h s c) -> p h s c", s=2, c=64)
        p.step("vector", lambda e, st_=st_: e.tensor_tensor(
            out=rB4[:, :, 0, :], in0=qk4[:, :, 1, :],
            in1=st_[:, 0:64].unsqueeze(1).broadcast_to([128, 5, 64]),
            op=ALU.mult))
        p.step("vector", lambda e, st_=st_: e.tensor_tensor(
            out=rB4[:, :, 1, :], in0=qk4[:, :, 0, :],
            in1=st_[:, 64:128].unsqueeze(1).broadcast_to([128, 5, 64]),
            op=ALU.mult))
        rv = p.step("vector", lambda e: e.tensor_tensor(
            out=qkr, in0=ropeA, in1=ropeB, op=ALU.add),
            waits=[("q_tr", p.val("q_tr"))], sig="q_rp")
        p.step("vector", lambda e, tt=tt: e.tensor_copy(
            v_sb[:, tt * 128:(tt + 1) * 128],
            PS[:, 5 * 512 + 128:5 * 512 + 256]), sig="q_rp")
        for hh in range(5):
            bnk = bankb(6 + hh % 2)
            tw2 = [("q_rp", rv), ("q_ev", p.val("q_ev") - 1)]
            tv = p.step("tensor", lambda e, hh=hh, bnk=bnk: e.transpose(
                bnk, qkr[:, hh * 128:(hh + 1) * 128], ident),
                waits=tw2, sig="q_tr")
            dst = (q_sb[:, hh * NT + tt * 128:hh * NT + (tt + 1) * 128]
                   if hh < 4 else k_sb[:, tt * 128:(tt + 1) * 128])
            p.step("vector", lambda e, dst=dst, bnk=bnk: e.tensor_copy(
                dst, bnk), waits=[("q_tr", tv)], sig="q_ev")

    QEV = p.val("q_ev")
    GPSF = (3, 7, 11, 14)

    dq_jobs = []
    for i in range(HBAND // 128):
        for jh in range(2):
            dq_jobs.append(("w1", i, jh))
    for i in range(HBAND // 128):
        for jh in range(2):
            dq_jobs.append(("w2", i, jh))
    for i in range(D // 128):
        dq_jobs.append(("w3", i, 0))

    def deq_units():
        for i in range(4):
            cw = [("q_ev", QEV)] if i == 0 else [("wo_c", p.val("wo_c"))]
            lv = p.dma("gpsimd", cstb, wot_in[i * 128:(i + 1) * 128, :],
                       waits=cw, sig="wo_l")
            yield
            p.dma("gpsimd", b_wo[i * 128:(i + 1) * 128, :], cstb,
                  waits=[("wo_l", lv)], sig="wo_c")
            yield
        for (ui, (mat, i, jh)) in enumerate(dq_jobs):
            if mat == "w3":
                width, sigout = HBAND, "d_w3"
                src_ap = w3_in[i * 128:(i + 1) * 128, :]
            else:
                srcT = w1_in if mat == "w1" else w2_in
                width, sigout = CW, ("d_w1" if mat == "w1" else "d_w2")
                src_ap = srcT[i * 128:(i + 1) * 128, jh * CW:(jh + 1) * CW]
            G = width // 64
            nsub = width // 128
            wb = wbuf[ui % 2]
            lw = [("dq_u", p.val("dq_u") - 1)]
            if ui <= 1:
                lw.append(("q_ev", QEV))
            lv = p.dma("sync", wb[:, :width], src_ap, waits=lw, sig="dq_ld")
            yield
            p.fstep("vector", lambda e, wb=wb, G=G, width=width:
                    e.tensor_reduce(
                out=amax[:, :G],
                in_=wb[:, :width].rearrange("p (g k) -> p g k", k=64),
                axis=AX.X, op=ALU.max, apply_absolute_value=True),
                waits=[("dq_ld", lv), ("dq_q", p.val("dq_q"))])
            yield
            p.fstep("vector", lambda e, G=G: e.tensor_scalar_max(
                rcp[:, :G], amax[:, :G], 1e-38),
                waits=[("dq_u", p.val("dq_u"))])
            rcv = p.fstep("vector", lambda e, G=G: e.reciprocal(
                rcp[:, :G], rcp[:, :G]), sig="dq_rc")
            yield
            uv = p.step("gpsimd", lambda e, wb=wb, G=G, width=width:
                        e.tensor_tensor(
                out=ubuf[:, :width].rearrange("p (g k) -> p g k", k=64),
                in0=wb[:, :width].rearrange("p (g k) -> p g k", k=64),
                in1=rcp[:, :G].unsqueeze(2).broadcast_to([128, G, 64]),
                op=ALU.mult),
                waits=[("dq_rc", rcv), ("dq_p", p.val("dq_p"))], sig="dq_u")
            yield
            for j in range(15):
                pl = plane[j % 2]
                sw = [("dq_u", uv)]
                if j >= 2:
                    sw.append(("dq_f", p.val("dq_f") - 1))
                sv = p.step("scalar", lambda e, pl=pl, j=j, width=width:
                            e.activation(out=pl[:, :width],
                                         in_=ubuf[:, :width],
                                         func=AF.Sign, bias=nf4b[:, j:j + 1],
                                         scale=1.0),
                            waits=sw, sig="dq_p")
                if j == 0:
                    p.step("vector", lambda e, pl=pl, width=width:
                           e.tensor_scalar(
                        out=accb[0][:, :width], in0=pl[:, :width],
                        scalar1=float(NF4_DH[0]), scalar2=float(NF4_K0),
                        op0=ALU.mult, op1=ALU.add),
                        waits=[("dq_p", sv)], sig="dq_f")
                elif j in GPSF:
                    p.step("gpsimd", lambda e, pl=pl, j=j, width=width:
                           e.tensor_tensor(
                        out=pl[:, :width].rearrange("p (a c) -> p a c", a=1),
                        in0=pl[:, :width].rearrange("p (a c) -> p a c", a=1),
                        in1=dh_sb[:, j:j + 1].unsqueeze(2)
                        .broadcast_to([128, 1, width]),
                        op=ALU.mult), waits=[("dq_p", sv)])
                    a_in, a_out = accb[(j + 1) % 2], accb[j % 2]
                    p.step("gpsimd", lambda e, pl=pl, a_in=a_in, a_out=a_out,
                           width=width: e.tensor_tensor(
                        out=a_out[:, :width], in0=pl[:, :width],
                        in1=a_in[:, :width], op=ALU.add),
                        waits=[("dq_f", p.val("dq_f"))], sig="dq_f")
                else:
                    a_in, a_out = accb[(j + 1) % 2], accb[j % 2]
                    p.step("vector", lambda e, pl=pl, a_in=a_in, a_out=a_out,
                           dh=float(NF4_DH[j]), width=width:
                           e.scalar_tensor_tensor(
                        out=a_out[:, :width], in0=pl[:, :width], scalar=dh,
                        in1=a_in[:, :width], op0=ALU.mult, op1=ALU.add),
                        waits=[("dq_p", sv), ("dq_f", p.val("dq_f"))],
                        sig="dq_f")
                yield
            dqu = dqb[ui % 2]
            qv = p.step("gpsimd", lambda e, dqu=dqu, G=G, width=width:
                        e.tensor_tensor(
                out=dqu[:, :width].rearrange("p (g k) -> p g k", k=64),
                in0=accb[0][:, :width].rearrange("p (g k) -> p g k", k=64),
                in1=amax[:, :G].unsqueeze(2).broadcast_to([128, G, 64]),
                op=ALU.mult),
                waits=[("dq_f", p.val("dq_f")), ("dq_s", p.val("dq_s") - 1)],
                sig="dq_q")
            yield
            p.dma("sync", d_dq[ui, :, :width], dqu[:, :width],
                  waits=[("dq_q", qv)], sig="dq_s")
            yield

    _gen = deq_units()

    def pump(n):
        for _ in range(n):
            if next(_gen, "END") == "END":
                break


    p.wait("scalar", [("q_ev", p.val("q_ev"))])
    p.wait("tensor", [("q_ev", p.val("q_ev"))])
    p.wait("vector", [("q_ev", p.val("q_ev"))])
    p.step("vector", lambda e: e.memset(nrm, 0.0), sig="nm_v")

    def vec_norm_max(src, n512, out_idx):
        for kk in range(n512):
            sv = p.step("scalar", lambda e, kk=kk, src=src: e.activation(
                out=sqb, in_=src[:, kk * 512:(kk + 1) * 512], func=AF.Square),
                waits=[("nm_t", p.val("nm_t"))], sig="nm_a")
            bi = 4 + kk % 2
            mv = p.step("tensor", lambda e, bi=bi: e.matmul(
                PS[0:1, bi * 512:bi * 512 + 512], ones_b, sqb,
                start=True, stop=True),
                waits=[("nm_a", sv), ("nm_v", p.val("nm_v"))], sig="nm_t")
            p.fstep("vector", lambda e, bi=bi: e.tensor_reduce(
                out=nrm[0:1, 8:9], in_=PS[0:1, bi * 512:bi * 512 + 512]
                .rearrange("p (a c) -> p a c", a=1), axis=AX.X, op=ALU.max),
                waits=[("nm_t", mv)])
            p.step("vector", lambda e, oi=out_idx: e.tensor_tensor(
                out=nrm[0:1, oi:oi + 1], in0=nrm[0:1, oi:oi + 1],
                in1=nrm[0:1, 8:9], op=ALU.max), sig="nm_v")

    vec_norm_max(k_sb, NT // 512, 0)
    for hh in range(4):
        vec_norm_max(q_sb[:, hh * NT:(hh + 1) * NT], NT // 512, 1)
    p.fstep("vector", lambda e: e.tensor_tensor(
        out=nrm[0:1, 2:3], in0=nrm[0:1, 0:1], in1=nrm[0:1, 1:2], op=ALU.mult))
    sv = p.fstep("vector", lambda e: e.tensor_copy(nrm[0:1, 3:4], nrm[0:1, 2:3]),
                 sig="nm_v")
    av = p.step("scalar", lambda e: e.activation(
        out=nrm[0:1, 4:5], in_=nrm[0:1, 3:4], func=AF.Sqrt),
        waits=[("nm_v", sv)], sig="nm_a")
    bv = p.step("vector", lambda e: e.tensor_scalar(
        out=nrm[0:1, 5:6], in0=nrm[0:1, 4:5], scalar1=-SCALE, scalar2=-2.0,
        op0=ALU.mult, op1=ALU.add), waits=[("nm_a", av)], sig="nm_v")
    tv = p.step("tensor", lambda e: e.matmul(
        PS[:, 4 * 512:4 * 512 + 1], ones_f, nrm[0:1, 5:6],
        start=True, stop=True), waits=[("nm_v", bv)], sig="nm_t")
    bcv = p.step("vector", lambda e: e.tensor_copy(
        bias_sb, PS[:, 4 * 512:4 * 512 + 1]), waits=[("nm_t", tv)], sig="nm_v")
    p.wait("scalar", [("nm_v", bcv)])
    p.wait("tensor", [("nm_v", bcv)])

    def emit_sc(b, h, q0, kt):
        ps_s = bank(kt % 4)
        xv_now = p.val("at_x")
        mw = [("at_x", xv_now - 3)]
        mv = p.step("tensor", lambda e, kt=kt, b=b, q0=q0, h=h,
                    ps_s=ps_s: e.matmul(
            ps_s, k_sb[:, b * T + kt * 128:b * T + (kt + 1) * 128],
            q_sb[:, h * NT + q0:h * NT + q0 + 512],
            start=True, stop=True), waits=mw, sig="at_mm")
        dtile = kt - 4 * ((q0 % T) // 512)
        if dtile >= 0:
            mv2 = p.step("vector", lambda e, ps_s=ps_s, dt_=dtile:
                         e.tensor_tensor(out=ps_s, in0=ps_s,
                                         in1=masks[:, dt_, :], op=ALU.add),
                         waits=[("at_mm", mv)], sig="at_mk")
            ew = [("at_mk", mv2)]
        else:
            ew = [("at_mm", mv)]
        ab_ = attb[kt % 4]
        ew.append(("at_av", p.val("at_av")))
        p.step("scalar", lambda e, ps_s=ps_s, ab_=ab_:
               e.activation(out=ab_, in_=ps_s, func=AF.Exp,
                            bias=bias_sb[:, 0:1], scale=SCALE),
               waits=ew, sig="at_x")

    for b in range(B):
        for h in range(NQH):
            for s in range(8):
                q0 = b * T + s * 512
                nk = 4 * (s + 1)
                xbase = p.val("at_x")
                for j in range(min(3, nk)):
                    emit_sc(b, h, q0, j)
                av_b, sm_b = 4 + 2 * (s % 2), 5 + 2 * (s % 2)
                for kt in range(nk):
                    ab_ = attb[kt % 4]
                    aw = [("at_x", xbase + kt + 1)]
                    if kt == 0:
                        aw.append(("at_e", p.val("at_e") - 1))
                    p.step("tensor", lambda e, kt=kt, b=b, ab_=ab_, nk=nk,
                           av_b=av_b: e.matmul(
                        bank(av_b),
                        v_sb[:, b * T + kt * 128:b * T + (kt + 1) * 128], ab_,
                        start=(kt == 0), stop=(kt == nk - 1)),
                        waits=aw)
                    sw0 = [("at_rc", p.val("at_rc") - 1)] if kt == 0 else []
                    p.step("tensor", lambda e, kt=kt, ab_=ab_, nk=nk,
                           sm_b=sm_b: e.matmul(
                        PS[0:1, sm_b * 512:sm_b * 512 + 512], ones_b, ab_,
                        start=(kt == 0), stop=(kt == nk - 1)),
                        waits=sw0, sig="at_av")
                    if kt + 3 < nk:
                        emit_sc(b, h, q0, kt + 3)
                    pump(2)
                avc = p.val("at_av")
                rw = [("at_av", avc), ("at_r1", p.val("at_r1") - 16)]
                rcv2 = p.step("vector", lambda e, sm_b=sm_b: e.reciprocal(
                    rs_sm, PS[0:1, sm_b * 512:sm_b * 512 + 512]),
                    waits=rw, sig="at_rc")
                rrow = (b * NQH + h) * 8 + s
                r1 = p.dma("sync", d_rr[rrow:rrow + 1, :], rs_sm,
                           waits=[("at_rc", rcv2)], sig="at_r1")
                bcv = p.dma("sync", rbb[s % 2],
                            d_rr[rrow:rrow + 1, :].partition_broadcast(128),
                            waits=[("at_r1", r1),
                                   ("at_e", p.val("at_e") - 1)], sig="at_r2")
                ost = otst[s % 2]
                ew2 = [("at_r2", bcv), ("at_st", p.val("at_st") - 16)]
                ev = p.step("vector", lambda e, ost=ost, av_b=av_b, s=s:
                            e.tensor_tensor(out=ost, in0=bank(av_b),
                                            in1=rbb[s % 2], op=ALU.mult),
                            waits=ew2, sig="at_e")
                p.dma("sync", b_oT[h * 128:(h + 1) * 128, q0:q0 + 512], ost,
                      waits=[("at_e", ev)], sig="at_st")

    pump(10 ** 9)

    def ag(src, dst, waits):
        return p.step("gpsimd", lambda e, s=src, d=dst: e.collective_compute(
            "AllGather", ALU.bypass, replica_groups=[list(range(NC))],
            ins=[s], outs=[d.rearrange("a b c -> (a b) c")]),
            waits=waits, sig="cc")

    agwo = ag(b_wo, g_wo, [("wo_c", p.val("wo_c"))])
    agoT = ag(b_oT, g_oT, [("at_st", p.val("at_st"))])

    ATX = p.val("at_x")
    DQS = p.val("dq_s")
    tv_after = {}
    st_after = {}
    gcnt = 0
    for ui, (mat, i, jh) in enumerate(dq_jobs):
        width = HBAND if mat == "w3" else CW
        nsub = width // 128
        ldb = dqb[ui % 2]
        lw = [("dq_s", DQS)]
        if ui >= 2:
            lw.append(("t2_t", tv_after[ui - 2]))
        lv = p.dma("sync", ldb[:, :width], d_dq[ui, :, :width],
                   waits=lw, sig="t2_l")
        ts = tstage[ui % 2]
        for g0 in range(0, nsub, 8):
            gn = min(8, nsub - g0)
            bkid = gcnt % 2
            tv = None
            for k in range(g0, g0 + gn):
                tw = [("t2_l", lv)]
                if k == g0:
                    if gcnt == 0:
                        tw.append(("at_x", ATX))
                    if gcnt >= 2:
                        tw.append(("t2_e", p.val("t2_e") - 1))
                tv = p.step("tensor", lambda e, k=k, g0=g0, bkid=bkid,
                            ldb=ldb: e.transpose(
                    PSB[:, bkid * 1024 + (k - g0) * 128:
                        bkid * 1024 + (k - g0 + 1) * 128],
                    ldb[:, k * 128:(k + 1) * 128], ident),
                    waits=tw, sig="t2_t")
            ew = [("t2_t", tv)]
            if g0 == 0 and ui >= 2:
                ew.append(st_after[ui - 2])
            p.step("scalar", lambda e, g0=g0, gn=gn, bkid=bkid, ts=ts:
                   e.copy(ts[:, g0 * 128:(g0 + gn) * 128],
                          PSB[:, bkid * 1024:bkid * 1024 + gn * 128]),
                   waits=ew, sig="t2_e")
            gcnt += 1
        tv_after[ui] = p.val("t2_t")
        if mat == "w3":
            dst_ap = b_w3[i // 4, :, :, (i % 4) * 128:(i % 4 + 1) * 128]
            sigout = "d_w3"
        else:
            dstT = b_w1 if mat == "w1" else b_w2
            dst_ap = dstT[:, i, jh * 16:(jh + 1) * 16, :]
            sigout = "d_w1" if mat == "w1" else "d_w2"
        p.dma("sync", dst_ap,
              ts[:, :nsub * 128].rearrange("p (k r) -> p k r", r=128),
              waits=[("t2_e", p.val("t2_e"))], sig=sigout)
        st_after[ui] = (sigout, p.val(sigout))

    agw1 = p.step("gpsimd", lambda e: e.collective_compute(
        "AllGather", ALU.bypass, replica_groups=[list(range(NC))],
        ins=[b_w1.rearrange("p a b c -> (p a b) c")],
        outs=[g_w1.rearrange("n p a b c -> (n p a b) c")]),
        waits=[("d_w1", p.val("d_w1"))], sig="cc")
    agw2 = p.step("gpsimd", lambda e: e.collective_compute(
        "AllGather", ALU.bypass, replica_groups=[list(range(NC))],
        ins=[b_w2.rearrange("p a b c -> (p a b) c")],
        outs=[g_w2.rearrange("n p a b c -> (n p a b) c")]),
        waits=[("d_w2", p.val("d_w2"))], sig="cc")
    agw3 = p.step("gpsimd", lambda e: e.collective_compute(
        "AllGather", ALU.bypass, replica_groups=[list(range(NC))],
        ins=[b_w3.rearrange("a p b c -> (a p b) c")],
        outs=[g_w3.rearrange("n a p b c -> (n a p b) c")]),
        waits=[("d_w3", p.val("d_w3"))], sig="cc")

    ar.reset(m0)
    a_n = ar.alloc([128, 32 * 1024], BF16)
    wo_ch = [ar.alloc([128, 512], BF16) for _ in range(2)]
    yst = [ar.alloc([128, 512], F32) for _ in range(2)]
    ysq = ar.alloc([128, 512], F32)

    p.wait("sync", [("cc", agoT), ("d_w3", p.val("d_w3"))])
    l_oT = dint("l_oT", [NC, DQ, TLOC], BF16)
    slv = p.step("sync", lambda e: e.dma_start(
        out=l_oT.rearrange("a b c -> (a b) c"),
        in_=g_oT.rearrange("a b c -> (a b) c")[:, pid_of(e)]),
        sig="p4_l", inc=16)
    for hg in range(32):
        bandi, hl = hg // 4, hg % 4
        p.dma("sync", a_n[:, hg * 1024:(hg + 1) * 1024],
              l_oT[bandi, hl * 128:(hl + 1) * 128, :],
              waits=[("p4_l", slv)], sig="p4_n")

    if DEBUG:
        d_an = dint("d_an", [128, 32 * 1024], BF16)
        p.dma("sync", d_an, a_n, waits=[("p4_n", p.val("p4_n"))], sig="p4_l")
    p.wait("tensor", [("p4_n", p.val("p4_n")), ("cc", agwo),
                      ("t2_e", p.val("t2_e"))])
    p.step("vector", lambda e: e.memset(mean8, 0.0))
    p.step("vector", lambda e: e.memset(msq8, 0.0), sig="p4_z")
    p.wait("scalar", [("p4_z", p.val("p4_z"))])
    a_nv = a_n.rearrange("p (g t) -> p g t", t=1024)
    wosp = [ar.alloc([128, 32, 512], BF16) for _ in range(2)]
    for ds in range(8):
        wsp = wosp[ds % 2]
        lw = [("wo_m", p.val("wo_m") - 8)] if ds >= 2 else \
            [("d_w3", p.val("d_w3"))]
        lv = p.dma("sync", wsp,
                   g_wo.rearrange("a (k p) c -> p (a k) c", p=128)
                   [:, :, ds * 512:(ds + 1) * 512], waits=lw, sig="wo_ld")
        p.wait("tensor", [("wo_ld", lv), ("wo_e", p.val("wo_e"))])
        for ti in range(8):
            for hc in range(32):
                p.step("tensor", lambda e, hc=hc, ti=ti, wsp=wsp: e.matmul(
                    bank(ti), a_nv[:, hc, ti * 128:(ti + 1) * 128],
                    wsp[:, hc, :], start=(hc == 0), stop=(hc == 31)),
                    sig="wo_m" if hc == 31 else None)
        mvw = p.val("wo_m")
        for ti in range(8):
            ys = yst[ti % 2]
            ew = [("wo_m", mvw), ("wo_st", p.val("wo_st") - 16)]
            ev = p.step("vector", lambda e, ys=ys, ti=ti: e.tensor_copy(
                ys, bank(ti)), waits=ew, sig="wo_e")
            p.fstep("vector", lambda e, ys=ys: e.tensor_reduce(
                out=sm2[:, 6:7], in_=ys.rearrange("p (a c) -> p a c", a=1),
                axis=AX.X, op=ALU.add))
            p.fstep("vector", lambda e, ti=ti: e.tensor_tensor(
                out=mean8[:, ti:ti + 1], in0=mean8[:, ti:ti + 1],
                in1=sm2[:, 6:7], op=ALU.add))
            sv = p.step("scalar", lambda e, ys=ys: e.activation(
                out=ysq, in_=ys, func=AF.Square, accum_out=sm2[:, 7:8]),
                waits=[("wo_e", ev), ("wo_v", p.val("wo_v"))], sig="wo_a")
            p.step("vector", lambda e, ti=ti: e.tensor_tensor(
                out=msq8[:, ti:ti + 1], in0=msq8[:, ti:ti + 1],
                in1=sm2[:, 7:8], op=ALU.add), waits=[("wo_a", sv)],
                sig="wo_v")
            p.wait("vector", [("wo_v", p.val("wo_v"))])
            p.dma("sync", s_y[ti * 128:(ti + 1) * 128, ds * 512:(ds + 1) * 512],
                  ys, waits=[("wo_e", ev)], sig="wo_st")

    p.wait("vector", [("wo_v", p.val("wo_v"))])
    p.fstep("vector", lambda e: e.tensor_scalar_mul(mean8, mean8, 1.0 / D))
    p.fstep("vector", lambda e: e.tensor_scalar_mul(msq8, msq8, 1.0 / D))
    p.fstep("vector", lambda e: e.tensor_tensor(
        out=sm2[:, 0:8], in0=mean8, in1=mean8, op=ALU.mult))
    p.fstep("vector", lambda e: e.tensor_tensor(
        out=msq8, in0=msq8, in1=sm2[:, 0:8], op=ALU.subtract))
    vv = p.step("vector", lambda e: e.tensor_scalar_add(msq8, msq8, EPS),
                sig="p4_v")
    sv = p.step("scalar", lambda e: e.activation(
        out=str_, in_=msq8, func=AF.Sqrt), waits=[("p4_v", vv)], sig="p4_a")
    p.fstep("vector", lambda e: e.reciprocal(str_, str_), waits=[("p4_a", sv)])
    p.fstep("vector", lambda e: e.tensor_scalar_mul(stm, mean8, -1.0))

    ar.reset(m0)
    xb4 = [ar.alloc([128, D], F32) for _ in range(2)]
    yb4 = [ar.alloc([128, D], F32) for _ in range(2)]
    scr4 = ar.alloc([128, D], F32)
    x1b = ar.alloc([128, D], F32)
    lnb2 = ar.alloc([128, D], BF16)
    tst2 = ar.alloc([128, 32 * 128], BF16)

    wv = p.val("wo_st")
    for i in range(8):
        x_t, y_t = xb4[i % 2], yb4[i % 2]
        lw = [("p4_x", p.val("p4_x") - 1)] if i >= 2 else []
        p.dma("sync", x_t, x_in[i * 128:(i + 1) * 128, :],
              waits=[("wo_st", wv)] + lw, sig="p4_l2")
        lv2 = p.dma("sync", y_t, s_y[i * 128:(i + 1) * 128, :], sig="p4_l2")
        p.wait("vector", [("p4_l2", lv2)])
        p.step("vector", lambda e, y=y_t, i=i: e.tensor_scalar(
            out=scr4, in0=y, scalar1=stm[:, i:i + 1], scalar2=str_[:, i:i + 1],
            op0=ALU.add, op1=ALU.mult))
        x1v = p.step("vector", lambda e, x=x_t: e.tensor_tensor(
            out=x1b, in0=scr4, in1=x, op=ALU.add),
            waits=[("p4_s", p.val("p4_s") - 16)], sig="p4_x")
        p.dma("sync", s_x1[i * 128:(i + 1) * 128, :], x1b,
              waits=[("p4_x", x1v)], sig="p4_s")
        ln_prep(x1b)
        lnv = p.step("vector", lambda e: e.tensor_scalar(
            out=lnb2, in0=x1b, scalar1=sm2[:, 4:5], scalar2=sm2[:, 5:6],
            op0=ALU.add, op1=ALU.mult),
            waits=[("p4_t", p.val("p4_t"))], sig="ln_v")
        for k in range(32):
            bnk = bankb(k % 2)
            tw = [("ln_v", lnv)]
            if k >= 2:
                tw.append(("p4_e", p.val("p4_e") - 1))
            else:
                tw.append(("p4_s", p.val("p4_s")))
            tv = p.step("tensor", lambda e, k=k, bnk=bnk: e.transpose(
                bnk, lnb2[:, k * 128:(k + 1) * 128], ident),
                waits=tw, sig="p4_t")
            p.step("vector", lambda e, k=k, bnk=bnk: e.tensor_copy(
                tst2[:, k * 128:(k + 1) * 128], bnk),
                waits=[("p4_t", tv)], sig="p4_e")
        p.dma("sync", s_ln2.rearrange("(k q) r -> q k r", q=128)
              [:, :, i * 128:(i + 1) * 128],
              tst2.rearrange("p (k r) -> p k r", r=128),
              waits=[("p4_e", p.val("p4_e"))], sig="p4_s")

    ar.reset(m0)
    h_sb = ar.alloc([128, 88 * TCH], BF16)
    ln2c = ar.alloc([128, 32 * TCH], BF16)
    w1c = [ar.alloc([128, 32 * 128], BF16) for _ in range(2)]
    w2c = [ar.alloc([128, 32 * 128], BF16) for _ in range(2)]
    w3bb = [ar.alloc([128, (HBAND // 128) * 512], BF16) for _ in range(2)]
    sgb = ar.alloc([128, TCH], F32)
    sgb2 = ar.alloc([128, TCH], F32)
    y2st = [ar.alloc([128, 512], F32) for _ in range(2)]
    y2sq = sgb

    p.step("vector", lambda e: e.memset(mean8, 0.0))
    p.step("vector", lambda e: e.memset(msq8, 0.0), sig="f_z")
    p.wait("scalar", [("f_z", p.val("f_z"))])
    p.wait("sync", [("p4_s", p.val("p4_s")), ("cc", agw3)])
    for ci in range(2):
        lnw = [("f_g", p.val("f_g"))] if ci else []
        lv = p.dma("sync", ln2c.rearrange("p (k c) -> p k c", c=TCH),
                   s_ln2.rearrange("(k p) c -> p k c", p=128)
                   [:, :, ci * TCH:(ci + 1) * TCH], waits=lnw, sig="f_l")
        p.wait("tensor", [("f_l", lv), ("cc", agw2)])
        hs_guard = p.val("f_dm")
        for ht in range(88):
            bandi, hl = ht // 11, (ht % 11) * 128
            wa, wb_ = w1c[ht % 2], w2c[ht % 2]
            lw = [("f_su", p.val("f_su") - 1)] if (ci or ht >= 2) else []
            p.dma("sync", wa.rearrange("p (k c) -> p k c", c=128),
                  g_w1[bandi][:, ht % 11, :, :], waits=lw, sig="f_wl")
            wlv = p.dma("sync", wb_.rearrange("p (k c) -> p k c", c=128),
                        g_w2[bandi][:, ht % 11, :, :], sig="f_wl")
            bg, bu = (0, 1) if ht % 2 == 0 else (6, 7)
            sgx = sgb if ht % 2 == 0 else sgb2
            gw = [("f_wl", wlv), ("f_su", p.val("f_su") - 1)]
            p.wait("tensor", gw)
            for Dc in range(32):
                p.step("tensor", lambda e, Dc=Dc, wa=wa, bg=bg: e.matmul(
                    bank(bg), wa[:, Dc * 128:(Dc + 1) * 128],
                    ln2c[:, Dc * TCH:(Dc + 1) * TCH],
                    start=(Dc == 0), stop=(Dc == 31)))
                p.step("tensor", lambda e, Dc=Dc, wb_=wb_, bu=bu: e.matmul(
                    bank(bu), wb_[:, Dc * 128:(Dc + 1) * 128],
                    ln2c[:, Dc * TCH:(Dc + 1) * TCH],
                    start=(Dc == 0), stop=(Dc == 31)),
                    sig="f_g" if Dc == 31 else None)
            gv = p.val("f_g")
            sv = p.step("scalar", lambda e, bg=bg, sgx=sgx: e.activation(
                out=sgx, in_=bank(bg), func=AF.Silu), waits=[("f_g", gv)],
                sig="f_sl")
            hw = [("f_sl", sv)]
            if ci and ht == 0:
                hw.append(("f_dm", hs_guard))
            p.step("vector", lambda e, ht=ht, bu=bu, sgx=sgx: e.tensor_tensor(
                out=h_sb[:, ht * TCH:(ht + 1) * TCH], in0=sgx, in1=bank(bu),
                op=ALU.mult), waits=hw, sig="f_su")
        huv = p.val("f_su")
        p.wait("tensor", [("f_su", huv)])
        gidx = 0
        for ds in range(8):
            for bandi in range(8):
                wbb = w3bb[gidx % 2]
                lw = [("f_dm", p.val("f_dm") - 11)] if (ci or gidx >= 2) else []
                wlv = p.dma("sync", wbb.rearrange("p (a c) -> p a c", c=512),
                            g_w3[bandi, ds], waits=lw, sig="f_wl")
                p.wait("tensor", [("f_wl", wlv)])
                if bandi == 0:
                    p.wait("tensor", [("f_de", p.val("f_de"))])
                for htl in range(11):
                    ht = bandi * 11 + htl
                    for ti in range(4):
                        p.step("tensor", lambda e, ht=ht, ti=ti, wbb=wbb,
                               htl=htl: e.matmul(
                            bank(2 + ti),
                            h_sb[:, ht * TCH + ti * 128:
                                 ht * TCH + (ti + 1) * 128],
                            wbb[:, htl * 512:(htl + 1) * 512],
                            start=(ht == 0), stop=(ht == 87)),
                            sig="f_dm" if ti == 3 else None)
                gidx += 1
            dmv = p.val("f_dm")
            for ti in range(4):
                tt2 = ci * 4 + ti
                ys = y2st[ti % 2]
                ew = [("f_dm", dmv)]
                ew.append(("f_st", p.val("f_st") - 16))
                ev = p.step("vector", lambda e, ti=ti, ys=ys: e.tensor_copy(
                    ys, bank(2 + ti)), waits=ew, sig="f_de")
                p.fstep("vector", lambda e, ys=ys: e.tensor_reduce(
                    out=sm2[:, 6:7], in_=ys.rearrange("p (a c) -> p a c", a=1),
                    axis=AX.X, op=ALU.add))
                p.fstep("vector", lambda e, tt2=tt2: e.tensor_tensor(
                    out=mean8[:, tt2:tt2 + 1], in0=mean8[:, tt2:tt2 + 1],
                    in1=sm2[:, 6:7], op=ALU.add))
                sv = p.step("scalar", lambda e, ys=ys: e.activation(
                    out=y2sq, in_=ys, func=AF.Square, accum_out=sm2[:, 7:8]),
                    waits=[("f_de", ev), ("f_v2", p.val("f_v2"))], sig="f_a")
                p.step("vector", lambda e, tt2=tt2: e.tensor_tensor(
                    out=msq8[:, tt2:tt2 + 1], in0=msq8[:, tt2:tt2 + 1],
                    in1=sm2[:, 7:8], op=ALU.add), waits=[("f_a", sv)],
                    sig="f_v2")
                p.wait("vector", [("f_v2", p.val("f_v2"))])
                p.dma("sync", s_y2[tt2 * 128:(tt2 + 1) * 128,
                                   ds * 512:(ds + 1) * 512], ys,
                      waits=[("f_de", ev)], sig="f_st")

    p.wait("vector", [("f_v2", p.val("f_v2"))])
    p.fstep("vector", lambda e: e.tensor_scalar_mul(mean8, mean8, 1.0 / D))
    p.fstep("vector", lambda e: e.tensor_scalar_mul(msq8, msq8, 1.0 / D))
    p.fstep("vector", lambda e: e.tensor_tensor(
        out=sm2[:, 0:8], in0=mean8, in1=mean8, op=ALU.mult))
    p.fstep("vector", lambda e: e.tensor_tensor(
        out=msq8, in0=msq8, in1=sm2[:, 0:8], op=ALU.subtract))
    vv = p.step("vector", lambda e: e.tensor_scalar_add(msq8, msq8, EPS),
                sig="f_v")
    sv = p.step("scalar", lambda e: e.activation(
        out=str_, in_=msq8, func=AF.Sqrt), waits=[("f_v", vv)], sig="f_a2")
    p.fstep("vector", lambda e: e.reciprocal(str_, str_), waits=[("f_a2", sv)])
    p.fstep("vector", lambda e: e.tensor_scalar_mul(stm, mean8, -1.0))

    ar.reset(m0)
    xb5 = [ar.alloc([128, D], F32) for _ in range(2)]
    yb5 = [ar.alloc([128, D], F32) for _ in range(2)]
    scr5 = ar.alloc([128, D], F32)
    fout = ar.alloc([128, D], F32)

    fv = p.val("f_st")
    for i in range(8):
        x_t, y_t = xb5[i % 2], yb5[i % 2]
        lw = [("fin_o", p.val("fin_o") - 1)] if i >= 2 else []
        p.dma("sync", x_t, s_x1[i * 128:(i + 1) * 128, :],
              waits=[("f_st", fv)] + lw, sig="fin_l")
        lv2 = p.dma("sync", y_t, s_y2[i * 128:(i + 1) * 128, :], sig="fin_l")
        p.wait("vector", [("fin_l", lv2)])
        p.step("vector", lambda e, y=y_t, i=i: e.tensor_scalar(
            out=scr5, in0=y, scalar1=stm[:, i:i + 1], scalar2=str_[:, i:i + 1],
            op0=ALU.add, op1=ALU.mult))
        ov = p.step("vector", lambda e, x=x_t: e.tensor_tensor(
            out=fout, in0=scr5, in1=x, op=ALU.add),
            waits=[("fin_s", p.val("fin_s"))], sig="fin_o")
        p.dma("sync", out_d[i * 128:(i + 1) * 128, :], fout,
              waits=[("fin_o", ov)], sig="fin_s")

    if DEBUG:
        dbg = {
            "dbg_oT": b_oT, "dbg_y": s_y, "dbg_x1": s_x1,
            "dbg_ln2": s_ln2, "dbg_y2": s_y2, "dbg_loT": l_oT,
            "dbg_an": d_an, "dbg_gwo": g_wo,
        }
        for nm, src_t in dbg.items():
            od2 = nc.dram_tensor(nm, list(src_t.tensor.shape),
                                 src_t.dtype, kind="ExternalOutput").ap()
            p.dma("sync", od2, src_t, waits=[("fin_s", p.val("fin_s"))],
                  sig="dbg_s")
        dsb = nc.dram_tensor("dbg_sb", [128, 64], F32,
                             kind="ExternalOutput").ap()
        dsb_st = ar.alloc([128, 64], F32)
        p.step("vector", lambda e: e.tensor_copy(dsb_st[:, 0:1], bias_sb),
               waits=[("fin_o", p.val("fin_o"))])
        p.step("vector", lambda e: e.tensor_copy(dsb_st[:, 1:17], sm2))
        p.step("vector", lambda e: e.tensor_copy(dsb_st[:, 17:25], mean8))
        p.step("vector", lambda e: e.tensor_copy(dsb_st[:, 25:33], msq8))
        p.step("vector", lambda e: e.tensor_copy(dsb_st[:, 33:41], stm))
        dv_ = p.step("vector", lambda e: e.tensor_copy(dsb_st[:, 41:49], str_),
                     sig="dbg_v")
        p.step("vector", lambda e: e.tensor_copy(
            dsb_st[0:1, 48:64], nrm), sig="dbg_v")
        p.dma("sync", dsb, dsb_st, waits=[("dbg_v", p.val("dbg_v"))],
              sig="dbg_s")
        p.wait("sync", [("dbg_s", p.val("dbg_s"))])
    p.wait("sync", [("fin_s", p.val("fin_s"))])
    return nc, p


_CACHE = {}


def _get_built():
    if "nc" not in _CACHE:
        nc, p = build_program()
        with nc.Block() as blk:
            for name in Prog.ENGS:
                fns = p.q[name]

                def body(e, fns=fns):
                    for f in fns:
                        f(e)
                getattr(blk, name)(body)
        _CACHE["nc"] = nc
    return _CACHE["nc"]


def kernel(L0_attn_x, L0_attn_Wq, L0_attn_Wk, L0_attn_Wv, L0_attn_Wo,
           L0_ffn_w1, L0_ffn_w2, L0_ffn_w3):
    x = np.ascontiguousarray(np.asarray(L0_attn_x, np.float32).reshape(NT, D))
    WqT = np.ascontiguousarray(np.asarray(L0_attn_Wq, np.float32).T)
    WkT = np.ascontiguousarray(np.asarray(L0_attn_Wk, np.float32).T)
    WvT = np.ascontiguousarray(np.asarray(L0_attn_Wv, np.float32).T)
    WoT = np.ascontiguousarray(np.asarray(L0_attn_Wo, np.float32).T)
    w1p = np.zeros((HPAD, D), np.float32)
    w1p[:H] = np.asarray(L0_ffn_w1, np.float32)
    w2p = np.zeros((HPAD, D), np.float32)
    w2p[:H] = np.asarray(L0_ffn_w2, np.float32)
    w3p = np.zeros((D, HPAD), np.float32)
    w3p[:, :H] = np.asarray(L0_ffn_w3, np.float32)

    cos_t, sinS_t = _rope_tables()
    kk = np.arange(128)[:, None]
    qq = np.arange(512)[None, :]
    mask_t = np.stack([np.where(i * 128 + kk <= qq, 0.0, MVAL)
                       for i in range(4)]).astype(np.float32)
    ident_t = np.eye(128, dtype=ml_dtypes.bfloat16)
    nf4b_t = np.tile(-NF4_B, (128, 1)).astype(np.float32)

    nc = _get_built()
    in_maps = []
    for c in range(NC):
        in_maps.append(dict(
            x_c=np.ascontiguousarray(x[c * TLOC:(c + 1) * TLOC]),
            wqt_c=np.ascontiguousarray(WqT[:, c * DQ:(c + 1) * DQ]),
            wkt_c=np.ascontiguousarray(WkT[:, c * HD:(c + 1) * HD]),
            wvt_c=np.ascontiguousarray(WvT[:, c * HD:(c + 1) * HD]),
            wot_c=np.ascontiguousarray(WoT[c * DQ:(c + 1) * DQ, :]),
            w1_c=np.ascontiguousarray(w1p[c * HBAND:(c + 1) * HBAND]),
            w2_c=np.ascontiguousarray(w2p[c * HBAND:(c + 1) * HBAND]),
            w3_c=np.ascontiguousarray(w3p[:, c * HBAND:(c + 1) * HBAND]),
            cos_t=cos_t, sinS_t=sinS_t, mask_t=mask_t, ident_t=ident_t,
            nf4b_t=nf4b_t,
        ))
    res = run_bass_kernel_spmd(nc, in_maps, core_ids=list(range(NC)))
    out = np.concatenate([res.results[c]["out_c"] for c in range(NC)], axis=0)
    if DEBUG:
        return out.reshape(B, T, D), res
    return out.reshape(B, T, D)


# version 16
# speedup vs baseline: 1.0958x; 1.0958x over previous
import math
import numpy as np
import ml_dtypes
import concourse.bass as bass
import concourse.mybir as mybir
from concourse.bass_utils import run_bass_kernel_spmd

F32 = mybir.dt.float32
BF16 = mybir.dt.bfloat16
AF = mybir.ActivationFunctionType
ALU = mybir.AluOpType
AX = mybir.AxisListType

NC = 8
D = 4096
T = 4096
B = 2
NT = B * T
TLOC = NT // NC
HD = 128
NQH = 4
DQ = NQH * HD
H = 11008
HPAD = 11264
HBAND = HPAD // NC
SCALE = 1.0 / math.sqrt(HD)
EPS = 1e-5
MVAL = -1e30
TCH = 512

NF4 = np.array([-1.0, -0.6961928009986877, -0.5250730514526367,
                -0.39491748809814453, -0.28444138169288635,
                -0.18477343022823334, -0.09105003625154495, 0.0,
                0.07958029955625534, 0.16093020141124725,
                0.24611230194568634, 0.33791524171829224,
                0.44070982933044434, 0.5626170039176941,
                0.7229568362236023, 1.0], dtype=np.float32)
NF4_B = ((NF4[:-1] + NF4[1:]) * np.float32(0.5)).astype(np.float32)
NF4_DH = ((NF4[1:] - NF4[:-1]) * np.float32(0.5)).astype(np.float32)
NF4_K0 = float(np.float32(NF4[0] + np.float64(NF4_DH.astype(np.float64).sum())))


class Prog:
    ENGS = ("sync", "gpsimd", "tensor", "vector", "scalar")

    def __init__(self, nc):
        self.nc = nc
        self.q = {e: [] for e in self.ENGS}
        self.sems = {}

    def _sem(self, name):
        if name not in self.sems:
            self.sems[name] = [self.nc.alloc_semaphore(name), 0]
        return self.sems[name]

    def val(self, name):
        return self._sem(name)[1]

    def step(self, eng, build, waits=(), sig=None, inc=1):
        tgt, s = None, None
        if sig is not None:
            s = self._sem(sig)
            s[1] += inc
            tgt = s[1]
        wl = tuple((self._sem(n)[0], v) for (n, v) in waits if v > 0)

        def f(e, wl=wl, build=build, s=s, inc=inc):
            for (hh, v) in wl:
                e.wait_ge(hh, v)
            ins = build(e)
            if s is not None:
                ins.then_inc(s[0], inc)
        self.q[eng].append(f)
        return tgt

    def fstep(self, eng, build, waits=(), sig=None, inc=1):
        s = sig if sig is not None else ("F_" + eng)
        v = self.step(eng, build, waits, s, inc)
        self.wait(eng, [(s, v)])
        return v

    def wait(self, eng, waits):
        wl = tuple((self._sem(n)[0], v) for (n, v) in waits if v > 0)

        def f(e, wl=wl):
            for (hh, v) in wl:
                e.wait_ge(hh, v)
        self.q[eng].append(f)

    def dma(self, eng, out, in_, waits=(), sig="dma"):
        return self.step(eng, lambda e, o=out, i=in_: e.dma_start(out=o, in_=i),
                         waits=waits, sig=sig, inc=16)


class Arena:
    def __init__(self, nc, base, top, tag):
        self.nc, self.base, self.top, self.off = nc, base, top, base
        self.tag, self.n = tag, 0

    def alloc(self, shape, dtype):
        nb = int(np.prod(shape[1:])) * mybir.dt.size(dtype)
        off = (self.off + 31) & ~31
        assert off + nb <= self.top, f"sbuf overflow {off}+{nb}>{self.top}"
        self.off = off + nb
        self.n += 1
        return self.nc.alloc_sbuf_tensor_at(
            f"{self.tag}{self.n}", list(shape), dtype, offset=off).ap()

    def mark(self):
        return self.off

    def reset(self, m):
        self.off = m


def _rope_tables():
    base = 500000.0
    inv = 1.0 / base ** (np.arange(0, HD, 2, dtype=np.float32) / HD)
    factor, low_f, high_f, old_ctx = 32.0, 1.0, 4.0, 8192
    low_wl, high_wl = old_ctx / low_f, old_ctx / high_f
    wl = 2 * math.pi / inv
    inv_l = np.where(wl > low_wl, inv / factor, inv)
    smooth = (old_ctx / wl - low_f) / (high_f - low_f)
    smoothed = (1 - smooth) * inv_l / factor + smooth * inv_l
    med = ~(wl < high_wl) & ~(wl > low_wl)
    freq = np.where(med, smoothed, inv_l).astype(np.float32)
    ang = np.arange(T, dtype=np.float32)[:, None] * freq[None, :]
    emb = np.concatenate([ang, ang], axis=1)
    cos = np.cos(emb).astype(np.float32)
    sinS = np.sin(emb).astype(np.float32)
    sinS[:, :HD // 2] *= -1.0
    return cos, sinS


DEBUG = False


def build_program():
    nc = bass.Bass("TRN2", target_bir_lowering=False)
    p = Prog(nc)

    def din(name, shape, dt=F32):
        return nc.dram_tensor(name, list(shape), dt, kind="ExternalInput").ap()

    x_in = din("x_c", [TLOC, D])
    wqt_in = din("wqt_c", [D, DQ])
    wkt_in = din("wkt_c", [D, HD])
    wvt_in = din("wvt_c", [D, HD])
    wot_in = din("wot_c", [DQ, D])
    w1_in = din("w1_c", [HBAND, D])
    w2_in = din("w2_c", [HBAND, D])
    w3_in = din("w3_c", [D, HBAND])
    cos_in = din("cos_t", [T, HD])
    sin_in = din("sinS_t", [T, HD])
    mask_in = din("mask_t", [4, 128, 512])
    id_in = din("ident_t", [128, 128], BF16)
    nf4b_in = din("nf4b_t", [128, 15])

    out_d = nc.dram_tensor("out_c", [TLOC, D], F32, kind="ExternalOutput").ap()

    def dint(name, shape, dt):
        return nc.dram_tensor(name, list(shape), dt, kind="Internal").ap()

    b_ln1 = dint("b_ln1", [128, TLOC // 128, 32, 128], BF16)
    g_ln1 = dint("g_ln1", [NC, 128, TLOC // 128, 32, 128], BF16)
    b_w1 = dint("b_w1", [128, HBAND // 128, 32, 128], BF16)
    b_w2 = dint("b_w2", [128, HBAND // 128, 32, 128], BF16)
    b_w3 = dint("b_w3", [8, 128, HBAND // 128, 512], BF16)
    g_w1 = dint("g_w1", [NC, 128, HBAND // 128, 32, 128], BF16)
    g_w2 = dint("g_w2", [NC, 128, HBAND // 128, 32, 128], BF16)
    g_w3 = dint("g_w3", [NC, 8, 128, HBAND // 128, 512], BF16)
    b_wo = dint("b_wo", [DQ, D], BF16)
    g_wo = dint("g_wo", [NC, DQ, D], BF16)
    b_oT = dint("b_oT", [DQ, NT], BF16)
    g_oT = dint("g_oT", [NC, DQ, NT], BF16)
    d_dq = dint("d_dq", [76, 128, 2048], BF16)
    d_rr = dint("d_rr", [64, 512], F32)
    s_y = dint("s_y", [TLOC, D], F32)
    s_x1 = dint("s_x1", [TLOC, D], F32)
    s_ln2 = dint("s_ln2", [D, TLOC], BF16)
    s_y2 = dint("s_y2", [TLOC, D], F32)

    PS = nc.alloc_psum_tensor("PS", [128, 4096], F32).ap()
    PSB = PS.bitcast(BF16)

    def bank(i):
        return PS[:, i * 512:(i + 1) * 512]

    def bankb(i):
        return PSB[:, i * 1024:i * 1024 + 128]

    SB_TOP = 222 * 1024
    pers = Arena(nc, 24 * 1024, 34 * 1024, "pe")
    ar = Arena(nc, 34 * 1024, SB_TOP, "ar")

    ident = pers.alloc([128, 128], BF16)
    ones_b = pers.alloc([128, 1], BF16)
    ones_f = pers.alloc([1, 128], F32)
    masks = pers.alloc([128, 4, 512], F32)
    small = pers.alloc([128, 16], F32)
    sm2 = pers.alloc([128, 16], F32)
    nrm = pers.alloc([1, 16], F32)
    bias_sb = pers.alloc([128, 1], F32)
    mean8 = pers.alloc([128, 8], F32)
    msq8 = pers.alloc([128, 8], F32)
    stm = pers.alloc([128, 8], F32)
    str_ = pers.alloc([128, 8], F32)
    st6 = pers.alloc([128, 48], F32)
    nf4b = pers.alloc([128, 15], F32)
    dh_sb = pers.alloc([128, 15], BF16)

    _pid = {}

    def pid_of(e):
        if id(e) not in _pid:
            _pid[id(e)] = bass.ts(e.partition_id(), TLOC)
        return _pid[id(e)]

    p.q["sync"].append(lambda e: pid_of(e))

    p.dma("sync", ident, id_in, sig="cst")
    p.dma("sync", masks, mask_in.transpose([1, 0, 2]), sig="cst")
    p.dma("sync", nf4b, nf4b_in, sig="cst")
    p.step("gpsimd", lambda e: e.memset(ones_b, 1.0), sig="cst")
    p.step("gpsimd", lambda e: e.memset(ones_f, 1.0), sig="cst")
    for j in range(15):
        p.step("gpsimd", lambda e, j=j: e.memset(
            dh_sb[:, j:j + 1], float(NF4_DH[j])), sig="cst")
    CST = p.val("cst")

    m0 = ar.mark()
    q_sb = ar.alloc([128, 4 * NT], BF16)
    k_sb = ar.alloc([128, NT], BF16)
    v_sb = ar.alloc([128, NT], BF16)
    attb = [ar.alloc([128, 512], BF16) for _ in range(4)]
    otst = [ar.alloc([128, 512], BF16) for _ in range(2)]
    rs_sm = ar.alloc([1, 512], F32)
    rbb = [ar.alloc([128, 512], F32) for _ in range(2)]
    sqb = ar.alloc([128, 512], BF16)
    mSh = ar.mark()
    arD = Arena(nc, mSh, SB_TOP, "dq")
    CW = 2048
    wbuf = [arD.alloc([128, CW], F32) for _ in range(2)]
    ubuf = arD.alloc([128, CW], F32)
    plane = [arD.alloc([128, CW], BF16) for _ in range(2)]
    accb = [arD.alloc([128, CW], F32) for _ in range(2)]
    dqb = [arD.alloc([128, CW], BF16) for _ in range(2)]
    tstage = [arD.alloc([128, 16 * 128], BF16) for _ in range(2)]
    amax2 = [arD.alloc([128, 32], F32) for _ in range(2)]
    rcp2 = [arD.alloc([128, 32], F32) for _ in range(2)]
    cstb = arD.alloc([128, D], BF16)
    ubufs = [ubuf, cstb.bitcast(F32)]

    xb_ln = [ar.alloc([128, D], F32) for _ in range(2)]
    lnb = ar.alloc([128, D], BF16)
    tstage_l = ar.alloc([128, 32 * 128], BF16)

    def ln_prep(xsrc):
        for ci6 in range(8):
            p.fstep("vector", lambda e, x=xsrc, ci6=ci6: e.bn_stats(
                st6[:, ci6 * 6:(ci6 + 1) * 6],
                x[:, ci6 * 512:(ci6 + 1) * 512]))
        p.fstep("vector", lambda e: e.bn_aggr(small[:, 0:2], st6))
        sv = p.step("vector", lambda e: e.tensor_scalar_add(
            sm2[:, 0:1], small[:, 1:2], EPS), sig="ln_v")
        av = p.step("scalar", lambda e: e.activation(
            out=sm2[:, 1:2], in_=sm2[:, 0:1], func=AF.Sqrt),
            waits=[("ln_v", sv)], sig="ln_a")
        p.wait("vector", [("ln_a", av)])
        p.fstep("vector", lambda e: e.reciprocal(sm2[:, 5:6], sm2[:, 1:2]))
        p.fstep("vector", lambda e: e.tensor_scalar_mul(
            sm2[:, 4:5], small[:, 0:1], -1.0))

    for i in range(TLOC // 128):
        x_t = xb_ln[i % 2]
        lw = [] if i < 2 else [("ln_v", p.val("ln_v") - 1)]
        lv = p.dma("sync", x_t, x_in[i * 128:(i + 1) * 128, :], waits=lw,
                   sig="x_ld")
        p.wait("vector", [("x_ld", lv)])
        ln_prep(x_t)
        lnv = p.step("vector", lambda e, x=x_t: e.tensor_scalar(
            out=lnb, in0=x, scalar1=sm2[:, 4:5], scalar2=sm2[:, 5:6],
            op0=ALU.add, op1=ALU.mult),
            waits=[("l1_t", p.val("l1_t"))], sig="ln_v")
        for k in range(32):
            bnk = bankb(k % 2)
            tw = [("ln_v", lnv)]
            if k >= 2:
                tw.append(("l1_e", p.val("l1_e") - 1))
            else:
                tw.append(("ln1_d", p.val("ln1_d")))
            tv = p.step("tensor", lambda e, k=k, bnk=bnk: e.transpose(
                bnk, lnb[:, k * 128:(k + 1) * 128], ident),
                waits=tw, sig="l1_t")
            p.step("vector", lambda e, k=k, bnk=bnk: e.tensor_copy(
                tstage_l[:, k * 128:(k + 1) * 128], bnk),
                waits=[("l1_t", tv)], sig="l1_e")
        p.dma("sync", b_ln1[:, i, :, :],
              tstage_l.rearrange("p (k r) -> p k r", r=128),
              waits=[("l1_e", p.val("l1_e"))], sig="ln1_d")

    agln1 = p.step("gpsimd", lambda e: e.collective_compute(
        "AllGather", ALU.bypass, replica_groups=[list(range(NC))],
        ins=[b_ln1.rearrange("p a b c -> (p a b) c")],
        outs=[g_ln1.rearrange("n p a b c -> (n p a b) c")]),
        waits=[("ln1_d", p.val("ln1_d"))], sig="cc")
    LND = p.val("ln1_d")

    ar.reset(mSh)
    wq_sb = ar.alloc([128, 32 * DQ], BF16)
    wkv_sb = ar.alloc([128, 32 * 256], BF16)
    lnch = [ar.alloc([128, 32 * 128], BF16) for _ in range(2)]
    costl = [ar.alloc([128, 128], F32) for _ in range(2)]
    sintl = [ar.alloc([128, 128], F32) for _ in range(2)]
    ropeA = ar.alloc([128, 640], F32)
    ropeB = ar.alloc([128, 640], F32)
    qkr = ar.alloc([128, 640], BF16)

    wsb_w = [("ln1_d", LND)]
    p.dma("gpsimd", wq_sb.rearrange("p (k c) -> p k c", c=DQ),
          wqt_in.rearrange("(k p) c -> p k c", p=128), waits=wsb_w, sig="w_sb")
    p.dma("gpsimd", wkv_sb.rearrange("p (k c) -> p k c", c=256)[:, :, 0:128],
          wkt_in.rearrange("(k p) c -> p k c", p=128), sig="w_sb")
    wsb = p.dma("gpsimd", wkv_sb.rearrange("p (k c) -> p k c", c=256)
                [:, :, 128:256],
                wvt_in.rearrange("(k p) c -> p k c", p=128), sig="w_sb")
    p.wait("tensor", [("w_sb", wsb), ("cc", agln1)])
    p.wait("sync", [("cc", agln1)])
    p.wait("vector", [("w_sb", wsb)])

    for tt in range(NT // 128):
        bandi, tl = tt // 8, (tt % 8) * 128
        lc = lnch[tt % 2]
        lw = [("q_mm2", p.val("q_mm2") - 1)] if tt >= 2 else []
        p.dma("sync", lc.rearrange("p (k c) -> p k c", c=128),
              g_ln1[bandi][:, (tt % 8), :, :], waits=lw, sig="q_ld")
        ct, st_ = costl[tt % 2], sintl[tt % 2]
        pos = (tt % 32) * 128
        rw = [("q_rp", p.val("q_rp") - 2)] if tt >= 2 else []
        p.dma("sync", ct, cos_in[pos:pos + 128, :], waits=rw, sig="q_ld")
        cv = p.dma("sync", st_, sin_in[pos:pos + 128, :], sig="q_ld")
        tw = [("q_ld", cv)]
        if tt >= 1:
            tw.append(("q_rp", p.val("q_rp")))
        p.wait("tensor", tw)
        for Dc in range(32):
            p.step("tensor", lambda e, Dc=Dc, lc=lc: e.matmul(
                bank(4), lc[:, Dc * 128:(Dc + 1) * 128],
                wq_sb[:, Dc * DQ:(Dc + 1) * DQ],
                start=(Dc == 0), stop=(Dc == 31)))
            p.step("tensor", lambda e, Dc=Dc, lc=lc: e.matmul(
                PS[:, 5 * 512:5 * 512 + 256], lc[:, Dc * 128:(Dc + 1) * 128],
                wkv_sb[:, Dc * 256:(Dc + 1) * 256],
                start=(Dc == 0), stop=(Dc == 31)),
                sig="q_mm2" if Dc == 31 else None)
        mmv = p.val("q_mm2")
        qk_ps = PS[:, 4 * 512:4 * 512 + 640]
        p.step("vector", lambda e, ct=ct: e.tensor_tensor(
            out=ropeA.rearrange("p (h c) -> p h c", c=128),
            in0=qk_ps.rearrange("p (h c) -> p h c", c=128),
            in1=ct.unsqueeze(1).broadcast_to([128, 5, 128]), op=ALU.mult),
            waits=[("q_mm2", mmv)])
        qk4 = qk_ps.rearrange("p (h s c) -> p h s c", s=2, c=64)
        rB4 = ropeB.rearrange("p (h s c) -> p h s c", s=2, c=64)
        p.step("vector", lambda e, st_=st_: e.tensor_tensor(
            out=rB4[:, :, 0, :], in0=qk4[:, :, 1, :],
            in1=st_[:, 0:64].unsqueeze(1).broadcast_to([128, 5, 64]),
            op=ALU.mult))
        p.step("vector", lambda e, st_=st_: e.tensor_tensor(
            out=rB4[:, :, 1, :], in0=qk4[:, :, 0, :],
            in1=st_[:, 64:128].unsqueeze(1).broadcast_to([128, 5, 64]),
            op=ALU.mult))
        rv = p.step("vector", lambda e: e.tensor_tensor(
            out=qkr, in0=ropeA, in1=ropeB, op=ALU.add),
            waits=[("q_tr", p.val("q_tr"))], sig="q_rp")
        p.step("vector", lambda e, tt=tt: e.tensor_copy(
            v_sb[:, tt * 128:(tt + 1) * 128],
            PS[:, 5 * 512 + 128:5 * 512 + 256]), sig="q_rp")
        for hh in range(5):
            bnk = bankb(6 + hh % 2)
            tw2 = [("q_rp", rv), ("q_ev", p.val("q_ev") - 1)]
            tv = p.step("tensor", lambda e, hh=hh, bnk=bnk: e.transpose(
                bnk, qkr[:, hh * 128:(hh + 1) * 128], ident),
                waits=tw2, sig="q_tr")
            dst = (q_sb[:, hh * NT + tt * 128:hh * NT + (tt + 1) * 128]
                   if hh < 4 else k_sb[:, tt * 128:(tt + 1) * 128])
            p.step("vector", lambda e, dst=dst, bnk=bnk: e.tensor_copy(
                dst, bnk), waits=[("q_tr", tv)], sig="q_ev")

    QEV = p.val("q_ev")

    dq_jobs = []
    for i in range(HBAND // 128):
        for jh in range(2):
            dq_jobs.append(("w1", i, jh))
    for i in range(HBAND // 128):
        for jh in range(2):
            dq_jobs.append(("w2", i, jh))
    for i in range(D // 128):
        dq_jobs.append(("w3", i, 0))

    def deq_units():
        for i in range(4):
            cw = [("q_ev", QEV)] if i == 0 else [("wo_c", p.val("wo_c"))]
            lv = p.dma("gpsimd", cstb, wot_in[i * 128:(i + 1) * 128, :],
                       waits=cw, sig="wo_l")
            yield
            p.dma("gpsimd", b_wo[i * 128:(i + 1) * 128, :], cstb,
                  waits=[("wo_l", lv)], sig="wo_c")
            yield
        for (ui, (mat, i, jh)) in enumerate(dq_jobs):
            if mat == "w3":
                width, sigout = HBAND, "d_w3"
                src_ap = w3_in[i * 128:(i + 1) * 128, :]
            else:
                srcT = w1_in if mat == "w1" else w2_in
                width, sigout = CW, ("d_w1" if mat == "w1" else "d_w2")
                src_ap = srcT[i * 128:(i + 1) * 128, jh * CW:(jh + 1) * CW]
            G = width // 64
            nsub = width // 128
            wb = wbuf[ui % 2]
            lw = [("dq_u", p.val("dq_u") - 1)]
            if ui <= 1:
                lw.append(("q_ev", QEV))
            lv = p.dma("sync", wb[:, :width], src_ap, waits=lw, sig="dq_ld")
            yield
            amx = amax2[ui % 2]
            rcu = rcp2[ui % 2]
            p.fstep("vector", lambda e, wb=wb, amx=amx, G=G, width=width:
                    e.tensor_reduce(
                out=amx[:, :G],
                in_=wb[:, :width].rearrange("p (g k) -> p g k", k=64),
                axis=AX.X, op=ALU.max, apply_absolute_value=True),
                waits=[("dq_ld", lv), ("dq_q", p.val("dq_q") - 1)])
            yield
            p.fstep("vector", lambda e, amx=amx, rcu=rcu, G=G:
                    e.tensor_scalar_max(
                rcu[:, :G], amx[:, :G], 1e-38),
                waits=[("dq_u", p.val("dq_u") - 1)])
            rcv = p.fstep("vector", lambda e, rcu=rcu, G=G: e.reciprocal(
                rcu[:, :G], rcu[:, :G]), sig="dq_rc")
            yield
            ub = ubufs[ui % 2]
            uw = [("dq_rc", rcv), ("dq_p", p.val("dq_p") - 15)]
            if ui == 1:
                uw.append(("wo_c", p.val("wo_c")))
            uv = p.step("gpsimd", lambda e, wb=wb, ub=ub, rcu=rcu, G=G,
                        width=width: e.tensor_tensor(
                out=ub[:, :width].rearrange("p (g k) -> p g k", k=64),
                in0=wb[:, :width].rearrange("p (g k) -> p g k", k=64),
                in1=rcu[:, :G].unsqueeze(2).broadcast_to([128, G, 64]),
                op=ALU.mult),
                waits=uw, sig="dq_u")
            yield
            acu = accb[ui % 2]
            for j in range(15):
                pl = plane[j % 2]
                sw = [("dq_u", uv)]
                sw.append(("dq_f", p.val("dq_f") - (0 if j == 0 else 1)))
                sv = p.step("scalar", lambda e, pl=pl, ub=ub, j=j,
                            width=width:
                            e.activation(out=pl[:, :width],
                                         in_=ub[:, :width],
                                         func=AF.Sign, bias=nf4b[:, j:j + 1],
                                         scale=1.0),
                            waits=sw, sig="dq_p")
                if j == 0:
                    p.step("vector", lambda e, pl=pl, acu=acu, width=width:
                           e.tensor_scalar(
                        out=acu[:, :width], in0=pl[:, :width],
                        scalar1=float(NF4_DH[0]), scalar2=float(NF4_K0),
                        op0=ALU.mult, op1=ALU.add),
                        waits=[("dq_p", sv), ("dq_q", p.val("dq_q") - 1)],
                        sig="dq_f")
                else:
                    p.step("vector", lambda e, pl=pl, acu=acu,
                           dh=float(NF4_DH[j]), width=width:
                           e.scalar_tensor_tensor(
                        out=acu[:, :width], in0=pl[:, :width], scalar=dh,
                        in1=acu[:, :width], op0=ALU.mult, op1=ALU.add),
                        waits=[("dq_p", sv)], sig="dq_f")
                yield
            dqu = dqb[ui % 2]
            qv = p.step("gpsimd", lambda e, dqu=dqu, acu=acu, amx=amx,
                        G=G, width=width: e.tensor_tensor(
                out=dqu[:, :width].rearrange("p (g k) -> p g k", k=64),
                in0=acu[:, :width].rearrange("p (g k) -> p g k", k=64),
                in1=amx[:, :G].unsqueeze(2).broadcast_to([128, G, 64]),
                op=ALU.mult),
                waits=[("dq_f", p.val("dq_f")), ("dq_s", p.val("dq_s") - 1)],
                sig="dq_q")
            yield
            p.dma("sync", d_dq[ui, :, :width], dqu[:, :width],
                  waits=[("dq_q", qv)], sig="dq_s")
            yield

    _gen = deq_units()

    def pump(n):
        for _ in range(n):
            if next(_gen, "END") == "END":
                break


    p.wait("scalar", [("q_ev", p.val("q_ev"))])
    p.wait("tensor", [("q_ev", p.val("q_ev"))])
    p.wait("vector", [("q_ev", p.val("q_ev"))])
    p.step("vector", lambda e: e.memset(nrm, 0.0), sig="nm_v")

    def vec_norm_max(src, n512, out_idx):
        for kk in range(n512):
            sv = p.step("scalar", lambda e, kk=kk, src=src: e.activation(
                out=sqb, in_=src[:, kk * 512:(kk + 1) * 512], func=AF.Square),
                waits=[("nm_t", p.val("nm_t"))], sig="nm_a")
            bi = 4 + kk % 2
            mv = p.step("tensor", lambda e, bi=bi: e.matmul(
                PS[0:1, bi * 512:bi * 512 + 512], ones_b, sqb,
                start=True, stop=True),
                waits=[("nm_a", sv), ("nm_v", p.val("nm_v"))], sig="nm_t")
            p.fstep("vector", lambda e, bi=bi: e.tensor_reduce(
                out=nrm[0:1, 8:9], in_=PS[0:1, bi * 512:bi * 512 + 512]
                .rearrange("p (a c) -> p a c", a=1), axis=AX.X, op=ALU.max),
                waits=[("nm_t", mv)])
            p.step("vector", lambda e, oi=out_idx: e.tensor_tensor(
                out=nrm[0:1, oi:oi + 1], in0=nrm[0:1, oi:oi + 1],
                in1=nrm[0:1, 8:9], op=ALU.max), sig="nm_v")

    vec_norm_max(k_sb, NT // 512, 0)
    for hh in range(4):
        vec_norm_max(q_sb[:, hh * NT:(hh + 1) * NT], NT // 512, 1)
    p.fstep("vector", lambda e: e.tensor_tensor(
        out=nrm[0:1, 2:3], in0=nrm[0:1, 0:1], in1=nrm[0:1, 1:2], op=ALU.mult))
    sv = p.fstep("vector", lambda e: e.tensor_copy(nrm[0:1, 3:4], nrm[0:1, 2:3]),
                 sig="nm_v")
    av = p.step("scalar", lambda e: e.activation(
        out=nrm[0:1, 4:5], in_=nrm[0:1, 3:4], func=AF.Sqrt),
        waits=[("nm_v", sv)], sig="nm_a")
    bv = p.step("vector", lambda e: e.tensor_scalar(
        out=nrm[0:1, 5:6], in0=nrm[0:1, 4:5], scalar1=-SCALE, scalar2=-2.0,
        op0=ALU.mult, op1=ALU.add), waits=[("nm_a", av)], sig="nm_v")
    tv = p.step("tensor", lambda e: e.matmul(
        PS[:, 4 * 512:4 * 512 + 1], ones_f, nrm[0:1, 5:6],
        start=True, stop=True), waits=[("nm_v", bv)], sig="nm_t")
    bcv = p.step("vector", lambda e: e.tensor_copy(
        bias_sb, PS[:, 4 * 512:4 * 512 + 1]), waits=[("nm_t", tv)], sig="nm_v")
    p.wait("scalar", [("nm_v", bcv)])
    p.wait("tensor", [("nm_v", bcv)])

    def emit_sc(b, h, q0, kt):
        ps_s = bank(kt % 4)
        xv_now = p.val("at_x")
        mw = [("at_x", xv_now - 3)]
        mv = p.step("tensor", lambda e, kt=kt, b=b, q0=q0, h=h,
                    ps_s=ps_s: e.matmul(
            ps_s, k_sb[:, b * T + kt * 128:b * T + (kt + 1) * 128],
            q_sb[:, h * NT + q0:h * NT + q0 + 512],
            start=True, stop=True), waits=mw, sig="at_mm")
        dtile = kt - 4 * ((q0 % T) // 512)
        if dtile >= 0:
            mv2 = p.step("vector", lambda e, ps_s=ps_s, dt_=dtile:
                         e.tensor_tensor(out=ps_s, in0=ps_s,
                                         in1=masks[:, dt_, :], op=ALU.add),
                         waits=[("at_mm", mv)], sig="at_mk")
            ew = [("at_mk", mv2)]
        else:
            ew = [("at_mm", mv)]
        ab_ = attb[kt % 4]
        ew.append(("at_av", p.val("at_av")))
        p.step("scalar", lambda e, ps_s=ps_s, ab_=ab_:
               e.activation(out=ab_, in_=ps_s, func=AF.Exp,
                            bias=bias_sb[:, 0:1], scale=SCALE),
               waits=ew, sig="at_x")

    for b in range(B):
        for h in range(NQH):
            for s in range(8):
                q0 = b * T + s * 512
                nk = 4 * (s + 1)
                xbase = p.val("at_x")
                for j in range(min(3, nk)):
                    emit_sc(b, h, q0, j)
                av_b, sm_b = 4 + 2 * (s % 2), 5 + 2 * (s % 2)
                for kt in range(nk):
                    ab_ = attb[kt % 4]
                    aw = [("at_x", xbase + kt + 1)]
                    if kt == 0:
                        aw.append(("at_e", p.val("at_e") - 1))
                    p.step("tensor", lambda e, kt=kt, b=b, ab_=ab_, nk=nk,
                           av_b=av_b: e.matmul(
                        bank(av_b),
                        v_sb[:, b * T + kt * 128:b * T + (kt + 1) * 128], ab_,
                        start=(kt == 0), stop=(kt == nk - 1)),
                        waits=aw)
                    sw0 = [("at_rc", p.val("at_rc") - 1)] if kt == 0 else []
                    p.step("tensor", lambda e, kt=kt, ab_=ab_, nk=nk,
                           sm_b=sm_b: e.matmul(
                        PS[0:1, sm_b * 512:sm_b * 512 + 512], ones_b, ab_,
                        start=(kt == 0), stop=(kt == nk - 1)),
                        waits=sw0, sig="at_av")
                    if kt + 3 < nk:
                        emit_sc(b, h, q0, kt + 3)
                    pump(2)
                avc = p.val("at_av")
                rw = [("at_av", avc), ("at_r1", p.val("at_r1") - 16)]
                rcv2 = p.step("vector", lambda e, sm_b=sm_b: e.reciprocal(
                    rs_sm, PS[0:1, sm_b * 512:sm_b * 512 + 512]),
                    waits=rw, sig="at_rc")
                rrow = (b * NQH + h) * 8 + s
                r1 = p.dma("sync", d_rr[rrow:rrow + 1, :], rs_sm,
                           waits=[("at_rc", rcv2)], sig="at_r1")
                bcv = p.dma("sync", rbb[s % 2],
                            d_rr[rrow:rrow + 1, :].partition_broadcast(128),
                            waits=[("at_r1", r1),
                                   ("at_e", p.val("at_e") - 1)], sig="at_r2")
                ost = otst[s % 2]
                ew2 = [("at_r2", bcv), ("at_st", p.val("at_st") - 16)]
                ev = p.step("vector", lambda e, ost=ost, av_b=av_b, s=s:
                            e.tensor_tensor(out=ost, in0=bank(av_b),
                                            in1=rbb[s % 2], op=ALU.mult),
                            waits=ew2, sig="at_e")
                p.dma("sync", b_oT[h * 128:(h + 1) * 128, q0:q0 + 512], ost,
                      waits=[("at_e", ev)], sig="at_st")

    pump(10 ** 9)

    def ag(src, dst, waits):
        return p.step("gpsimd", lambda e, s=src, d=dst: e.collective_compute(
            "AllGather", ALU.bypass, replica_groups=[list(range(NC))],
            ins=[s], outs=[d.rearrange("a b c -> (a b) c")]),
            waits=waits, sig="cc")

    agwo = ag(b_wo, g_wo, [("wo_c", p.val("wo_c"))])
    agoT = ag(b_oT, g_oT, [("at_st", p.val("at_st"))])

    ATX = p.val("at_x")
    DQS = p.val("dq_s")
    tv_after = {}
    st_after = {}
    gcnt = 0
    for ui, (mat, i, jh) in enumerate(dq_jobs):
        width = HBAND if mat == "w3" else CW
        nsub = width // 128
        ldb = dqb[ui % 2]
        lw = [("dq_s", DQS)]
        if ui >= 2:
            lw.append(("t2_t", tv_after[ui - 2]))
        lv = p.dma("sync", ldb[:, :width], d_dq[ui, :, :width],
                   waits=lw, sig="t2_l")
        ts = tstage[ui % 2]
        for g0 in range(0, nsub, 8):
            gn = min(8, nsub - g0)
            bkid = gcnt % 2
            tv = None
            for k in range(g0, g0 + gn):
                tw = [("t2_l", lv)]
                if k == g0:
                    if gcnt == 0:
                        tw.append(("at_x", ATX))
                    if gcnt >= 2:
                        tw.append(("t2_e", p.val("t2_e") - 1))
                tv = p.step("tensor", lambda e, k=k, g0=g0, bkid=bkid,
                            ldb=ldb: e.transpose(
                    PSB[:, bkid * 1024 + (k - g0) * 128:
                        bkid * 1024 + (k - g0 + 1) * 128],
                    ldb[:, k * 128:(k + 1) * 128], ident),
                    waits=tw, sig="t2_t")
            ew = [("t2_t", tv)]
            if g0 == 0 and ui >= 2:
                ew.append(st_after[ui - 2])
            p.step("scalar", lambda e, g0=g0, gn=gn, bkid=bkid, ts=ts:
                   e.copy(ts[:, g0 * 128:(g0 + gn) * 128],
                          PSB[:, bkid * 1024:bkid * 1024 + gn * 128]),
                   waits=ew, sig="t2_e")
            gcnt += 1
        tv_after[ui] = p.val("t2_t")
        if mat == "w3":
            dst_ap = b_w3[i // 4, :, :, (i % 4) * 128:(i % 4 + 1) * 128]
            sigout = "d_w3"
        else:
            dstT = b_w1 if mat == "w1" else b_w2
            dst_ap = dstT[:, i, jh * 16:(jh + 1) * 16, :]
            sigout = "d_w1" if mat == "w1" else "d_w2"
        p.dma("sync", dst_ap,
              ts[:, :nsub * 128].rearrange("p (k r) -> p k r", r=128),
              waits=[("t2_e", p.val("t2_e"))], sig=sigout)
        st_after[ui] = (sigout, p.val(sigout))

    agw1 = p.step("gpsimd", lambda e: e.collective_compute(
        "AllGather", ALU.bypass, replica_groups=[list(range(NC))],
        ins=[b_w1.rearrange("p a b c -> (p a b) c")],
        outs=[g_w1.rearrange("n p a b c -> (n p a b) c")]),
        waits=[("d_w1", p.val("d_w1"))], sig="cc")
    agw2 = p.step("gpsimd", lambda e: e.collective_compute(
        "AllGather", ALU.bypass, replica_groups=[list(range(NC))],
        ins=[b_w2.rearrange("p a b c -> (p a b) c")],
        outs=[g_w2.rearrange("n p a b c -> (n p a b) c")]),
        waits=[("d_w2", p.val("d_w2"))], sig="cc")
    agw3 = p.step("gpsimd", lambda e: e.collective_compute(
        "AllGather", ALU.bypass, replica_groups=[list(range(NC))],
        ins=[b_w3.rearrange("a p b c -> (a p b) c")],
        outs=[g_w3.rearrange("n a p b c -> (n a p b) c")]),
        waits=[("d_w3", p.val("d_w3"))], sig="cc")

    ar.reset(m0)
    a_n = ar.alloc([128, 32 * 1024], BF16)
    wo_ch = [ar.alloc([128, 512], BF16) for _ in range(2)]
    yst = [ar.alloc([128, 512], F32) for _ in range(2)]
    ysq = ar.alloc([128, 512], F32)

    p.wait("sync", [("cc", agoT), ("d_w3", p.val("d_w3"))])
    l_oT = dint("l_oT", [NC, DQ, TLOC], BF16)
    slv = p.step("sync", lambda e: e.dma_start(
        out=l_oT.rearrange("a b c -> (a b) c"),
        in_=g_oT.rearrange("a b c -> (a b) c")[:, pid_of(e)]),
        sig="p4_l", inc=16)
    for hg in range(32):
        bandi, hl = hg // 4, hg % 4
        p.dma("sync", a_n[:, hg * 1024:(hg + 1) * 1024],
              l_oT[bandi, hl * 128:(hl + 1) * 128, :],
              waits=[("p4_l", slv)], sig="p4_n")

    if DEBUG:
        d_an = dint("d_an", [128, 32 * 1024], BF16)
        p.dma("sync", d_an, a_n, waits=[("p4_n", p.val("p4_n"))], sig="p4_l")
    p.wait("tensor", [("p4_n", p.val("p4_n")), ("cc", agwo),
                      ("t2_e", p.val("t2_e"))])
    p.step("vector", lambda e: e.memset(mean8, 0.0))
    p.step("vector", lambda e: e.memset(msq8, 0.0), sig="p4_z")
    p.wait("scalar", [("p4_z", p.val("p4_z"))])
    a_nv = a_n.rearrange("p (g t) -> p g t", t=1024)
    wosp = [ar.alloc([128, 32, 512], BF16) for _ in range(2)]
    for ds in range(8):
        wsp = wosp[ds % 2]
        lw = [("wo_m", p.val("wo_m") - 8)] if ds >= 2 else \
            [("d_w3", p.val("d_w3"))]
        lv = p.dma("sync", wsp,
                   g_wo.rearrange("a (k p) c -> p (a k) c", p=128)
                   [:, :, ds * 512:(ds + 1) * 512], waits=lw, sig="wo_ld")
        p.wait("tensor", [("wo_ld", lv), ("wo_e", p.val("wo_e"))])
        for ti in range(8):
            for hc in range(32):
                p.step("tensor", lambda e, hc=hc, ti=ti, wsp=wsp: e.matmul(
                    bank(ti), a_nv[:, hc, ti * 128:(ti + 1) * 128],
                    wsp[:, hc, :], start=(hc == 0), stop=(hc == 31)),
                    sig="wo_m" if hc == 31 else None)
        mvw = p.val("wo_m")
        for ti in range(8):
            ys = yst[ti % 2]
            ew = [("wo_m", mvw), ("wo_st", p.val("wo_st") - 16)]
            ev = p.step("vector", lambda e, ys=ys, ti=ti: e.tensor_copy(
                ys, bank(ti)), waits=ew, sig="wo_e")
            p.fstep("vector", lambda e, ys=ys: e.tensor_reduce(
                out=sm2[:, 6:7], in_=ys.rearrange("p (a c) -> p a c", a=1),
                axis=AX.X, op=ALU.add))
            p.fstep("vector", lambda e, ti=ti: e.tensor_tensor(
                out=mean8[:, ti:ti + 1], in0=mean8[:, ti:ti + 1],
                in1=sm2[:, 6:7], op=ALU.add))
            sv = p.step("scalar", lambda e, ys=ys: e.activation(
                out=ysq, in_=ys, func=AF.Square, accum_out=sm2[:, 7:8]),
                waits=[("wo_e", ev), ("wo_v", p.val("wo_v"))], sig="wo_a")
            p.step("vector", lambda e, ti=ti: e.tensor_tensor(
                out=msq8[:, ti:ti + 1], in0=msq8[:, ti:ti + 1],
                in1=sm2[:, 7:8], op=ALU.add), waits=[("wo_a", sv)],
                sig="wo_v")
            p.wait("vector", [("wo_v", p.val("wo_v"))])
            p.dma("sync", s_y[ti * 128:(ti + 1) * 128, ds * 512:(ds + 1) * 512],
                  ys, waits=[("wo_e", ev)], sig="wo_st")

    p.wait("vector", [("wo_v", p.val("wo_v"))])
    p.fstep("vector", lambda e: e.tensor_scalar_mul(mean8, mean8, 1.0 / D))
    p.fstep("vector", lambda e: e.tensor_scalar_mul(msq8, msq8, 1.0 / D))
    p.fstep("vector", lambda e: e.tensor_tensor(
        out=sm2[:, 0:8], in0=mean8, in1=mean8, op=ALU.mult))
    p.fstep("vector", lambda e: e.tensor_tensor(
        out=msq8, in0=msq8, in1=sm2[:, 0:8], op=ALU.subtract))
    vv = p.step("vector", lambda e: e.tensor_scalar_add(msq8, msq8, EPS),
                sig="p4_v")
    sv = p.step("scalar", lambda e: e.activation(
        out=str_, in_=msq8, func=AF.Sqrt), waits=[("p4_v", vv)], sig="p4_a")
    p.fstep("vector", lambda e: e.reciprocal(str_, str_), waits=[("p4_a", sv)])
    p.fstep("vector", lambda e: e.tensor_scalar_mul(stm, mean8, -1.0))

    ar.reset(m0)
    xb4 = [ar.alloc([128, D], F32) for _ in range(2)]
    yb4 = [ar.alloc([128, D], F32) for _ in range(2)]
    scr4 = ar.alloc([128, D], F32)
    x1b = ar.alloc([128, D], F32)
    lnb2 = ar.alloc([128, D], BF16)
    tst2 = ar.alloc([128, 32 * 128], BF16)

    wv = p.val("wo_st")
    for i in range(8):
        x_t, y_t = xb4[i % 2], yb4[i % 2]
        lw = [("p4_x", p.val("p4_x") - 1)] if i >= 2 else []
        p.dma("sync", x_t, x_in[i * 128:(i + 1) * 128, :],
              waits=[("wo_st", wv)] + lw, sig="p4_l2")
        lv2 = p.dma("sync", y_t, s_y[i * 128:(i + 1) * 128, :], sig="p4_l2")
        p.wait("vector", [("p4_l2", lv2)])
        p.step("vector", lambda e, y=y_t, i=i: e.tensor_scalar(
            out=scr4, in0=y, scalar1=stm[:, i:i + 1], scalar2=str_[:, i:i + 1],
            op0=ALU.add, op1=ALU.mult))
        x1v = p.step("vector", lambda e, x=x_t: e.tensor_tensor(
            out=x1b, in0=scr4, in1=x, op=ALU.add),
            waits=[("p4_s", p.val("p4_s") - 16)], sig="p4_x")
        p.dma("sync", s_x1[i * 128:(i + 1) * 128, :], x1b,
              waits=[("p4_x", x1v)], sig="p4_s")
        ln_prep(x1b)
        lnv = p.step("vector", lambda e: e.tensor_scalar(
            out=lnb2, in0=x1b, scalar1=sm2[:, 4:5], scalar2=sm2[:, 5:6],
            op0=ALU.add, op1=ALU.mult),
            waits=[("p4_t", p.val("p4_t"))], sig="ln_v")
        for k in range(32):
            bnk = bankb(k % 2)
            tw = [("ln_v", lnv)]
            if k >= 2:
                tw.append(("p4_e", p.val("p4_e") - 1))
            else:
                tw.append(("p4_s", p.val("p4_s")))
            tv = p.step("tensor", lambda e, k=k, bnk=bnk: e.transpose(
                bnk, lnb2[:, k * 128:(k + 1) * 128], ident),
                waits=tw, sig="p4_t")
            p.step("vector", lambda e, k=k, bnk=bnk: e.tensor_copy(
                tst2[:, k * 128:(k + 1) * 128], bnk),
                waits=[("p4_t", tv)], sig="p4_e")
        p.dma("sync", s_ln2.rearrange("(k q) r -> q k r", q=128)
              [:, :, i * 128:(i + 1) * 128],
              tst2.rearrange("p (k r) -> p k r", r=128),
              waits=[("p4_e", p.val("p4_e"))], sig="p4_s")

    ar.reset(m0)
    h_sb = ar.alloc([128, 88 * TCH], BF16)
    ln2c = ar.alloc([128, 32 * TCH], BF16)
    w1c = [ar.alloc([128, 32 * 128], BF16) for _ in range(2)]
    w2c = [ar.alloc([128, 32 * 128], BF16) for _ in range(2)]
    w3bb = [ar.alloc([128, (HBAND // 128) * 512], BF16) for _ in range(2)]
    sgb = ar.alloc([128, TCH], F32)
    sgb2 = ar.alloc([128, TCH], F32)
    y2st = [ar.alloc([128, 512], F32) for _ in range(2)]
    y2sq = sgb

    p.step("vector", lambda e: e.memset(mean8, 0.0))
    p.step("vector", lambda e: e.memset(msq8, 0.0), sig="f_z")
    p.wait("scalar", [("f_z", p.val("f_z"))])
    p.wait("sync", [("p4_s", p.val("p4_s")), ("cc", agw3)])
    for ci in range(2):
        lnw = [("f_g", p.val("f_g"))] if ci else []
        lv = p.dma("sync", ln2c.rearrange("p (k c) -> p k c", c=TCH),
                   s_ln2.rearrange("(k p) c -> p k c", p=128)
                   [:, :, ci * TCH:(ci + 1) * TCH], waits=lnw, sig="f_l")
        p.wait("tensor", [("f_l", lv), ("cc", agw2)])
        hs_guard = p.val("f_dm")
        for ht in range(88):
            bandi, hl = ht // 11, (ht % 11) * 128
            wa, wb_ = w1c[ht % 2], w2c[ht % 2]
            lw = [("f_su", p.val("f_su") - 1)] if (ci or ht >= 2) else []
            p.dma("sync", wa.rearrange("p (k c) -> p k c", c=128),
                  g_w1[bandi][:, ht % 11, :, :], waits=lw, sig="f_wl")
            wlv = p.dma("sync", wb_.rearrange("p (k c) -> p k c", c=128),
                        g_w2[bandi][:, ht % 11, :, :], sig="f_wl")
            bg, bu = (0, 1) if ht % 2 == 0 else (6, 7)
            sgx = sgb if ht % 2 == 0 else sgb2
            gw = [("f_wl", wlv), ("f_su", p.val("f_su") - 1)]
            p.wait("tensor", gw)
            for Dc in range(32):
                p.step("tensor", lambda e, Dc=Dc, wa=wa, bg=bg: e.matmul(
                    bank(bg), wa[:, Dc * 128:(Dc + 1) * 128],
                    ln2c[:, Dc * TCH:(Dc + 1) * TCH],
                    start=(Dc == 0), stop=(Dc == 31)))
                p.step("tensor", lambda e, Dc=Dc, wb_=wb_, bu=bu: e.matmul(
                    bank(bu), wb_[:, Dc * 128:(Dc + 1) * 128],
                    ln2c[:, Dc * TCH:(Dc + 1) * TCH],
                    start=(Dc == 0), stop=(Dc == 31)),
                    sig="f_g" if Dc == 31 else None)
            gv = p.val("f_g")
            sv = p.step("scalar", lambda e, bg=bg, sgx=sgx: e.activation(
                out=sgx, in_=bank(bg), func=AF.Silu), waits=[("f_g", gv)],
                sig="f_sl")
            hw = [("f_sl", sv)]
            if ci and ht == 0:
                hw.append(("f_dm", hs_guard))
            p.step("vector", lambda e, ht=ht, bu=bu, sgx=sgx: e.tensor_tensor(
                out=h_sb[:, ht * TCH:(ht + 1) * TCH], in0=sgx, in1=bank(bu),
                op=ALU.mult), waits=hw, sig="f_su")
        huv = p.val("f_su")
        p.wait("tensor", [("f_su", huv)])
        gidx = 0
        for ds in range(8):
            for bandi in range(8):
                wbb = w3bb[gidx % 2]
                lw = [("f_dm", p.val("f_dm") - 11)] if (ci or gidx >= 2) else []
                wlv = p.dma("sync", wbb.rearrange("p (a c) -> p a c", c=512),
                            g_w3[bandi, ds], waits=lw, sig="f_wl")
                p.wait("tensor", [("f_wl", wlv)])
                if bandi == 0:
                    p.wait("tensor", [("f_de", p.val("f_de"))])
                for htl in range(11):
                    ht = bandi * 11 + htl
                    for ti in range(4):
                        p.step("tensor", lambda e, ht=ht, ti=ti, wbb=wbb,
                               htl=htl: e.matmul(
                            bank(2 + ti),
                            h_sb[:, ht * TCH + ti * 128:
                                 ht * TCH + (ti + 1) * 128],
                            wbb[:, htl * 512:(htl + 1) * 512],
                            start=(ht == 0), stop=(ht == 87)),
                            sig="f_dm" if ti == 3 else None)
                gidx += 1
            dmv = p.val("f_dm")
            for ti in range(4):
                tt2 = ci * 4 + ti
                ys = y2st[ti % 2]
                ew = [("f_dm", dmv)]
                ew.append(("f_st", p.val("f_st") - 16))
                ev = p.step("vector", lambda e, ti=ti, ys=ys: e.tensor_copy(
                    ys, bank(2 + ti)), waits=ew, sig="f_de")
                p.fstep("vector", lambda e, ys=ys: e.tensor_reduce(
                    out=sm2[:, 6:7], in_=ys.rearrange("p (a c) -> p a c", a=1),
                    axis=AX.X, op=ALU.add))
                p.fstep("vector", lambda e, tt2=tt2: e.tensor_tensor(
                    out=mean8[:, tt2:tt2 + 1], in0=mean8[:, tt2:tt2 + 1],
                    in1=sm2[:, 6:7], op=ALU.add))
                sv = p.step("scalar", lambda e, ys=ys: e.activation(
                    out=y2sq, in_=ys, func=AF.Square, accum_out=sm2[:, 7:8]),
                    waits=[("f_de", ev), ("f_v2", p.val("f_v2"))], sig="f_a")
                p.step("vector", lambda e, tt2=tt2: e.tensor_tensor(
                    out=msq8[:, tt2:tt2 + 1], in0=msq8[:, tt2:tt2 + 1],
                    in1=sm2[:, 7:8], op=ALU.add), waits=[("f_a", sv)],
                    sig="f_v2")
                p.wait("vector", [("f_v2", p.val("f_v2"))])
                p.dma("sync", s_y2[tt2 * 128:(tt2 + 1) * 128,
                                   ds * 512:(ds + 1) * 512], ys,
                      waits=[("f_de", ev)], sig="f_st")

    p.wait("vector", [("f_v2", p.val("f_v2"))])
    p.fstep("vector", lambda e: e.tensor_scalar_mul(mean8, mean8, 1.0 / D))
    p.fstep("vector", lambda e: e.tensor_scalar_mul(msq8, msq8, 1.0 / D))
    p.fstep("vector", lambda e: e.tensor_tensor(
        out=sm2[:, 0:8], in0=mean8, in1=mean8, op=ALU.mult))
    p.fstep("vector", lambda e: e.tensor_tensor(
        out=msq8, in0=msq8, in1=sm2[:, 0:8], op=ALU.subtract))
    vv = p.step("vector", lambda e: e.tensor_scalar_add(msq8, msq8, EPS),
                sig="f_v")
    sv = p.step("scalar", lambda e: e.activation(
        out=str_, in_=msq8, func=AF.Sqrt), waits=[("f_v", vv)], sig="f_a2")
    p.fstep("vector", lambda e: e.reciprocal(str_, str_), waits=[("f_a2", sv)])
    p.fstep("vector", lambda e: e.tensor_scalar_mul(stm, mean8, -1.0))

    ar.reset(m0)
    xb5 = [ar.alloc([128, D], F32) for _ in range(2)]
    yb5 = [ar.alloc([128, D], F32) for _ in range(2)]
    scr5 = ar.alloc([128, D], F32)
    fout = ar.alloc([128, D], F32)

    fv = p.val("f_st")
    for i in range(8):
        x_t, y_t = xb5[i % 2], yb5[i % 2]
        lw = [("fin_o", p.val("fin_o") - 1)] if i >= 2 else []
        p.dma("sync", x_t, s_x1[i * 128:(i + 1) * 128, :],
              waits=[("f_st", fv)] + lw, sig="fin_l")
        lv2 = p.dma("sync", y_t, s_y2[i * 128:(i + 1) * 128, :], sig="fin_l")
        p.wait("vector", [("fin_l", lv2)])
        p.step("vector", lambda e, y=y_t, i=i: e.tensor_scalar(
            out=scr5, in0=y, scalar1=stm[:, i:i + 1], scalar2=str_[:, i:i + 1],
            op0=ALU.add, op1=ALU.mult))
        ov = p.step("vector", lambda e, x=x_t: e.tensor_tensor(
            out=fout, in0=scr5, in1=x, op=ALU.add),
            waits=[("fin_s", p.val("fin_s"))], sig="fin_o")
        p.dma("sync", out_d[i * 128:(i + 1) * 128, :], fout,
              waits=[("fin_o", ov)], sig="fin_s")

    if DEBUG:
        dbg = {
            "dbg_oT": b_oT, "dbg_y": s_y, "dbg_x1": s_x1,
            "dbg_ln2": s_ln2, "dbg_y2": s_y2, "dbg_loT": l_oT,
            "dbg_an": d_an, "dbg_gwo": g_wo,
        }
        for nm, src_t in dbg.items():
            od2 = nc.dram_tensor(nm, list(src_t.tensor.shape),
                                 src_t.dtype, kind="ExternalOutput").ap()
            p.dma("sync", od2, src_t, waits=[("fin_s", p.val("fin_s"))],
                  sig="dbg_s")
        dsb = nc.dram_tensor("dbg_sb", [128, 64], F32,
                             kind="ExternalOutput").ap()
        dsb_st = ar.alloc([128, 64], F32)
        p.step("vector", lambda e: e.tensor_copy(dsb_st[:, 0:1], bias_sb),
               waits=[("fin_o", p.val("fin_o"))])
        p.step("vector", lambda e: e.tensor_copy(dsb_st[:, 1:17], sm2))
        p.step("vector", lambda e: e.tensor_copy(dsb_st[:, 17:25], mean8))
        p.step("vector", lambda e: e.tensor_copy(dsb_st[:, 25:33], msq8))
        p.step("vector", lambda e: e.tensor_copy(dsb_st[:, 33:41], stm))
        dv_ = p.step("vector", lambda e: e.tensor_copy(dsb_st[:, 41:49], str_),
                     sig="dbg_v")
        p.step("vector", lambda e: e.tensor_copy(
            dsb_st[0:1, 48:64], nrm), sig="dbg_v")
        p.dma("sync", dsb, dsb_st, waits=[("dbg_v", p.val("dbg_v"))],
              sig="dbg_s")
        p.wait("sync", [("dbg_s", p.val("dbg_s"))])
    p.wait("sync", [("fin_s", p.val("fin_s"))])
    return nc, p


_CACHE = {}


def _get_built():
    if "nc" not in _CACHE:
        nc, p = build_program()
        with nc.Block() as blk:
            for name in Prog.ENGS:
                fns = p.q[name]

                def body(e, fns=fns):
                    for f in fns:
                        f(e)
                getattr(blk, name)(body)
        _CACHE["nc"] = nc
    return _CACHE["nc"]


def kernel(L0_attn_x, L0_attn_Wq, L0_attn_Wk, L0_attn_Wv, L0_attn_Wo,
           L0_ffn_w1, L0_ffn_w2, L0_ffn_w3):
    x = np.ascontiguousarray(np.asarray(L0_attn_x, np.float32).reshape(NT, D))
    WqT = np.ascontiguousarray(np.asarray(L0_attn_Wq, np.float32).T)
    WkT = np.ascontiguousarray(np.asarray(L0_attn_Wk, np.float32).T)
    WvT = np.ascontiguousarray(np.asarray(L0_attn_Wv, np.float32).T)
    WoT = np.ascontiguousarray(np.asarray(L0_attn_Wo, np.float32).T)
    w1p = np.zeros((HPAD, D), np.float32)
    w1p[:H] = np.asarray(L0_ffn_w1, np.float32)
    w2p = np.zeros((HPAD, D), np.float32)
    w2p[:H] = np.asarray(L0_ffn_w2, np.float32)
    w3p = np.zeros((D, HPAD), np.float32)
    w3p[:, :H] = np.asarray(L0_ffn_w3, np.float32)

    cos_t, sinS_t = _rope_tables()
    kk = np.arange(128)[:, None]
    qq = np.arange(512)[None, :]
    mask_t = np.stack([np.where(i * 128 + kk <= qq, 0.0, MVAL)
                       for i in range(4)]).astype(np.float32)
    ident_t = np.eye(128, dtype=ml_dtypes.bfloat16)
    nf4b_t = np.tile(-NF4_B, (128, 1)).astype(np.float32)

    nc = _get_built()
    in_maps = []
    for c in range(NC):
        in_maps.append(dict(
            x_c=np.ascontiguousarray(x[c * TLOC:(c + 1) * TLOC]),
            wqt_c=np.ascontiguousarray(WqT[:, c * DQ:(c + 1) * DQ]),
            wkt_c=np.ascontiguousarray(WkT[:, c * HD:(c + 1) * HD]),
            wvt_c=np.ascontiguousarray(WvT[:, c * HD:(c + 1) * HD]),
            wot_c=np.ascontiguousarray(WoT[c * DQ:(c + 1) * DQ, :]),
            w1_c=np.ascontiguousarray(w1p[c * HBAND:(c + 1) * HBAND]),
            w2_c=np.ascontiguousarray(w2p[c * HBAND:(c + 1) * HBAND]),
            w3_c=np.ascontiguousarray(w3p[:, c * HBAND:(c + 1) * HBAND]),
            cos_t=cos_t, sinS_t=sinS_t, mask_t=mask_t, ident_t=ident_t,
            nf4b_t=nf4b_t,
        ))
    res = run_bass_kernel_spmd(nc, in_maps, core_ids=list(range(NC)))
    out = np.concatenate([res.results[c]["out_c"] for c in range(NC)], axis=0)
    if DEBUG:
        return out.reshape(B, T, D), res
    return out.reshape(B, T, D)


# version 18
# speedup vs baseline: 1.1138x; 1.0165x over previous
import math
import numpy as np
import ml_dtypes
import concourse.bass as bass
import concourse.mybir as mybir
from concourse.bass_utils import run_bass_kernel_spmd

F32 = mybir.dt.float32
BF16 = mybir.dt.bfloat16
AF = mybir.ActivationFunctionType
ALU = mybir.AluOpType
AX = mybir.AxisListType

NC = 8
D = 4096
T = 4096
B = 2
NT = B * T
TLOC = NT // NC
HD = 128
NQH = 4
DQ = NQH * HD
H = 11008
HPAD = 11264
HBAND = HPAD // NC
SCALE = 1.0 / math.sqrt(HD)
EPS = 1e-5
MVAL = -1e30
TCH = 512

NF4 = np.array([-1.0, -0.6961928009986877, -0.5250730514526367,
                -0.39491748809814453, -0.28444138169288635,
                -0.18477343022823334, -0.09105003625154495, 0.0,
                0.07958029955625534, 0.16093020141124725,
                0.24611230194568634, 0.33791524171829224,
                0.44070982933044434, 0.5626170039176941,
                0.7229568362236023, 1.0], dtype=np.float32)
NF4_B = ((NF4[:-1] + NF4[1:]) * np.float32(0.5)).astype(np.float32)
NF4_DH = ((NF4[1:] - NF4[:-1]) * np.float32(0.5)).astype(np.float32)
NF4_K0 = float(np.float32(NF4[0] + np.float64(NF4_DH.astype(np.float64).sum())))


class Prog:
    ENGS = ("sync", "gpsimd", "tensor", "vector", "scalar")

    def __init__(self, nc):
        self.nc = nc
        self.q = {e: [] for e in self.ENGS}
        self.sems = {}

    def _sem(self, name):
        if name not in self.sems:
            self.sems[name] = [self.nc.alloc_semaphore(name), 0]
        return self.sems[name]

    def val(self, name):
        return self._sem(name)[1]

    def step(self, eng, build, waits=(), sig=None, inc=1):
        tgt, s = None, None
        if sig is not None:
            s = self._sem(sig)
            s[1] += inc
            tgt = s[1]
        wl = tuple((self._sem(n)[0], v) for (n, v) in waits if v > 0)

        def f(e, wl=wl, build=build, s=s, inc=inc):
            for (hh, v) in wl:
                e.wait_ge(hh, v)
            ins = build(e)
            if s is not None:
                ins.then_inc(s[0], inc)
        self.q[eng].append(f)
        return tgt

    def fstep(self, eng, build, waits=(), sig=None, inc=1):
        s = sig if sig is not None else ("F_" + eng)
        v = self.step(eng, build, waits, s, inc)
        self.wait(eng, [(s, v)])
        return v

    def wait(self, eng, waits):
        wl = tuple((self._sem(n)[0], v) for (n, v) in waits if v > 0)

        def f(e, wl=wl):
            for (hh, v) in wl:
                e.wait_ge(hh, v)
        self.q[eng].append(f)

    def dma(self, eng, out, in_, waits=(), sig="dma"):
        return self.step(eng, lambda e, o=out, i=in_: e.dma_start(out=o, in_=i),
                         waits=waits, sig=sig, inc=16)


class Arena:
    def __init__(self, nc, base, top, tag):
        self.nc, self.base, self.top, self.off = nc, base, top, base
        self.tag, self.n = tag, 0

    def alloc(self, shape, dtype):
        nb = int(np.prod(shape[1:])) * mybir.dt.size(dtype)
        off = (self.off + 31) & ~31
        assert off + nb <= self.top, f"sbuf overflow {off}+{nb}>{self.top}"
        self.off = off + nb
        self.n += 1
        return self.nc.alloc_sbuf_tensor_at(
            f"{self.tag}{self.n}", list(shape), dtype, offset=off).ap()

    def mark(self):
        return self.off

    def reset(self, m):
        self.off = m


def _rope_tables():
    base = 500000.0
    inv = 1.0 / base ** (np.arange(0, HD, 2, dtype=np.float32) / HD)
    factor, low_f, high_f, old_ctx = 32.0, 1.0, 4.0, 8192
    low_wl, high_wl = old_ctx / low_f, old_ctx / high_f
    wl = 2 * math.pi / inv
    inv_l = np.where(wl > low_wl, inv / factor, inv)
    smooth = (old_ctx / wl - low_f) / (high_f - low_f)
    smoothed = (1 - smooth) * inv_l / factor + smooth * inv_l
    med = ~(wl < high_wl) & ~(wl > low_wl)
    freq = np.where(med, smoothed, inv_l).astype(np.float32)
    ang = np.arange(T, dtype=np.float32)[:, None] * freq[None, :]
    emb = np.concatenate([ang, ang], axis=1)
    cos = np.cos(emb).astype(np.float32)
    sinS = np.sin(emb).astype(np.float32)
    sinS[:, :HD // 2] *= -1.0
    return cos, sinS


DEBUG = False


def build_program():
    nc = bass.Bass("TRN2", target_bir_lowering=False)
    p = Prog(nc)

    def din(name, shape, dt=F32):
        return nc.dram_tensor(name, list(shape), dt, kind="ExternalInput").ap()

    x_in = din("x_c", [TLOC, D])
    wqt_in = din("wqt_c", [D, DQ])
    wkt_in = din("wkt_c", [D, HD])
    wvt_in = din("wvt_c", [D, HD])
    wot_in = din("wot_c", [DQ, D])
    w1_in = din("w1_c", [HBAND, D])
    w2_in = din("w2_c", [HBAND, D])
    w3_in = din("w3_c", [D, HBAND])
    cos_in = din("cos_t", [T, HD])
    sin_in = din("sinS_t", [T, HD])
    mask_in = din("mask_t", [4, 128, 512], BF16)
    id_in = din("ident_t", [128, 128], BF16)
    nf4b_in = din("nf4b_t", [128, 15])

    out_d = nc.dram_tensor("out_c", [TLOC, D], F32, kind="ExternalOutput").ap()

    def dint(name, shape, dt):
        return nc.dram_tensor(name, list(shape), dt, kind="Internal").ap()

    b_ln1 = dint("b_ln1", [128, TLOC // 128, 32, 128], BF16)
    g_ln1 = dint("g_ln1", [NC, 128, TLOC // 128, 32, 128], BF16)
    b_w1 = dint("b_w1", [128, HBAND // 128, 32, 128], BF16)
    b_w2 = dint("b_w2", [128, HBAND // 128, 32, 128], BF16)
    b_w3 = dint("b_w3", [8, 128, HBAND // 128, 512], BF16)
    g_w1 = dint("g_w1", [NC, 128, HBAND // 128, 32, 128], BF16)
    g_w2 = dint("g_w2", [NC, 128, HBAND // 128, 32, 128], BF16)
    g_w3 = dint("g_w3", [NC, 8, 128, HBAND // 128, 512], BF16)
    b_wo = dint("b_wo", [DQ, D], BF16)
    b_oT = dint("b_oT", [DQ, NT], BF16)
    d_dq = dint("d_dq", [76, 128, 2048], BF16)
    d_rr = dint("d_rr", [64, 512], F32)
    s_x1 = dint("s_x1", [TLOC, D], F32)
    s_ln2 = dint("s_ln2", [D, TLOC], BF16)
    s_y2 = dint("s_y2", [TLOC, D], F32)

    PS = nc.alloc_psum_tensor("PS", [128, 4096], F32).ap()
    PSB = PS.bitcast(BF16)

    def bank(i):
        return PS[:, i * 512:(i + 1) * 512]

    def bankb(i):
        return PSB[:, i * 1024:i * 1024 + 128]

    SB_TOP = 222 * 1024
    pers = Arena(nc, 24 * 1024, 34 * 1024, "pe")
    ar = Arena(nc, 34 * 1024, SB_TOP, "ar")

    ident = pers.alloc([128, 128], BF16)
    ones_b = pers.alloc([128, 1], BF16)
    ones_f = pers.alloc([1, 128], F32)
    masks = pers.alloc([128, 4, 512], BF16)
    small = pers.alloc([128, 16], F32)
    sm2 = pers.alloc([128, 16], F32)
    nrm = pers.alloc([1, 16], F32)
    bias_sb = pers.alloc([128, 1], F32)
    mean8 = pers.alloc([128, 8], F32)
    msq8 = pers.alloc([128, 8], F32)
    stm = pers.alloc([128, 8], F32)
    str_ = pers.alloc([128, 8], F32)
    st6 = pers.alloc([128, 48], F32)
    nf4b = pers.alloc([128, 15], F32)
    dh_sb = pers.alloc([128, 15], BF16)

    _pid = {}

    def pid_of(e):
        if id(e) not in _pid:
            _pid[id(e)] = bass.ts(e.partition_id(), TLOC)
        return _pid[id(e)]

    p.q["sync"].append(lambda e: pid_of(e))

    p.dma("sync", ident, id_in, sig="cst")
    p.dma("sync", masks, mask_in.transpose([1, 0, 2]), sig="cst")
    p.dma("sync", nf4b, nf4b_in, sig="cst")
    p.step("gpsimd", lambda e: e.memset(ones_b, 1.0), sig="cst")
    p.step("gpsimd", lambda e: e.memset(ones_f, 1.0), sig="cst")
    for j in range(15):
        p.step("gpsimd", lambda e, j=j: e.memset(
            dh_sb[:, j:j + 1], float(NF4_DH[j])), sig="cst")
    CST = p.val("cst")

    m0 = ar.mark()
    q_sb = ar.alloc([128, 4 * NT], BF16)
    k_sb = ar.alloc([128, NT], BF16)
    v_sb = ar.alloc([128, NT], BF16)
    attb = [ar.alloc([128, 512], BF16) for _ in range(4)]
    otst = [ar.alloc([128, 512], BF16) for _ in range(2)]
    rs_sm = ar.alloc([1, 512], F32)
    rbb = [ar.alloc([128, 512], F32) for _ in range(2)]
    sqb = ar.alloc([128, 512], BF16)
    mSh = ar.mark()
    arD = Arena(nc, mSh, SB_TOP, "dq")
    CW = 2048
    wbuf = [arD.alloc([128, CW], F32) for _ in range(2)]
    ubuf = arD.alloc([128, CW], F32)
    plane = [arD.alloc([128, CW], BF16) for _ in range(2)]
    accb = [arD.alloc([128, CW], F32) for _ in range(2)]
    dqb = [arD.alloc([128, CW], BF16) for _ in range(2)]
    tstage = [arD.alloc([128, 16 * 128], BF16) for _ in range(2)]
    amax2 = [arD.alloc([128, 32], F32) for _ in range(2)]
    rcp2 = [arD.alloc([128, 32], F32) for _ in range(2)]
    cstb = arD.alloc([128, D], BF16)
    ubufs = [ubuf, cstb.bitcast(F32)]

    xb_ln = [ar.alloc([128, D], F32) for _ in range(2)]
    lnb = ar.alloc([128, D], BF16)
    tstage_l = ar.alloc([128, 32 * 128], BF16)

    def ln_prep(xsrc):
        for ci6 in range(8):
            p.fstep("vector", lambda e, x=xsrc, ci6=ci6: e.bn_stats(
                st6[:, ci6 * 6:(ci6 + 1) * 6],
                x[:, ci6 * 512:(ci6 + 1) * 512]))
        p.fstep("vector", lambda e: e.bn_aggr(small[:, 0:2], st6))
        sv = p.step("vector", lambda e: e.tensor_scalar_add(
            sm2[:, 0:1], small[:, 1:2], EPS), sig="ln_v")
        av = p.step("scalar", lambda e: e.activation(
            out=sm2[:, 1:2], in_=sm2[:, 0:1], func=AF.Sqrt),
            waits=[("ln_v", sv)], sig="ln_a")
        p.wait("vector", [("ln_a", av)])
        p.fstep("vector", lambda e: e.reciprocal(sm2[:, 5:6], sm2[:, 1:2]))
        p.fstep("vector", lambda e: e.tensor_scalar_mul(
            sm2[:, 4:5], small[:, 0:1], -1.0))

    for i in range(TLOC // 128):
        x_t = xb_ln[i % 2]
        lw = [] if i < 2 else [("ln_v", p.val("ln_v") - 1)]
        lv = p.dma("sync", x_t, x_in[i * 128:(i + 1) * 128, :], waits=lw,
                   sig="x_ld")
        p.wait("vector", [("x_ld", lv)])
        ln_prep(x_t)
        lnv = p.step("vector", lambda e, x=x_t: e.tensor_scalar(
            out=lnb, in0=x, scalar1=sm2[:, 4:5], scalar2=sm2[:, 5:6],
            op0=ALU.add, op1=ALU.mult),
            waits=[("l1_t", p.val("l1_t"))], sig="ln_v")
        for k in range(32):
            bnk = bankb(k % 2)
            tw = [("ln_v", lnv)]
            if k >= 2:
                tw.append(("l1_e", p.val("l1_e") - 1))
            else:
                tw.append(("ln1_d", p.val("ln1_d")))
            tv = p.step("tensor", lambda e, k=k, bnk=bnk: e.transpose(
                bnk, lnb[:, k * 128:(k + 1) * 128], ident),
                waits=tw, sig="l1_t")
            p.step("vector", lambda e, k=k, bnk=bnk: e.tensor_copy(
                tstage_l[:, k * 128:(k + 1) * 128], bnk),
                waits=[("l1_t", tv)], sig="l1_e")
        p.dma("sync", b_ln1[:, i, :, :],
              tstage_l.rearrange("p (k r) -> p k r", r=128),
              waits=[("l1_e", p.val("l1_e"))], sig="ln1_d")

    agln1 = p.step("gpsimd", lambda e: e.collective_compute(
        "AllGather", ALU.bypass, replica_groups=[list(range(NC))],
        ins=[b_ln1.rearrange("p a b c -> (p a b) c")],
        outs=[g_ln1.rearrange("n p a b c -> (n p a b) c")]),
        waits=[("ln1_d", p.val("ln1_d"))], sig="cc")
    LND = p.val("ln1_d")

    ar.reset(mSh)
    wq_sb = ar.alloc([128, 32 * DQ], BF16)
    wkv_sb = ar.alloc([128, 32 * 256], BF16)
    lnch = [ar.alloc([128, 32 * 128], BF16) for _ in range(2)]
    costl = [ar.alloc([128, 128], F32) for _ in range(2)]
    sintl = [ar.alloc([128, 128], F32) for _ in range(2)]
    ropeA = ar.alloc([128, 640], F32)
    ropeB = ar.alloc([128, 640], F32)
    qkr = ar.alloc([128, 640], BF16)

    wsb_w = [("ln1_d", LND)]
    p.dma("gpsimd", wq_sb.rearrange("p (k c) -> p k c", c=DQ),
          wqt_in.rearrange("(k p) c -> p k c", p=128), waits=wsb_w, sig="w_sb")
    p.dma("gpsimd", wkv_sb.rearrange("p (k c) -> p k c", c=256)[:, :, 0:128],
          wkt_in.rearrange("(k p) c -> p k c", p=128), sig="w_sb")
    wsb = p.dma("gpsimd", wkv_sb.rearrange("p (k c) -> p k c", c=256)
                [:, :, 128:256],
                wvt_in.rearrange("(k p) c -> p k c", p=128), sig="w_sb")
    p.wait("tensor", [("w_sb", wsb), ("cc", agln1)])
    p.wait("sync", [("cc", agln1)])
    p.wait("vector", [("w_sb", wsb)])

    for tt in range(NT // 128):
        bandi, tl = tt // 8, (tt % 8) * 128
        lc = lnch[tt % 2]
        lw = [("q_mm2", p.val("q_mm2") - 1)] if tt >= 2 else []
        p.dma("sync", lc.rearrange("p (k c) -> p k c", c=128),
              g_ln1[bandi][:, (tt % 8), :, :], waits=lw, sig="q_ld")
        ct, st_ = costl[tt % 2], sintl[tt % 2]
        pos = (tt % 32) * 128
        rw = [("q_rp", p.val("q_rp") - 2)] if tt >= 2 else []
        p.dma("sync", ct, cos_in[pos:pos + 128, :], waits=rw, sig="q_ld")
        cv = p.dma("sync", st_, sin_in[pos:pos + 128, :], sig="q_ld")
        tw = [("q_ld", cv)]
        if tt >= 1:
            tw.append(("q_rp", p.val("q_rp")))
        p.wait("tensor", tw)
        for Dc in range(32):
            p.step("tensor", lambda e, Dc=Dc, lc=lc: e.matmul(
                bank(4), lc[:, Dc * 128:(Dc + 1) * 128],
                wq_sb[:, Dc * DQ:(Dc + 1) * DQ],
                start=(Dc == 0), stop=(Dc == 31)))
            p.step("tensor", lambda e, Dc=Dc, lc=lc: e.matmul(
                PS[:, 5 * 512:5 * 512 + 256], lc[:, Dc * 128:(Dc + 1) * 128],
                wkv_sb[:, Dc * 256:(Dc + 1) * 256],
                start=(Dc == 0), stop=(Dc == 31)),
                sig="q_mm2" if Dc == 31 else None)
        mmv = p.val("q_mm2")
        qk_ps = PS[:, 4 * 512:4 * 512 + 640]
        p.step("vector", lambda e, ct=ct: e.tensor_tensor(
            out=ropeA.rearrange("p (h c) -> p h c", c=128),
            in0=qk_ps.rearrange("p (h c) -> p h c", c=128),
            in1=ct.unsqueeze(1).broadcast_to([128, 5, 128]), op=ALU.mult),
            waits=[("q_mm2", mmv)])
        qk4 = qk_ps.rearrange("p (h s c) -> p h s c", s=2, c=64)
        rB4 = ropeB.rearrange("p (h s c) -> p h s c", s=2, c=64)
        p.step("vector", lambda e, st_=st_: e.tensor_tensor(
            out=rB4[:, :, 0, :], in0=qk4[:, :, 1, :],
            in1=st_[:, 0:64].unsqueeze(1).broadcast_to([128, 5, 64]),
            op=ALU.mult))
        p.step("vector", lambda e, st_=st_: e.tensor_tensor(
            out=rB4[:, :, 1, :], in0=qk4[:, :, 0, :],
            in1=st_[:, 64:128].unsqueeze(1).broadcast_to([128, 5, 64]),
            op=ALU.mult))
        rv = p.step("vector", lambda e: e.tensor_tensor(
            out=qkr, in0=ropeA, in1=ropeB, op=ALU.add),
            waits=[("q_tr", p.val("q_tr"))], sig="q_rp")
        p.step("vector", lambda e, tt=tt: e.tensor_copy(
            v_sb[:, tt * 128:(tt + 1) * 128],
            PS[:, 5 * 512 + 128:5 * 512 + 256]), sig="q_rp")
        for hh in range(5):
            bnk = bankb(6 + hh % 2)
            tw2 = [("q_rp", rv), ("q_ev", p.val("q_ev") - 1)]
            tv = p.step("tensor", lambda e, hh=hh, bnk=bnk: e.transpose(
                bnk, qkr[:, hh * 128:(hh + 1) * 128], ident),
                waits=tw2, sig="q_tr")
            dst = (q_sb[:, hh * NT + tt * 128:hh * NT + (tt + 1) * 128]
                   if hh < 4 else k_sb[:, tt * 128:(tt + 1) * 128])
            p.step("vector", lambda e, dst=dst, bnk=bnk: e.tensor_copy(
                dst, bnk), waits=[("q_tr", tv)], sig="q_ev")

    QEV = p.val("q_ev")

    dq_jobs = []
    for i in range(HBAND // 128):
        for jh in range(2):
            dq_jobs.append(("w1", i, jh))
    for i in range(HBAND // 128):
        for jh in range(2):
            dq_jobs.append(("w2", i, jh))
    for i in range(D // 128):
        dq_jobs.append(("w3", i, 0))

    def deq_units():
        for i in range(4):
            cw = [("q_ev", QEV)] if i == 0 else [("wo_c", p.val("wo_c"))]
            lv = p.dma("gpsimd", cstb, wot_in[i * 128:(i + 1) * 128, :],
                       waits=cw, sig="wo_l")
            yield
            p.dma("gpsimd", b_wo[i * 128:(i + 1) * 128, :], cstb,
                  waits=[("wo_l", lv)], sig="wo_c")
            yield
        for (ui, (mat, i, jh)) in enumerate(dq_jobs):
            if mat == "w3":
                width, sigout = HBAND, "d_w3"
                src_ap = w3_in[i * 128:(i + 1) * 128, :]
            else:
                srcT = w1_in if mat == "w1" else w2_in
                width, sigout = CW, ("d_w1" if mat == "w1" else "d_w2")
                src_ap = srcT[i * 128:(i + 1) * 128, jh * CW:(jh + 1) * CW]
            G = width // 64
            nsub = width // 128
            wb = wbuf[ui % 2]
            lw = [("dq_u", p.val("dq_u") - 1)]
            if ui <= 1:
                lw.append(("q_ev", QEV))
            lv = p.dma("sync", wb[:, :width], src_ap, waits=lw, sig="dq_ld")
            yield
            amx = amax2[ui % 2]
            rcu = rcp2[ui % 2]
            p.fstep("vector", lambda e, wb=wb, amx=amx, G=G, width=width:
                    e.tensor_reduce(
                out=amx[:, :G],
                in_=wb[:, :width].rearrange("p (g k) -> p g k", k=64),
                axis=AX.X, op=ALU.max, apply_absolute_value=True),
                waits=[("dq_ld", lv), ("dq_q", p.val("dq_q") - 1)])
            yield
            p.fstep("vector", lambda e, amx=amx, rcu=rcu, G=G:
                    e.tensor_scalar_max(
                rcu[:, :G], amx[:, :G], 1e-38),
                waits=[("dq_u", p.val("dq_u") - 1)])
            rcv = p.fstep("vector", lambda e, rcu=rcu, G=G: e.reciprocal(
                rcu[:, :G], rcu[:, :G]), sig="dq_rc")
            yield
            ub = ubufs[ui % 2]
            uw = [("dq_rc", rcv), ("dq_p", p.val("dq_p") - 15)]
            if ui == 1:
                uw.append(("wo_c", p.val("wo_c")))
            uv = p.step("gpsimd", lambda e, wb=wb, ub=ub, rcu=rcu, G=G,
                        width=width: e.tensor_tensor(
                out=ub[:, :width].rearrange("p (g k) -> p g k", k=64),
                in0=wb[:, :width].rearrange("p (g k) -> p g k", k=64),
                in1=rcu[:, :G].unsqueeze(2).broadcast_to([128, G, 64]),
                op=ALU.mult),
                waits=uw, sig="dq_u")
            yield
            acu = accb[ui % 2]
            for j in range(15):
                pl = plane[j % 2]
                sw = [("dq_u", uv)]
                sw.append(("dq_f", p.val("dq_f") - (0 if j == 0 else 1)))
                sv = p.step("scalar", lambda e, pl=pl, ub=ub, j=j,
                            width=width:
                            e.activation(out=pl[:, :width],
                                         in_=ub[:, :width],
                                         func=AF.Sign, bias=nf4b[:, j:j + 1],
                                         scale=1.0),
                            waits=sw, sig="dq_p")
                if j == 0:
                    p.step("vector", lambda e, pl=pl, acu=acu, width=width:
                           e.tensor_scalar(
                        out=acu[:, :width], in0=pl[:, :width],
                        scalar1=float(NF4_DH[0]), scalar2=float(NF4_K0),
                        op0=ALU.mult, op1=ALU.add),
                        waits=[("dq_p", sv), ("dq_q", p.val("dq_q") - 1)],
                        sig="dq_f")
                else:
                    p.step("vector", lambda e, pl=pl, acu=acu,
                           dh=float(NF4_DH[j]), width=width:
                           e.scalar_tensor_tensor(
                        out=acu[:, :width], in0=pl[:, :width], scalar=dh,
                        in1=acu[:, :width], op0=ALU.mult, op1=ALU.add),
                        waits=[("dq_p", sv)], sig="dq_f")
                yield
            dqu = dqb[ui % 2]
            qv = p.step("gpsimd", lambda e, dqu=dqu, acu=acu, amx=amx,
                        G=G, width=width: e.tensor_tensor(
                out=dqu[:, :width].rearrange("p (g k) -> p g k", k=64),
                in0=acu[:, :width].rearrange("p (g k) -> p g k", k=64),
                in1=amx[:, :G].unsqueeze(2).broadcast_to([128, G, 64]),
                op=ALU.mult),
                waits=[("dq_f", p.val("dq_f")), ("dq_s", p.val("dq_s") - 1)],
                sig="dq_q")
            yield
            p.dma("sync", d_dq[ui, :, :width], dqu[:, :width],
                  waits=[("dq_q", qv)], sig="dq_s")
            yield

    _gen = deq_units()
    _pct = [0]

    def pump(n):
        for _ in range(n):
            if next(_gen, "END") == "END":
                break

    def pumpa():
        _pct[0] += 1
        pump(1 + (_pct[0] % 2))


    p.wait("scalar", [("q_ev", p.val("q_ev"))])
    p.wait("tensor", [("q_ev", p.val("q_ev"))])
    p.wait("vector", [("q_ev", p.val("q_ev"))])
    p.step("vector", lambda e: e.memset(nrm, 0.0), sig="nm_v")

    def vec_norm_max(src, n512, out_idx):
        for kk in range(n512):
            sv = p.step("scalar", lambda e, kk=kk, src=src: e.activation(
                out=sqb, in_=src[:, kk * 512:(kk + 1) * 512], func=AF.Square),
                waits=[("nm_t", p.val("nm_t"))], sig="nm_a")
            bi = 4 + kk % 2
            mv = p.step("tensor", lambda e, bi=bi: e.matmul(
                PS[0:1, bi * 512:bi * 512 + 512], ones_b, sqb,
                start=True, stop=True),
                waits=[("nm_a", sv), ("nm_v", p.val("nm_v"))], sig="nm_t")
            p.fstep("vector", lambda e, bi=bi: e.tensor_reduce(
                out=nrm[0:1, 8:9], in_=PS[0:1, bi * 512:bi * 512 + 512]
                .rearrange("p (a c) -> p a c", a=1), axis=AX.X, op=ALU.max),
                waits=[("nm_t", mv)])
            p.step("vector", lambda e, oi=out_idx: e.tensor_tensor(
                out=nrm[0:1, oi:oi + 1], in0=nrm[0:1, oi:oi + 1],
                in1=nrm[0:1, 8:9], op=ALU.max), sig="nm_v")

    vec_norm_max(k_sb, NT // 512, 0)
    for hh in range(4):
        vec_norm_max(q_sb[:, hh * NT:(hh + 1) * NT], NT // 512, 1)
    p.fstep("vector", lambda e: e.tensor_tensor(
        out=nrm[0:1, 2:3], in0=nrm[0:1, 0:1], in1=nrm[0:1, 1:2], op=ALU.mult))
    sv = p.fstep("vector", lambda e: e.tensor_copy(nrm[0:1, 3:4], nrm[0:1, 2:3]),
                 sig="nm_v")
    av = p.step("scalar", lambda e: e.activation(
        out=nrm[0:1, 4:5], in_=nrm[0:1, 3:4], func=AF.Sqrt),
        waits=[("nm_v", sv)], sig="nm_a")
    bv = p.step("vector", lambda e: e.tensor_scalar(
        out=nrm[0:1, 5:6], in0=nrm[0:1, 4:5], scalar1=-SCALE, scalar2=-2.0,
        op0=ALU.mult, op1=ALU.add), waits=[("nm_a", av)], sig="nm_v")
    tv = p.step("tensor", lambda e: e.matmul(
        PS[:, 4 * 512:4 * 512 + 1], ones_f, nrm[0:1, 5:6],
        start=True, stop=True), waits=[("nm_v", bv)], sig="nm_t")
    bcv = p.step("vector", lambda e: e.tensor_copy(
        bias_sb, PS[:, 4 * 512:4 * 512 + 1]), waits=[("nm_t", tv)], sig="nm_v")
    p.wait("scalar", [("nm_v", bcv)])
    p.wait("tensor", [("nm_v", bcv)])

    def emit_sc(b, h, q0, kt):
        ps_s = bank(kt % 4)
        xv_now = p.val("at_x")
        mw = [("at_x", xv_now - 3)]
        dtile = kt - 4 * ((q0 % T) // 512)
        mv = p.step("tensor", lambda e, kt=kt, b=b, q0=q0, h=h, ps_s=ps_s,
                    dt_=dtile: e.matmul(
            ps_s, k_sb[:, b * T + kt * 128:b * T + (kt + 1) * 128],
            q_sb[:, h * NT + q0:h * NT + q0 + 512],
            start=True, stop=(dt_ < 0)), waits=mw,
            sig=None if dtile >= 0 else "at_mm")
        if dtile >= 0:
            mv = p.step("tensor", lambda e, ps_s=ps_s, dt_=dtile: e.matmul(
                ps_s, ident, masks[:, dt_, :], start=False, stop=True),
                sig="at_mm")
        ew = [("at_mm", mv)]
        ab_ = attb[kt % 4]
        ew.append(("at_av", p.val("at_av")))
        p.step("scalar", lambda e, ps_s=ps_s, ab_=ab_:
               e.activation(out=ab_, in_=ps_s, func=AF.Exp,
                            bias=bias_sb[:, 0:1], scale=SCALE),
               waits=ew, sig="at_x")

    for b in range(B):
        for h in range(NQH):
            for s in range(8):
                q0 = b * T + s * 512
                nk = 4 * (s + 1)
                xbase = p.val("at_x")
                for j in range(min(3, nk)):
                    emit_sc(b, h, q0, j)
                av_b, sm_b = 4 + 2 * (s % 2), 5 + 2 * (s % 2)
                for kt in range(nk):
                    ab_ = attb[kt % 4]
                    aw = [("at_x", xbase + kt + 1)]
                    if kt == 0:
                        aw.append(("at_e", p.val("at_e") - 1))
                    p.step("tensor", lambda e, kt=kt, b=b, ab_=ab_, nk=nk,
                           av_b=av_b: e.matmul(
                        bank(av_b),
                        v_sb[:, b * T + kt * 128:b * T + (kt + 1) * 128], ab_,
                        start=(kt == 0), stop=(kt == nk - 1)),
                        waits=aw)
                    sw0 = [("at_rc", p.val("at_rc") - 1)] if kt == 0 else []
                    p.step("tensor", lambda e, kt=kt, ab_=ab_, nk=nk,
                           sm_b=sm_b: e.matmul(
                        PS[0:1, sm_b * 512:sm_b * 512 + 512], ones_b, ab_,
                        start=(kt == 0), stop=(kt == nk - 1)),
                        waits=sw0, sig="at_av")
                    if kt + 3 < nk:
                        emit_sc(b, h, q0, kt + 3)
                    pumpa()
                avc = p.val("at_av")
                rw = [("at_av", avc), ("at_r1", p.val("at_r1") - 16)]
                rcv2 = p.step("vector", lambda e, sm_b=sm_b: e.reciprocal(
                    rs_sm, PS[0:1, sm_b * 512:sm_b * 512 + 512]),
                    waits=rw, sig="at_rc")
                rrow = (b * NQH + h) * 8 + s
                r1 = p.dma("sync", d_rr[rrow:rrow + 1, :], rs_sm,
                           waits=[("at_rc", rcv2)], sig="at_r1")
                bcv = p.dma("sync", rbb[s % 2],
                            d_rr[rrow:rrow + 1, :].partition_broadcast(128),
                            waits=[("at_r1", r1),
                                   ("at_e", p.val("at_e") - 1)], sig="at_r2")
                ost = otst[s % 2]
                ew2 = [("at_r2", bcv), ("at_st", p.val("at_st") - 16)]
                ev = p.step("vector", lambda e, ost=ost, av_b=av_b, s=s:
                            e.tensor_tensor(out=ost, in0=bank(av_b),
                                            in1=rbb[s % 2], op=ALU.mult),
                            waits=ew2, sig="at_e")
                p.dma("sync", b_oT[h * 128:(h + 1) * 128, q0:q0 + 512], ost,
                      waits=[("at_e", ev)], sig="at_st")

    pump(10 ** 9)

    ATX = p.val("at_x")
    DQS = p.val("dq_s")
    tv_after = {}
    st_after = {}
    gcnt = 0
    for ui, (mat, i, jh) in enumerate(dq_jobs):
        width = HBAND if mat == "w3" else CW
        nsub = width // 128
        ldb = dqb[ui % 2]
        lw = [("dq_s", DQS)]
        if ui >= 2:
            lw.append(("t2_t", tv_after[ui - 2]))
        lv = p.dma("sync", ldb[:, :width], d_dq[ui, :, :width],
                   waits=lw, sig="t2_l")
        ts = tstage[ui % 2]
        for g0 in range(0, nsub, 8):
            gn = min(8, nsub - g0)
            bkid = gcnt % 2
            tv = None
            for k in range(g0, g0 + gn):
                tw = [("t2_l", lv)]
                if k == g0:
                    if gcnt == 0:
                        tw.append(("at_x", ATX))
                    if gcnt >= 2:
                        tw.append(("t2_e", p.val("t2_e") - 1))
                tv = p.step("tensor", lambda e, k=k, g0=g0, bkid=bkid,
                            ldb=ldb: e.transpose(
                    PSB[:, bkid * 1024 + (k - g0) * 128:
                        bkid * 1024 + (k - g0 + 1) * 128],
                    ldb[:, k * 128:(k + 1) * 128], ident),
                    waits=tw, sig="t2_t")
            ew = [("t2_t", tv)]
            if g0 == 0 and ui >= 2:
                ew.append(st_after[ui - 2])
            p.step("scalar", lambda e, g0=g0, gn=gn, bkid=bkid, ts=ts:
                   e.copy(ts[:, g0 * 128:(g0 + gn) * 128],
                          PSB[:, bkid * 1024:bkid * 1024 + gn * 128]),
                   waits=ew, sig="t2_e")
            gcnt += 1
        tv_after[ui] = p.val("t2_t")
        if mat == "w3":
            dst_ap = b_w3[i // 4, :, :, (i % 4) * 128:(i % 4 + 1) * 128]
            sigout = "d_w3"
        else:
            dstT = b_w1 if mat == "w1" else b_w2
            dst_ap = dstT[:, i, jh * 16:(jh + 1) * 16, :]
            sigout = "d_w1" if mat == "w1" else "d_w2"
        p.dma("sync", dst_ap,
              ts[:, :nsub * 128].rearrange("p (k r) -> p k r", r=128),
              waits=[("t2_e", p.val("t2_e"))], sig=sigout)
        st_after[ui] = (sigout, p.val(sigout))

    agw1 = p.step("gpsimd", lambda e: e.collective_compute(
        "AllGather", ALU.bypass, replica_groups=[list(range(NC))],
        ins=[b_w1.rearrange("p a b c -> (p a b) c")],
        outs=[g_w1.rearrange("n p a b c -> (n p a b) c")]),
        waits=[("d_w1", p.val("d_w1"))], sig="cc")
    ar.reset(m0)
    a_n = ar.alloc([128, 4 * NT], BF16)
    yst = [ar.alloc([128, 512], BF16) for _ in range(2)]
    wosp = [ar.alloc([128, 4, 512], BF16) for _ in range(2)]
    b_yp = dint("b_yp", [NT, D], BF16)
    b_ys = dint("b_ys", [TLOC, D], BF16)

    ATST = p.val("at_st")
    for hc in range(4):
        p.dma("sync", a_n[:, hc * NT:(hc + 1) * NT],
              b_oT[hc * 128:(hc + 1) * 128, :],
              waits=[("at_st", ATST)], sig="p4_n")
    p.wait("tensor", [("p4_n", p.val("p4_n")), ("t2_e", p.val("t2_e"))])
    a_nv = a_n.rearrange("p (g t) -> p g t", t=NT)
    b_wov = b_wo.rearrange("(a k) c -> k a c", k=128)
    for ds in range(8):
        wsp = wosp[ds % 2]
        lw = [("wo_m", p.val("wo_m") - 64)] if ds >= 2 else \
            [("wo_c", p.val("wo_c"))]
        lv = p.dma("sync", wsp, b_wov[:, :, ds * 512:(ds + 1) * 512],
                   waits=lw, sig="wo_ld")
        p.wait("tensor", [("wo_ld", lv)])
        for tt in range(64):
            bnk = tt % 8
            mw = [("wo_e", p.val("wo_e") - 7)]
            for hc in range(4):
                p.step("tensor", lambda e, hc=hc, tt=tt, bnk=bnk, wsp=wsp:
                       e.matmul(bank(bnk),
                                a_nv[:, hc, tt * 128:(tt + 1) * 128],
                                wsp[:, hc, :], start=(hc == 0),
                                stop=(hc == 3)),
                       waits=(mw if hc == 0 else []),
                       sig="wo_m" if hc == 3 else None)
            mvw = p.val("wo_m")
            ys = yst[tt % 2]
            ev = p.step("vector", lambda e, ys=ys, bnk=bnk: e.tensor_copy(
                ys, bank(bnk)),
                waits=[("wo_m", mvw), ("wo_st", p.val("wo_st") - 16)],
                sig="wo_e")
            p.dma("sync",
                  b_yp[tt * 128:(tt + 1) * 128, ds * 512:(ds + 1) * 512],
                  ys, waits=[("wo_e", ev)], sig="wo_st")

    RSY = p.step("gpsimd", lambda e: e.collective_compute(
        "ReduceScatter", ALU.add, replica_groups=[list(range(NC))],
        ins=[b_yp], outs=[b_ys]),
        waits=[("wo_st", p.val("wo_st"))], sig="cc")
    agw2 = p.step("gpsimd", lambda e: e.collective_compute(
        "AllGather", ALU.bypass, replica_groups=[list(range(NC))],
        ins=[b_w2.rearrange("p a b c -> (p a b) c")],
        outs=[g_w2.rearrange("n p a b c -> (n p a b) c")]),
        waits=[("d_w2", p.val("d_w2"))], sig="cc")
    agw3 = p.step("gpsimd", lambda e: e.collective_compute(
        "AllGather", ALU.bypass, replica_groups=[list(range(NC))],
        ins=[b_w3.rearrange("a p b c -> (a p b) c")],
        outs=[g_w3.rearrange("n a p b c -> (n a p b) c")]),
        waits=[("d_w3", p.val("d_w3"))], sig="cc")

    ar.reset(m0)
    xb4 = [ar.alloc([128, D], F32) for _ in range(2)]
    yb4 = [ar.alloc([128, D], BF16) for _ in range(2)]
    scr4 = ar.alloc([128, D], F32)
    x1b = ar.alloc([128, D], F32)
    lnb2 = ar.alloc([128, D], BF16)
    tst2 = ar.alloc([128, 32 * 128], BF16)

    p.wait("sync", [("cc", RSY)])
    for i in range(8):
        y_t = yb4[i % 2]
        lw = [("p4_q", p.val("p4_q") - 2)] if i >= 2 else []
        lv = p.dma("sync", y_t, b_ys[i * 128:(i + 1) * 128, :],
                   waits=lw, sig="p4_l")
        p.step("vector", lambda e, y_t=y_t, i=i: e.tensor_reduce(
            out=mean8[:, i:i + 1],
            in_=y_t.rearrange("p (a c) -> p a c", a=1),
            axis=AX.X, op=ALU.add), waits=[("p4_l", lv)], sig="p4_q")
        p.step("scalar", lambda e, y_t=y_t, i=i: e.activation(
            out=scr4, in_=y_t, func=AF.Square,
            accum_out=msq8[:, i:i + 1]), waits=[("p4_l", lv)], sig="p4_q")

    p.wait("vector", [("p4_q", p.val("p4_q"))])
    p.fstep("vector", lambda e: e.tensor_scalar_mul(mean8, mean8, 1.0 / D))
    p.fstep("vector", lambda e: e.tensor_scalar_mul(msq8, msq8, 1.0 / D))
    p.fstep("vector", lambda e: e.tensor_tensor(
        out=sm2[:, 0:8], in0=mean8, in1=mean8, op=ALU.mult))
    p.fstep("vector", lambda e: e.tensor_tensor(
        out=msq8, in0=msq8, in1=sm2[:, 0:8], op=ALU.subtract))
    vv = p.step("vector", lambda e: e.tensor_scalar_add(msq8, msq8, EPS),
                sig="p4_v")
    sv = p.step("scalar", lambda e: e.activation(
        out=str_, in_=msq8, func=AF.Sqrt), waits=[("p4_v", vv)], sig="p4_a")
    p.fstep("vector", lambda e: e.reciprocal(str_, str_), waits=[("p4_a", sv)])
    p.fstep("vector", lambda e: e.tensor_scalar_mul(stm, mean8, -1.0))

    qv4 = p.val("p4_q")
    for i in range(8):
        x_t, y_t = xb4[i % 2], yb4[i % 2]
        lw = [("p4_x", p.val("p4_x") - 1)] if i >= 2 else []
        p.dma("sync", x_t, x_in[i * 128:(i + 1) * 128, :],
              waits=[("p4_q", qv4)] + lw, sig="p4_l2")
        lv2 = p.dma("sync", y_t, b_ys[i * 128:(i + 1) * 128, :], sig="p4_l2")
        p.wait("vector", [("p4_l2", lv2)])
        p.step("vector", lambda e, y=y_t, i=i: e.tensor_scalar(
            out=scr4, in0=y, scalar1=stm[:, i:i + 1], scalar2=str_[:, i:i + 1],
            op0=ALU.add, op1=ALU.mult))
        x1v = p.step("vector", lambda e, x=x_t: e.tensor_tensor(
            out=x1b, in0=scr4, in1=x, op=ALU.add),
            waits=[("p4_s", p.val("p4_s") - 16)], sig="p4_x")
        p.dma("sync", s_x1[i * 128:(i + 1) * 128, :], x1b,
              waits=[("p4_x", x1v)], sig="p4_s")
        ln_prep(x1b)
        lnv = p.step("vector", lambda e: e.tensor_scalar(
            out=lnb2, in0=x1b, scalar1=sm2[:, 4:5], scalar2=sm2[:, 5:6],
            op0=ALU.add, op1=ALU.mult),
            waits=[("p4_t", p.val("p4_t"))], sig="ln_v")
        for k in range(32):
            bnk = bankb(k % 2)
            tw = [("ln_v", lnv)]
            if k >= 2:
                tw.append(("p4_e", p.val("p4_e") - 1))
            else:
                tw.append(("p4_s", p.val("p4_s")))
            tv = p.step("tensor", lambda e, k=k, bnk=bnk: e.transpose(
                bnk, lnb2[:, k * 128:(k + 1) * 128], ident),
                waits=tw, sig="p4_t")
            p.step("vector", lambda e, k=k, bnk=bnk: e.tensor_copy(
                tst2[:, k * 128:(k + 1) * 128], bnk),
                waits=[("p4_t", tv)], sig="p4_e")
        p.dma("sync", s_ln2.rearrange("(k q) r -> q k r", q=128)
              [:, :, i * 128:(i + 1) * 128],
              tst2.rearrange("p (k r) -> p k r", r=128),
              waits=[("p4_e", p.val("p4_e"))], sig="p4_s")

    ar.reset(m0)
    h_sb = ar.alloc([128, 88 * TCH], BF16)
    ln2c = ar.alloc([128, 32 * TCH], BF16)
    w1c = [ar.alloc([128, 32 * 128], BF16) for _ in range(2)]
    w2c = [ar.alloc([128, 32 * 128], BF16) for _ in range(2)]
    w3bb = [ar.alloc([128, (HBAND // 128) * 512], BF16) for _ in range(2)]
    sgb = ar.alloc([128, TCH], F32)
    sgb2 = ar.alloc([128, TCH], F32)
    y2st = [ar.alloc([128, 512], F32) for _ in range(2)]
    y2sq = sgb

    p.step("vector", lambda e: e.memset(mean8, 0.0))
    p.step("vector", lambda e: e.memset(msq8, 0.0), sig="f_z")
    p.wait("scalar", [("f_z", p.val("f_z"))])
    p.wait("sync", [("p4_s", p.val("p4_s")), ("cc", agw3)])
    for ci in range(2):
        lnw = [("f_g", p.val("f_g"))] if ci else []
        lv = p.dma("sync", ln2c.rearrange("p (k c) -> p k c", c=TCH),
                   s_ln2.rearrange("(k p) c -> p k c", p=128)
                   [:, :, ci * TCH:(ci + 1) * TCH], waits=lnw, sig="f_l")
        p.wait("tensor", [("f_l", lv), ("cc", agw2)])
        hs_guard = p.val("f_dm")
        for ht in range(88):
            bandi, hl = ht // 11, (ht % 11) * 128
            wa, wb_ = w1c[ht % 2], w2c[ht % 2]
            lw = [("f_su", p.val("f_su") - 1)] if (ci or ht >= 2) else []
            p.dma("sync", wa.rearrange("p (k c) -> p k c", c=128),
                  g_w1[bandi][:, ht % 11, :, :], waits=lw, sig="f_wl")
            wlv = p.dma("sync", wb_.rearrange("p (k c) -> p k c", c=128),
                        g_w2[bandi][:, ht % 11, :, :], sig="f_wl")
            bg, bu = (0, 1) if ht % 2 == 0 else (6, 7)
            sgx = sgb if ht % 2 == 0 else sgb2
            gw = [("f_wl", wlv), ("f_su", p.val("f_su") - 1)]
            p.wait("tensor", gw)
            for Dc in range(32):
                p.step("tensor", lambda e, Dc=Dc, wa=wa, bg=bg: e.matmul(
                    bank(bg), wa[:, Dc * 128:(Dc + 1) * 128],
                    ln2c[:, Dc * TCH:(Dc + 1) * TCH],
                    start=(Dc == 0), stop=(Dc == 31)))
                p.step("tensor", lambda e, Dc=Dc, wb_=wb_, bu=bu: e.matmul(
                    bank(bu), wb_[:, Dc * 128:(Dc + 1) * 128],
                    ln2c[:, Dc * TCH:(Dc + 1) * TCH],
                    start=(Dc == 0), stop=(Dc == 31)),
                    sig="f_g" if Dc == 31 else None)
            gv = p.val("f_g")
            sv = p.step("scalar", lambda e, bg=bg, sgx=sgx: e.activation(
                out=sgx, in_=bank(bg), func=AF.Silu), waits=[("f_g", gv)],
                sig="f_sl")
            hw = [("f_sl", sv)]
            if ci and ht == 0:
                hw.append(("f_dm", hs_guard))
            p.step("vector", lambda e, ht=ht, bu=bu, sgx=sgx: e.tensor_tensor(
                out=h_sb[:, ht * TCH:(ht + 1) * TCH], in0=sgx, in1=bank(bu),
                op=ALU.mult), waits=hw, sig="f_su")
        huv = p.val("f_su")
        p.wait("tensor", [("f_su", huv)])
        gidx = 0
        for ds in range(8):
            for bandi in range(8):
                wbb = w3bb[gidx % 2]
                lw = [("f_dm", p.val("f_dm") - 11)] if (ci or gidx >= 2) else []
                wlv = p.dma("sync", wbb.rearrange("p (a c) -> p a c", c=512),
                            g_w3[bandi, ds], waits=lw, sig="f_wl")
                p.wait("tensor", [("f_wl", wlv)])
                if bandi == 0:
                    p.wait("tensor", [("f_de", p.val("f_de"))])
                for htl in range(11):
                    ht = bandi * 11 + htl
                    for ti in range(4):
                        p.step("tensor", lambda e, ht=ht, ti=ti, wbb=wbb,
                               htl=htl: e.matmul(
                            bank(2 + ti),
                            h_sb[:, ht * TCH + ti * 128:
                                 ht * TCH + (ti + 1) * 128],
                            wbb[:, htl * 512:(htl + 1) * 512],
                            start=(ht == 0), stop=(ht == 87)),
                            sig="f_dm" if ti == 3 else None)
                gidx += 1
            dmv = p.val("f_dm")
            for ti in range(4):
                tt2 = ci * 4 + ti
                ys = y2st[ti % 2]
                ew = [("f_dm", dmv)]
                ew.append(("f_st", p.val("f_st") - 16))
                ev = p.step("vector", lambda e, ti=ti, ys=ys: e.tensor_copy(
                    ys, bank(2 + ti)), waits=ew, sig="f_de")
                p.fstep("vector", lambda e, ys=ys: e.tensor_reduce(
                    out=sm2[:, 6:7], in_=ys.rearrange("p (a c) -> p a c", a=1),
                    axis=AX.X, op=ALU.add))
                p.fstep("vector", lambda e, tt2=tt2: e.tensor_tensor(
                    out=mean8[:, tt2:tt2 + 1], in0=mean8[:, tt2:tt2 + 1],
                    in1=sm2[:, 6:7], op=ALU.add))
                sv = p.step("scalar", lambda e, ys=ys: e.activation(
                    out=y2sq, in_=ys, func=AF.Square, accum_out=sm2[:, 7:8]),
                    waits=[("f_de", ev), ("f_v2", p.val("f_v2"))], sig="f_a")
                p.step("vector", lambda e, tt2=tt2: e.tensor_tensor(
                    out=msq8[:, tt2:tt2 + 1], in0=msq8[:, tt2:tt2 + 1],
                    in1=sm2[:, 7:8], op=ALU.add), waits=[("f_a", sv)],
                    sig="f_v2")
                p.wait("vector", [("f_v2", p.val("f_v2"))])
                p.dma("sync", s_y2[tt2 * 128:(tt2 + 1) * 128,
                                   ds * 512:(ds + 1) * 512], ys,
                      waits=[("f_de", ev)], sig="f_st")

    p.wait("vector", [("f_v2", p.val("f_v2"))])
    p.fstep("vector", lambda e: e.tensor_scalar_mul(mean8, mean8, 1.0 / D))
    p.fstep("vector", lambda e: e.tensor_scalar_mul(msq8, msq8, 1.0 / D))
    p.fstep("vector", lambda e: e.tensor_tensor(
        out=sm2[:, 0:8], in0=mean8, in1=mean8, op=ALU.mult))
    p.fstep("vector", lambda e: e.tensor_tensor(
        out=msq8, in0=msq8, in1=sm2[:, 0:8], op=ALU.subtract))
    vv = p.step("vector", lambda e: e.tensor_scalar_add(msq8, msq8, EPS),
                sig="f_v")
    sv = p.step("scalar", lambda e: e.activation(
        out=str_, in_=msq8, func=AF.Sqrt), waits=[("f_v", vv)], sig="f_a2")
    p.fstep("vector", lambda e: e.reciprocal(str_, str_), waits=[("f_a2", sv)])
    p.fstep("vector", lambda e: e.tensor_scalar_mul(stm, mean8, -1.0))

    ar.reset(m0)
    xb5 = [ar.alloc([128, D], F32) for _ in range(2)]
    yb5 = [ar.alloc([128, D], F32) for _ in range(2)]
    scr5 = ar.alloc([128, D], F32)
    fout = ar.alloc([128, D], F32)

    fv = p.val("f_st")
    for i in range(8):
        x_t, y_t = xb5[i % 2], yb5[i % 2]
        lw = [("fin_o", p.val("fin_o") - 1)] if i >= 2 else []
        p.dma("sync", x_t, s_x1[i * 128:(i + 1) * 128, :],
              waits=[("f_st", fv)] + lw, sig="fin_l")
        lv2 = p.dma("sync", y_t, s_y2[i * 128:(i + 1) * 128, :], sig="fin_l")
        p.wait("vector", [("fin_l", lv2)])
        p.step("vector", lambda e, y=y_t, i=i: e.tensor_scalar(
            out=scr5, in0=y, scalar1=stm[:, i:i + 1], scalar2=str_[:, i:i + 1],
            op0=ALU.add, op1=ALU.mult))
        ov = p.step("vector", lambda e, x=x_t: e.tensor_tensor(
            out=fout, in0=scr5, in1=x, op=ALU.add),
            waits=[("fin_s", p.val("fin_s"))], sig="fin_o")
        p.dma("sync", out_d[i * 128:(i + 1) * 128, :], fout,
              waits=[("fin_o", ov)], sig="fin_s")

    if DEBUG:
        dbg = {
            "dbg_oT": b_oT, "dbg_y": b_ys, "dbg_x1": s_x1,
            "dbg_ln2": s_ln2, "dbg_y2": s_y2,
        }
        for nm, src_t in dbg.items():
            od2 = nc.dram_tensor(nm, list(src_t.tensor.shape),
                                 src_t.dtype, kind="ExternalOutput").ap()
            p.dma("sync", od2, src_t, waits=[("fin_s", p.val("fin_s"))],
                  sig="dbg_s")
        dsb = nc.dram_tensor("dbg_sb", [128, 64], F32,
                             kind="ExternalOutput").ap()
        dsb_st = ar.alloc([128, 64], F32)
        p.step("vector", lambda e: e.tensor_copy(dsb_st[:, 0:1], bias_sb),
               waits=[("fin_o", p.val("fin_o"))])
        p.step("vector", lambda e: e.tensor_copy(dsb_st[:, 1:17], sm2))
        p.step("vector", lambda e: e.tensor_copy(dsb_st[:, 17:25], mean8))
        p.step("vector", lambda e: e.tensor_copy(dsb_st[:, 25:33], msq8))
        p.step("vector", lambda e: e.tensor_copy(dsb_st[:, 33:41], stm))
        dv_ = p.step("vector", lambda e: e.tensor_copy(dsb_st[:, 41:49], str_),
                     sig="dbg_v")
        p.step("vector", lambda e: e.tensor_copy(
            dsb_st[0:1, 48:64], nrm), sig="dbg_v")
        p.dma("sync", dsb, dsb_st, waits=[("dbg_v", p.val("dbg_v"))],
              sig="dbg_s")
        p.wait("sync", [("dbg_s", p.val("dbg_s"))])
    p.wait("sync", [("fin_s", p.val("fin_s"))])
    return nc, p


_CACHE = {}


def _get_built():
    if "nc" not in _CACHE:
        nc, p = build_program()
        with nc.Block() as blk:
            for name in Prog.ENGS:
                fns = p.q[name]

                def body(e, fns=fns):
                    for f in fns:
                        f(e)
                getattr(blk, name)(body)
        _CACHE["nc"] = nc
    return _CACHE["nc"]


def kernel(L0_attn_x, L0_attn_Wq, L0_attn_Wk, L0_attn_Wv, L0_attn_Wo,
           L0_ffn_w1, L0_ffn_w2, L0_ffn_w3):
    x = np.ascontiguousarray(np.asarray(L0_attn_x, np.float32).reshape(NT, D))
    WqT = np.ascontiguousarray(np.asarray(L0_attn_Wq, np.float32).T)
    WkT = np.ascontiguousarray(np.asarray(L0_attn_Wk, np.float32).T)
    WvT = np.ascontiguousarray(np.asarray(L0_attn_Wv, np.float32).T)
    WoT = np.ascontiguousarray(np.asarray(L0_attn_Wo, np.float32).T)
    w1p = np.zeros((HPAD, D), np.float32)
    w1p[:H] = np.asarray(L0_ffn_w1, np.float32)
    w2p = np.zeros((HPAD, D), np.float32)
    w2p[:H] = np.asarray(L0_ffn_w2, np.float32)
    w3p = np.zeros((D, HPAD), np.float32)
    w3p[:, :H] = np.asarray(L0_ffn_w3, np.float32)

    cos_t, sinS_t = _rope_tables()
    kk = np.arange(128)[:, None]
    qq = np.arange(512)[None, :]
    mask_t = np.stack([np.where(i * 128 + kk <= qq, 0.0, MVAL)
                       for i in range(4)]).astype(ml_dtypes.bfloat16)
    ident_t = np.eye(128, dtype=ml_dtypes.bfloat16)
    nf4b_t = np.tile(-NF4_B, (128, 1)).astype(np.float32)

    nc = _get_built()
    in_maps = []
    for c in range(NC):
        in_maps.append(dict(
            x_c=np.ascontiguousarray(x[c * TLOC:(c + 1) * TLOC]),
            wqt_c=np.ascontiguousarray(WqT[:, c * DQ:(c + 1) * DQ]),
            wkt_c=np.ascontiguousarray(WkT[:, c * HD:(c + 1) * HD]),
            wvt_c=np.ascontiguousarray(WvT[:, c * HD:(c + 1) * HD]),
            wot_c=np.ascontiguousarray(WoT[c * DQ:(c + 1) * DQ, :]),
            w1_c=np.ascontiguousarray(w1p[c * HBAND:(c + 1) * HBAND]),
            w2_c=np.ascontiguousarray(w2p[c * HBAND:(c + 1) * HBAND]),
            w3_c=np.ascontiguousarray(w3p[:, c * HBAND:(c + 1) * HBAND]),
            cos_t=cos_t, sinS_t=sinS_t, mask_t=mask_t, ident_t=ident_t,
            nf4b_t=nf4b_t,
        ))
    res = run_bass_kernel_spmd(nc, in_maps, core_ids=list(range(NC)))
    out = np.concatenate([res.results[c]["out_c"] for c in range(NC)], axis=0)
    if DEBUG:
        return out.reshape(B, T, D), res
    return out.reshape(B, T, D)


# version 20
# speedup vs baseline: 1.1350x; 1.0190x over previous
import math
import numpy as np
import ml_dtypes
import concourse.bass as bass
import concourse.mybir as mybir
from concourse.bass_utils import run_bass_kernel_spmd

F32 = mybir.dt.float32
BF16 = mybir.dt.bfloat16
AF = mybir.ActivationFunctionType
ALU = mybir.AluOpType
AX = mybir.AxisListType

NC = 8
D = 4096
T = 4096
B = 2
NT = B * T
TLOC = NT // NC
HD = 128
NQH = 4
DQ = NQH * HD
H = 11008
HPAD = 11264
HBAND = HPAD // NC
SCALE = 1.0 / math.sqrt(HD)
EPS = 1e-5
MVAL = -1e30
TCH = 512

NF4 = np.array([-1.0, -0.6961928009986877, -0.5250730514526367,
                -0.39491748809814453, -0.28444138169288635,
                -0.18477343022823334, -0.09105003625154495, 0.0,
                0.07958029955625534, 0.16093020141124725,
                0.24611230194568634, 0.33791524171829224,
                0.44070982933044434, 0.5626170039176941,
                0.7229568362236023, 1.0], dtype=np.float32)
NF4_B = ((NF4[:-1] + NF4[1:]) * np.float32(0.5)).astype(np.float32)
NF4_DH = ((NF4[1:] - NF4[:-1]) * np.float32(0.5)).astype(np.float32)
NF4_K0 = float(np.float32(NF4[0] + np.float64(NF4_DH.astype(np.float64).sum())))


class Prog:
    ENGS = ("sync", "gpsimd", "tensor", "vector", "scalar")

    def __init__(self, nc):
        self.nc = nc
        self.q = {e: [] for e in self.ENGS}
        self.sems = {}

    def _sem(self, name):
        if name not in self.sems:
            self.sems[name] = [self.nc.alloc_semaphore(name), 0]
        return self.sems[name]

    def val(self, name):
        return self._sem(name)[1]

    def step(self, eng, build, waits=(), sig=None, inc=1):
        tgt, s = None, None
        if sig is not None:
            s = self._sem(sig)
            s[1] += inc
            tgt = s[1]
        wl = tuple((self._sem(n)[0], v) for (n, v) in waits if v > 0)

        def f(e, wl=wl, build=build, s=s, inc=inc):
            for (hh, v) in wl:
                e.wait_ge(hh, v)
            ins = build(e)
            if s is not None:
                ins.then_inc(s[0], inc)
        self.q[eng].append(f)
        return tgt

    def fstep(self, eng, build, waits=(), sig=None, inc=1):
        s = sig if sig is not None else ("F_" + eng)
        v = self.step(eng, build, waits, s, inc)
        self.wait(eng, [(s, v)])
        return v

    def wait(self, eng, waits):
        wl = tuple((self._sem(n)[0], v) for (n, v) in waits if v > 0)

        def f(e, wl=wl):
            for (hh, v) in wl:
                e.wait_ge(hh, v)
        self.q[eng].append(f)

    def dma(self, eng, out, in_, waits=(), sig="dma"):
        return self.step(eng, lambda e, o=out, i=in_: e.dma_start(out=o, in_=i),
                         waits=waits, sig=sig, inc=16)


class Arena:
    def __init__(self, nc, base, top, tag):
        self.nc, self.base, self.top, self.off = nc, base, top, base
        self.tag, self.n = tag, 0

    def alloc(self, shape, dtype):
        nb = int(np.prod(shape[1:])) * mybir.dt.size(dtype)
        off = (self.off + 31) & ~31
        assert off + nb <= self.top, f"sbuf overflow {off}+{nb}>{self.top}"
        self.off = off + nb
        self.n += 1
        return self.nc.alloc_sbuf_tensor_at(
            f"{self.tag}{self.n}", list(shape), dtype, offset=off).ap()

    def mark(self):
        return self.off

    def reset(self, m):
        self.off = m


def _rope_tables():
    base = 500000.0
    inv = 1.0 / base ** (np.arange(0, HD, 2, dtype=np.float32) / HD)
    factor, low_f, high_f, old_ctx = 32.0, 1.0, 4.0, 8192
    low_wl, high_wl = old_ctx / low_f, old_ctx / high_f
    wl = 2 * math.pi / inv
    inv_l = np.where(wl > low_wl, inv / factor, inv)
    smooth = (old_ctx / wl - low_f) / (high_f - low_f)
    smoothed = (1 - smooth) * inv_l / factor + smooth * inv_l
    med = ~(wl < high_wl) & ~(wl > low_wl)
    freq = np.where(med, smoothed, inv_l).astype(np.float32)
    ang = np.arange(T, dtype=np.float32)[:, None] * freq[None, :]
    emb = np.concatenate([ang, ang], axis=1)
    cos = np.cos(emb).astype(np.float32)
    sinS = np.sin(emb).astype(np.float32)
    sinS[:, :HD // 2] *= -1.0
    return cos, sinS


DEBUG = False


def build_program():
    nc = bass.Bass("TRN2", target_bir_lowering=False)
    p = Prog(nc)

    def din(name, shape, dt=F32):
        return nc.dram_tensor(name, list(shape), dt, kind="ExternalInput").ap()

    x_in = din("x_c", [TLOC, D])
    wqt_in = din("wqt_c", [D, DQ])
    wkt_in = din("wkt_c", [D, HD])
    wvt_in = din("wvt_c", [D, HD])
    wot_in = din("wot_c", [DQ, D])
    w1_in = din("w1_c", [HBAND, D])
    w2_in = din("w2_c", [HBAND, D])
    w3_in = din("w3_c", [D, HBAND])
    cos_in = din("cos_t", [T, HD])
    sin_in = din("sinS_t", [T, HD])
    mask_in = din("mask_t", [4, 128, 512], BF16)
    id_in = din("ident_t", [128, 128], BF16)
    nf4b_in = din("nf4b_t", [128, 15])

    out_d = nc.dram_tensor("out_c", [TLOC, D], F32, kind="ExternalOutput").ap()

    def dint(name, shape, dt):
        return nc.dram_tensor(name, list(shape), dt, kind="Internal").ap()

    b_ln1 = dint("b_ln1", [128, TLOC // 128, 32, 128], BF16)
    g_ln1 = dint("g_ln1", [NC, 128, TLOC // 128, 32, 128], BF16)
    b_w1 = dint("b_w1", [128, HBAND // 128, 32, 128], BF16)
    b_w2 = dint("b_w2", [128, HBAND // 128, 32, 128], BF16)
    b_w3 = dint("b_w3", [8, 128, HBAND // 128, 512], BF16)
    g_w1 = dint("g_w1", [NC, 128, HBAND // 128, 32, 128], BF16)
    g_w2 = dint("g_w2", [NC, 128, HBAND // 128, 32, 128], BF16)
    g_w3 = dint("g_w3", [NC, 8, 128, HBAND // 128, 512], BF16)
    b_wo = dint("b_wo", [DQ, D], BF16)
    b_oT = dint("b_oT", [DQ, NT], BF16)
    d_dq = dint("d_dq", [76, 128, 2048], BF16)
    d_rr = dint("d_rr", [64, 512], F32)
    s_x1 = dint("s_x1", [TLOC, D], F32)
    s_ln2 = dint("s_ln2", [D, TLOC], BF16)
    s_y2 = dint("s_y2", [TLOC, D], F32)

    PS = nc.alloc_psum_tensor("PS", [128, 4096], F32).ap()
    PSB = PS.bitcast(BF16)

    def bank(i):
        return PS[:, i * 512:(i + 1) * 512]

    def bankb(i):
        return PSB[:, i * 1024:i * 1024 + 128]

    SB_TOP = 222 * 1024
    pers = Arena(nc, 24 * 1024, 34 * 1024, "pe")
    ar = Arena(nc, 34 * 1024, SB_TOP, "ar")

    ident = pers.alloc([128, 128], BF16)
    ones_b = pers.alloc([128, 1], BF16)
    ones_f = pers.alloc([1, 128], F32)
    masks = pers.alloc([128, 4, 512], BF16)
    small = pers.alloc([128, 16], F32)
    sm2 = pers.alloc([128, 16], F32)
    nrm = pers.alloc([1, 16], F32)
    bias_sb = pers.alloc([128, 1], F32)
    mean8 = pers.alloc([128, 8], F32)
    msq8 = pers.alloc([128, 8], F32)
    stm = pers.alloc([128, 8], F32)
    str_ = pers.alloc([128, 8], F32)
    st6 = pers.alloc([128, 48], F32)
    nf4b = pers.alloc([128, 15], F32)
    dh_sb = pers.alloc([128, 15], BF16)

    _pid = {}

    def pid_of(e):
        if id(e) not in _pid:
            _pid[id(e)] = bass.ts(e.partition_id(), TLOC)
        return _pid[id(e)]

    p.q["sync"].append(lambda e: pid_of(e))

    p.dma("sync", ident, id_in, sig="cst")
    p.dma("sync", masks, mask_in.transpose([1, 0, 2]), sig="cst")
    p.dma("sync", nf4b, nf4b_in, sig="cst")
    p.step("gpsimd", lambda e: e.memset(ones_b, 1.0), sig="cst")
    p.step("gpsimd", lambda e: e.memset(ones_f, 1.0), sig="cst")
    for j in range(15):
        p.step("gpsimd", lambda e, j=j: e.memset(
            dh_sb[:, j:j + 1], float(NF4_DH[j])), sig="cst")
    CST = p.val("cst")

    m0 = ar.mark()
    q_sb = ar.alloc([128, 4 * NT], BF16)
    k_sb = ar.alloc([128, NT], BF16)
    v_sb = ar.alloc([128, NT], BF16)
    attb = [ar.alloc([128, 512], BF16) for _ in range(4)]
    otst = [ar.alloc([128, 512], BF16) for _ in range(2)]
    rs_sm = ar.alloc([1, 512], F32)
    rbb = [ar.alloc([128, 512], F32) for _ in range(2)]
    sqb = ar.alloc([128, 512], BF16)
    mSh = ar.mark()
    arD = Arena(nc, mSh, SB_TOP, "dq")
    CW = 2048
    wbuf = [arD.alloc([128, CW], F32) for _ in range(2)]
    ubuf = arD.alloc([128, CW], F32)
    plane = [arD.alloc([128, CW], BF16) for _ in range(2)]
    accb = [arD.alloc([128, CW], F32) for _ in range(2)]
    dqb = [arD.alloc([128, CW], BF16) for _ in range(2)]
    tstage = [arD.alloc([128, 16 * 128], BF16) for _ in range(2)]
    amax2 = [arD.alloc([128, 32], F32) for _ in range(2)]
    rcp2 = [arD.alloc([128, 32], F32) for _ in range(2)]
    cstb = arD.alloc([128, D], BF16)
    ubufs = [ubuf, cstb.bitcast(F32)]

    xb_ln = [ar.alloc([128, D], F32) for _ in range(2)]
    lnb = ar.alloc([128, D], BF16)
    tstage_l = ar.alloc([128, 32 * 128], BF16)

    def ln_prep(xsrc):
        for ci6 in range(8):
            p.fstep("vector", lambda e, x=xsrc, ci6=ci6: e.bn_stats(
                st6[:, ci6 * 6:(ci6 + 1) * 6],
                x[:, ci6 * 512:(ci6 + 1) * 512]))
        p.fstep("vector", lambda e: e.bn_aggr(small[:, 0:2], st6))
        sv = p.step("vector", lambda e: e.tensor_scalar_add(
            sm2[:, 0:1], small[:, 1:2], EPS), sig="ln_v")
        av = p.step("scalar", lambda e: e.activation(
            out=sm2[:, 1:2], in_=sm2[:, 0:1], func=AF.Sqrt),
            waits=[("ln_v", sv)], sig="ln_a")
        p.wait("vector", [("ln_a", av)])
        p.fstep("vector", lambda e: e.reciprocal(sm2[:, 5:6], sm2[:, 1:2]))
        p.fstep("vector", lambda e: e.tensor_scalar_mul(
            sm2[:, 4:5], small[:, 0:1], -1.0))

    for i in range(TLOC // 128):
        x_t = xb_ln[i % 2]
        lw = [] if i < 2 else [("ln_v", p.val("ln_v") - 1)]
        lv = p.dma("sync", x_t, x_in[i * 128:(i + 1) * 128, :], waits=lw,
                   sig="x_ld")
        p.wait("vector", [("x_ld", lv)])
        ln_prep(x_t)
        lnv = p.step("vector", lambda e, x=x_t: e.tensor_scalar(
            out=lnb, in0=x, scalar1=sm2[:, 4:5], scalar2=sm2[:, 5:6],
            op0=ALU.add, op1=ALU.mult),
            waits=[("l1_t", p.val("l1_t"))], sig="ln_v")
        for k in range(32):
            bnk = bankb(k % 2)
            tw = [("ln_v", lnv)]
            if k >= 2:
                tw.append(("l1_e", p.val("l1_e") - 1))
            else:
                tw.append(("ln1_d", p.val("ln1_d")))
            tv = p.step("tensor", lambda e, k=k, bnk=bnk: e.transpose(
                bnk, lnb[:, k * 128:(k + 1) * 128], ident),
                waits=tw, sig="l1_t")
            p.step("vector", lambda e, k=k, bnk=bnk: e.tensor_copy(
                tstage_l[:, k * 128:(k + 1) * 128], bnk),
                waits=[("l1_t", tv)], sig="l1_e")
        p.dma("sync", b_ln1[:, i, :, :],
              tstage_l.rearrange("p (k r) -> p k r", r=128),
              waits=[("l1_e", p.val("l1_e"))], sig="ln1_d")

    agln1 = p.step("gpsimd", lambda e: e.collective_compute(
        "AllGather", ALU.bypass, replica_groups=[list(range(NC))],
        ins=[b_ln1.rearrange("p a b c -> (p a b) c")],
        outs=[g_ln1.rearrange("n p a b c -> (n p a b) c")]),
        waits=[("ln1_d", p.val("ln1_d"))], sig="cc")
    LND = p.val("ln1_d")

    ar.reset(mSh)
    wq_sb = ar.alloc([128, 32 * DQ], BF16)
    wkv_sb = ar.alloc([128, 32 * 256], BF16)
    lnch = [ar.alloc([128, 32 * 128], BF16) for _ in range(2)]
    costl = [ar.alloc([128, 128], F32) for _ in range(2)]
    sintl = [ar.alloc([128, 128], F32) for _ in range(2)]
    ropeA = ar.alloc([128, 640], F32)
    ropeB = ar.alloc([128, 640], F32)
    qkr = ar.alloc([128, 640], BF16)

    wsb_w = [("ln1_d", LND)]
    p.dma("gpsimd", wq_sb.rearrange("p (k c) -> p k c", c=DQ),
          wqt_in.rearrange("(k p) c -> p k c", p=128), waits=wsb_w, sig="w_sb")
    p.dma("gpsimd", wkv_sb.rearrange("p (k c) -> p k c", c=256)[:, :, 0:128],
          wkt_in.rearrange("(k p) c -> p k c", p=128), sig="w_sb")
    wsb = p.dma("gpsimd", wkv_sb.rearrange("p (k c) -> p k c", c=256)
                [:, :, 128:256],
                wvt_in.rearrange("(k p) c -> p k c", p=128), sig="w_sb")
    p.wait("tensor", [("w_sb", wsb), ("cc", agln1)])
    p.wait("sync", [("cc", agln1)])
    p.wait("vector", [("w_sb", wsb)])

    for tt in range(NT // 128):
        bandi, tl = tt // 8, (tt % 8) * 128
        lc = lnch[tt % 2]
        lw = [("q_mm2", p.val("q_mm2") - 1)] if tt >= 2 else []
        p.dma("sync", lc.rearrange("p (k c) -> p k c", c=128),
              g_ln1[bandi][:, (tt % 8), :, :], waits=lw, sig="q_ld")
        ct, st_ = costl[tt % 2], sintl[tt % 2]
        pos = (tt % 32) * 128
        rw = [("q_rp", p.val("q_rp") - 2)] if tt >= 2 else []
        p.dma("sync", ct, cos_in[pos:pos + 128, :], waits=rw, sig="q_ld")
        cv = p.dma("sync", st_, sin_in[pos:pos + 128, :], sig="q_ld")
        tw = [("q_ld", cv)]
        if tt >= 1:
            tw.append(("q_rp", p.val("q_rp")))
        p.wait("tensor", tw)
        for Dc in range(32):
            p.step("tensor", lambda e, Dc=Dc, lc=lc: e.matmul(
                bank(4), lc[:, Dc * 128:(Dc + 1) * 128],
                wq_sb[:, Dc * DQ:(Dc + 1) * DQ],
                start=(Dc == 0), stop=(Dc == 31)))
            p.step("tensor", lambda e, Dc=Dc, lc=lc: e.matmul(
                PS[:, 5 * 512:5 * 512 + 256], lc[:, Dc * 128:(Dc + 1) * 128],
                wkv_sb[:, Dc * 256:(Dc + 1) * 256],
                start=(Dc == 0), stop=(Dc == 31)),
                sig="q_mm2" if Dc == 31 else None)
        mmv = p.val("q_mm2")
        qk_ps = PS[:, 4 * 512:4 * 512 + 640]
        p.step("vector", lambda e, ct=ct: e.tensor_tensor(
            out=ropeA.rearrange("p (h c) -> p h c", c=128),
            in0=qk_ps.rearrange("p (h c) -> p h c", c=128),
            in1=ct.unsqueeze(1).broadcast_to([128, 5, 128]), op=ALU.mult),
            waits=[("q_mm2", mmv)])
        qk4 = qk_ps.rearrange("p (h s c) -> p h s c", s=2, c=64)
        rB4 = ropeB.rearrange("p (h s c) -> p h s c", s=2, c=64)
        p.step("vector", lambda e, st_=st_: e.tensor_tensor(
            out=rB4[:, :, 0, :], in0=qk4[:, :, 1, :],
            in1=st_[:, 0:64].unsqueeze(1).broadcast_to([128, 5, 64]),
            op=ALU.mult))
        p.step("vector", lambda e, st_=st_: e.tensor_tensor(
            out=rB4[:, :, 1, :], in0=qk4[:, :, 0, :],
            in1=st_[:, 64:128].unsqueeze(1).broadcast_to([128, 5, 64]),
            op=ALU.mult))
        rv = p.step("vector", lambda e: e.tensor_tensor(
            out=qkr, in0=ropeA, in1=ropeB, op=ALU.add),
            waits=[("q_tr", p.val("q_tr"))], sig="q_rp")
        p.step("vector", lambda e, tt=tt: e.tensor_copy(
            v_sb[:, tt * 128:(tt + 1) * 128],
            PS[:, 5 * 512 + 128:5 * 512 + 256]), sig="q_rp")
        for hh in range(5):
            bnk = bankb(6 + hh % 2)
            tw2 = [("q_rp", rv), ("q_ev", p.val("q_ev") - 1)]
            tv = p.step("tensor", lambda e, hh=hh, bnk=bnk: e.transpose(
                bnk, qkr[:, hh * 128:(hh + 1) * 128], ident),
                waits=tw2, sig="q_tr")
            dst = (q_sb[:, hh * NT + tt * 128:hh * NT + (tt + 1) * 128]
                   if hh < 4 else k_sb[:, tt * 128:(tt + 1) * 128])
            p.step("vector", lambda e, dst=dst, bnk=bnk: e.tensor_copy(
                dst, bnk), waits=[("q_tr", tv)], sig="q_ev")

    QEV = p.val("q_ev")

    dq_jobs = []
    for i in range(HBAND // 128):
        for jh in range(2):
            dq_jobs.append(("w1", i, jh))
    for i in range(HBAND // 128):
        for jh in range(2):
            dq_jobs.append(("w2", i, jh))
    for i in range(D // 128):
        dq_jobs.append(("w3", i, 0))

    def deq_units():
        for i in range(4):
            cw = [("q_ev", QEV)] if i == 0 else [("wo_c", p.val("wo_c"))]
            lv = p.dma("gpsimd", cstb, wot_in[i * 128:(i + 1) * 128, :],
                       waits=cw, sig="wo_l")
            yield
            p.dma("gpsimd", b_wo[i * 128:(i + 1) * 128, :], cstb,
                  waits=[("wo_l", lv)], sig="wo_c")
            yield
        for (ui, (mat, i, jh)) in enumerate(dq_jobs):
            if mat == "w3":
                width, sigout = HBAND, "d_w3"
                src_ap = w3_in[i * 128:(i + 1) * 128, :]
            else:
                srcT = w1_in if mat == "w1" else w2_in
                width, sigout = CW, ("d_w1" if mat == "w1" else "d_w2")
                src_ap = srcT[i * 128:(i + 1) * 128, jh * CW:(jh + 1) * CW]
            G = width // 64
            nsub = width // 128
            wb = wbuf[ui % 2]
            lw = [("dq_u", p.val("dq_u") - 1)]
            if ui <= 1:
                lw.append(("q_ev", QEV))
            lv = p.dma("sync", wb[:, :width], src_ap, waits=lw, sig="dq_ld")
            yield
            amx = amax2[ui % 2]
            rcu = rcp2[ui % 2]
            p.fstep("vector", lambda e, wb=wb, amx=amx, G=G, width=width:
                    e.tensor_reduce(
                out=amx[:, :G],
                in_=wb[:, :width].rearrange("p (g k) -> p g k", k=64),
                axis=AX.X, op=ALU.max, apply_absolute_value=True),
                waits=[("dq_ld", lv), ("dq_q", p.val("dq_q") - 1)])
            yield
            p.fstep("vector", lambda e, amx=amx, rcu=rcu, G=G:
                    e.tensor_scalar_max(
                rcu[:, :G], amx[:, :G], 1e-38),
                waits=[("dq_u", p.val("dq_u") - 1)])
            rcv = p.fstep("vector", lambda e, rcu=rcu, G=G: e.reciprocal(
                rcu[:, :G], rcu[:, :G]), sig="dq_rc")
            yield
            ub = ubufs[ui % 2]
            uw = [("dq_rc", rcv), ("dq_p", p.val("dq_p") - 15)]
            if ui == 1:
                uw.append(("wo_c", p.val("wo_c")))
            uv = p.step("gpsimd", lambda e, wb=wb, ub=ub, rcu=rcu, G=G,
                        width=width: e.tensor_tensor(
                out=ub[:, :width].rearrange("p (g k) -> p g k", k=64),
                in0=wb[:, :width].rearrange("p (g k) -> p g k", k=64),
                in1=rcu[:, :G].unsqueeze(2).broadcast_to([128, G, 64]),
                op=ALU.mult),
                waits=uw, sig="dq_u")
            yield
            acu = accb[ui % 2]
            for j in range(15):
                pl = plane[j % 2]
                sw = [("dq_u", uv)]
                sw.append(("dq_f", p.val("dq_f") - (0 if j == 0 else 1)))
                sv = p.step("scalar", lambda e, pl=pl, ub=ub, j=j,
                            width=width:
                            e.activation(out=pl[:, :width],
                                         in_=ub[:, :width],
                                         func=AF.Sign, bias=nf4b[:, j:j + 1],
                                         scale=1.0),
                            waits=sw, sig="dq_p")
                if j == 0:
                    p.step("vector", lambda e, pl=pl, acu=acu, width=width:
                           e.tensor_scalar(
                        out=acu[:, :width], in0=pl[:, :width],
                        scalar1=float(NF4_DH[0]), scalar2=float(NF4_K0),
                        op0=ALU.mult, op1=ALU.add),
                        waits=[("dq_p", sv), ("dq_q", p.val("dq_q") - 1)],
                        sig="dq_f")
                else:
                    p.step("vector", lambda e, pl=pl, acu=acu,
                           dh=float(NF4_DH[j]), width=width:
                           e.scalar_tensor_tensor(
                        out=acu[:, :width], in0=pl[:, :width], scalar=dh,
                        in1=acu[:, :width], op0=ALU.mult, op1=ALU.add),
                        waits=[("dq_p", sv)], sig="dq_f")
                yield
            dqu = dqb[ui % 2]
            qv = p.step("gpsimd", lambda e, dqu=dqu, acu=acu, amx=amx,
                        G=G, width=width: e.tensor_tensor(
                out=dqu[:, :width].rearrange("p (g k) -> p g k", k=64),
                in0=acu[:, :width].rearrange("p (g k) -> p g k", k=64),
                in1=amx[:, :G].unsqueeze(2).broadcast_to([128, G, 64]),
                op=ALU.mult),
                waits=[("dq_f", p.val("dq_f")), ("dq_s", p.val("dq_s") - 1)],
                sig="dq_q")
            yield
            p.dma("sync", d_dq[ui, :, :width], dqu[:, :width],
                  waits=[("dq_q", qv)], sig="dq_s")
            yield

    _gen = deq_units()
    _pct = [0]

    def pump(n):
        for _ in range(n):
            if next(_gen, "END") == "END":
                break

    def pumpa():
        _pct[0] += 1
        pump(1 + (_pct[0] % 2))


    p.wait("scalar", [("q_ev", p.val("q_ev"))])
    p.wait("tensor", [("q_ev", p.val("q_ev"))])
    p.wait("vector", [("q_ev", p.val("q_ev"))])
    p.step("vector", lambda e: e.memset(nrm, 0.0), sig="nm_v")

    def vec_norm_max(src, n512, out_idx):
        for kk in range(n512):
            sv = p.step("scalar", lambda e, kk=kk, src=src: e.activation(
                out=sqb, in_=src[:, kk * 512:(kk + 1) * 512], func=AF.Square),
                waits=[("nm_t", p.val("nm_t"))], sig="nm_a")
            bi = 4 + kk % 2
            mv = p.step("tensor", lambda e, bi=bi: e.matmul(
                PS[0:1, bi * 512:bi * 512 + 512], ones_b, sqb,
                start=True, stop=True),
                waits=[("nm_a", sv), ("nm_v", p.val("nm_v"))], sig="nm_t")
            p.fstep("vector", lambda e, bi=bi: e.tensor_reduce(
                out=nrm[0:1, 8:9], in_=PS[0:1, bi * 512:bi * 512 + 512]
                .rearrange("p (a c) -> p a c", a=1), axis=AX.X, op=ALU.max),
                waits=[("nm_t", mv)])
            p.step("vector", lambda e, oi=out_idx: e.tensor_tensor(
                out=nrm[0:1, oi:oi + 1], in0=nrm[0:1, oi:oi + 1],
                in1=nrm[0:1, 8:9], op=ALU.max), sig="nm_v")

    vec_norm_max(k_sb, NT // 512, 0)
    for hh in range(4):
        vec_norm_max(q_sb[:, hh * NT:(hh + 1) * NT], NT // 512, 1)
    p.fstep("vector", lambda e: e.tensor_tensor(
        out=nrm[0:1, 2:3], in0=nrm[0:1, 0:1], in1=nrm[0:1, 1:2], op=ALU.mult))
    sv = p.fstep("vector", lambda e: e.tensor_copy(nrm[0:1, 3:4], nrm[0:1, 2:3]),
                 sig="nm_v")
    av = p.step("scalar", lambda e: e.activation(
        out=nrm[0:1, 4:5], in_=nrm[0:1, 3:4], func=AF.Sqrt),
        waits=[("nm_v", sv)], sig="nm_a")
    bv = p.step("vector", lambda e: e.tensor_scalar(
        out=nrm[0:1, 5:6], in0=nrm[0:1, 4:5], scalar1=-SCALE, scalar2=-2.0,
        op0=ALU.mult, op1=ALU.add), waits=[("nm_a", av)], sig="nm_v")
    tv = p.step("tensor", lambda e: e.matmul(
        PS[:, 4 * 512:4 * 512 + 1], ones_f, nrm[0:1, 5:6],
        start=True, stop=True), waits=[("nm_v", bv)], sig="nm_t")
    bcv = p.step("vector", lambda e: e.tensor_copy(
        bias_sb, PS[:, 4 * 512:4 * 512 + 1]), waits=[("nm_t", tv)], sig="nm_v")
    p.wait("scalar", [("nm_v", bcv)])
    p.wait("tensor", [("nm_v", bcv)])

    def emit_sc(b, h, q0, kt):
        ps_s = bank(kt % 4)
        xv_now = p.val("at_x")
        mw = [("at_x", xv_now - 3)]
        dtile = kt - 4 * ((q0 % T) // 512)
        mv = p.step("tensor", lambda e, kt=kt, b=b, q0=q0, h=h, ps_s=ps_s,
                    dt_=dtile: e.matmul(
            ps_s, k_sb[:, b * T + kt * 128:b * T + (kt + 1) * 128],
            q_sb[:, h * NT + q0:h * NT + q0 + 512],
            start=True, stop=(dt_ < 0)), waits=mw,
            sig=None if dtile >= 0 else "at_mm")
        if dtile >= 0:
            mv = p.step("tensor", lambda e, ps_s=ps_s, dt_=dtile: e.matmul(
                ps_s, ident, masks[:, dt_, :], start=False, stop=True),
                sig="at_mm")
        ew = [("at_mm", mv)]
        ab_ = attb[kt % 4]
        ew.append(("at_av", p.val("at_av")))
        p.step("scalar", lambda e, ps_s=ps_s, ab_=ab_:
               e.activation(out=ab_, in_=ps_s, func=AF.Exp,
                            bias=bias_sb[:, 0:1], scale=SCALE),
               waits=ew, sig="at_x")

    for b in range(B):
        for h in range(NQH):
            for s in range(8):
                q0 = b * T + s * 512
                nk = 4 * (s + 1)
                xbase = p.val("at_x")
                for j in range(min(3, nk)):
                    emit_sc(b, h, q0, j)
                av_b, sm_b = 4 + 2 * (s % 2), 5 + 2 * (s % 2)
                for kt in range(nk):
                    ab_ = attb[kt % 4]
                    aw = [("at_x", xbase + kt + 1)]
                    if kt == 0:
                        aw.append(("at_e", p.val("at_e") - 1))
                    p.step("tensor", lambda e, kt=kt, b=b, ab_=ab_, nk=nk,
                           av_b=av_b: e.matmul(
                        bank(av_b),
                        v_sb[:, b * T + kt * 128:b * T + (kt + 1) * 128], ab_,
                        start=(kt == 0), stop=(kt == nk - 1)),
                        waits=aw)
                    sw0 = [("at_rc", p.val("at_rc") - 1)] if kt == 0 else []
                    p.step("tensor", lambda e, kt=kt, ab_=ab_, nk=nk,
                           sm_b=sm_b: e.matmul(
                        PS[0:1, sm_b * 512:sm_b * 512 + 512], ones_b, ab_,
                        start=(kt == 0), stop=(kt == nk - 1)),
                        waits=sw0, sig="at_av")
                    if kt + 3 < nk:
                        emit_sc(b, h, q0, kt + 3)
                    pumpa()
                avc = p.val("at_av")
                rw = [("at_av", avc), ("at_r1", p.val("at_r1") - 16)]
                rcv2 = p.step("vector", lambda e, sm_b=sm_b: e.reciprocal(
                    rs_sm, PS[0:1, sm_b * 512:sm_b * 512 + 512]),
                    waits=rw, sig="at_rc")
                rrow = (b * NQH + h) * 8 + s
                r1 = p.dma("sync", d_rr[rrow:rrow + 1, :], rs_sm,
                           waits=[("at_rc", rcv2)], sig="at_r1")
                bcv = p.dma("sync", rbb[s % 2],
                            d_rr[rrow:rrow + 1, :].partition_broadcast(128),
                            waits=[("at_r1", r1),
                                   ("at_e", p.val("at_e") - 1)], sig="at_r2")
                ost = otst[s % 2]
                ew2 = [("at_r2", bcv), ("at_st", p.val("at_st") - 16)]
                ev = p.step("vector", lambda e, ost=ost, av_b=av_b, s=s:
                            e.tensor_tensor(out=ost, in0=bank(av_b),
                                            in1=rbb[s % 2], op=ALU.mult),
                            waits=ew2, sig="at_e")
                p.dma("sync", b_oT[h * 128:(h + 1) * 128, q0:q0 + 512], ost,
                      waits=[("at_e", ev)], sig="at_st")

    pump(10 ** 9)

    ar.reset(m0)
    a_n = ar.alloc([128, 4 * NT], BF16)
    yst = [ar.alloc([128, 512], BF16) for _ in range(2)]
    wosp = [ar.alloc([128, 4, 512], BF16) for _ in range(2)]
    b_yp = dint("b_yp", [NT, D], BF16)
    b_ys = dint("b_ys", [TLOC, D], BF16)

    ATST = p.val("at_st")
    for hc in range(4):
        p.dma("sync", a_n[:, hc * NT:(hc + 1) * NT],
              b_oT[hc * 128:(hc + 1) * 128, :],
              waits=[("at_st", ATST)], sig="p4_n")
    p.wait("tensor", [("p4_n", p.val("p4_n"))])
    a_nv = a_n.rearrange("p (g t) -> p g t", t=NT)
    b_wov = b_wo.rearrange("(a k) c -> k a c", k=128)
    for ds in range(8):
        wsp = wosp[ds % 2]
        lw = [("wo_m", p.val("wo_m") - 64)] if ds >= 2 else \
            [("wo_c", p.val("wo_c"))]
        lv = p.dma("sync", wsp, b_wov[:, :, ds * 512:(ds + 1) * 512],
                   waits=lw, sig="wo_ld")
        p.wait("tensor", [("wo_ld", lv)])
        for tt in range(64):
            bnk = tt % 8
            mw = [("wo_e", p.val("wo_e") - 7)]
            for hc in range(4):
                p.step("tensor", lambda e, hc=hc, tt=tt, bnk=bnk, wsp=wsp:
                       e.matmul(bank(bnk),
                                a_nv[:, hc, tt * 128:(tt + 1) * 128],
                                wsp[:, hc, :], start=(hc == 0),
                                stop=(hc == 3)),
                       waits=(mw if hc == 0 else []),
                       sig="wo_m" if hc == 3 else None)
            mvw = p.val("wo_m")
            ys = yst[tt % 2]
            ev = p.step("vector", lambda e, ys=ys, bnk=bnk: e.tensor_copy(
                ys, bank(bnk)),
                waits=[("wo_m", mvw), ("wo_st", p.val("wo_st") - 16)],
                sig="wo_e")
            p.dma("sync",
                  b_yp[tt * 128:(tt + 1) * 128, ds * 512:(ds + 1) * 512],
                  ys, waits=[("wo_e", ev)], sig="wo_st")

    RSY = p.step("gpsimd", lambda e: e.collective_compute(
        "ReduceScatter", ALU.add, replica_groups=[list(range(NC))],
        ins=[b_yp], outs=[b_ys]),
        waits=[("wo_st", p.val("wo_st"))], sig="cc")

    ATX = p.val("at_x")
    DQS = p.val("dq_s")
    WOE = p.val("wo_e")
    arS = Arena(nc, mSh, SB_TOP, "st")
    ldb4 = [arS.alloc([128, CW], BF16) for _ in range(4)]
    tst4 = [arS.alloc([128, 16 * 128], BF16) for _ in range(4)]
    tv_after = {}
    st_after = {}
    gcnt = 0
    for ui, (mat, i, jh) in enumerate(dq_jobs):
        width = HBAND if mat == "w3" else CW
        nsub = width // 128
        ldb = ldb4[ui % 4]
        lw = [("dq_s", DQS)]
        if ui >= 4:
            lw.append(("t2_t", tv_after[ui - 4]))
        lv = p.dma("sync", ldb[:, :width], d_dq[ui, :, :width],
                   waits=lw, sig="t2_l")
        ts = tst4[ui % 4]
        for g0 in range(0, nsub, 8):
            gn = min(8, nsub - g0)
            bkid = gcnt % 4
            tv = None
            for k in range(g0, g0 + gn):
                tw = [("t2_l", lv)]
                if k == g0:
                    if gcnt == 0:
                        tw.append(("at_x", ATX))
                        tw.append(("wo_e", WOE))
                    if gcnt >= 4:
                        tw.append(("t2_e", p.val("t2_e") - 3))
                tv = p.step("tensor", lambda e, k=k, g0=g0, bkid=bkid,
                            ldb=ldb: e.transpose(
                    PSB[:, bkid * 1024 + (k - g0) * 128:
                        bkid * 1024 + (k - g0 + 1) * 128],
                    ldb[:, k * 128:(k + 1) * 128], ident),
                    waits=tw, sig="t2_t")
            ew = [("t2_t", tv)]
            if g0 == 0 and ui >= 4:
                ew.append(st_after[ui - 4])
            p.step("scalar", lambda e, g0=g0, gn=gn, bkid=bkid, ts=ts:
                   e.copy(ts[:, g0 * 128:(g0 + gn) * 128],
                          PSB[:, bkid * 1024:bkid * 1024 + gn * 128]),
                   waits=ew, sig="t2_e")
            gcnt += 1
        tv_after[ui] = p.val("t2_t")
        if mat == "w3":
            dst_ap = b_w3[i // 4, :, :, (i % 4) * 128:(i % 4 + 1) * 128]
            sigout = "d_w3"
        else:
            dstT = b_w1 if mat == "w1" else b_w2
            dst_ap = dstT[:, i, jh * 16:(jh + 1) * 16, :]
            sigout = "d_w1" if mat == "w1" else "d_w2"
        p.dma("sync", dst_ap,
              ts[:, :nsub * 128].rearrange("p (k r) -> p k r", r=128),
              waits=[("t2_e", p.val("t2_e"))], sig=sigout)
        st_after[ui] = (sigout, p.val(sigout))

    agw1 = p.step("gpsimd", lambda e: e.collective_compute(
        "AllGather", ALU.bypass, replica_groups=[list(range(NC))],
        ins=[b_w1.rearrange("p a b c -> (p a b) c")],
        outs=[g_w1.rearrange("n p a b c -> (n p a b) c")]),
        waits=[("d_w1", p.val("d_w1"))], sig="cc")
    agw2 = p.step("gpsimd", lambda e: e.collective_compute(
        "AllGather", ALU.bypass, replica_groups=[list(range(NC))],
        ins=[b_w2.rearrange("p a b c -> (p a b) c")],
        outs=[g_w2.rearrange("n p a b c -> (n p a b) c")]),
        waits=[("d_w2", p.val("d_w2"))], sig="cc")
    agw3 = p.step("gpsimd", lambda e: e.collective_compute(
        "AllGather", ALU.bypass, replica_groups=[list(range(NC))],
        ins=[b_w3.rearrange("a p b c -> (a p b) c")],
        outs=[g_w3.rearrange("n a p b c -> (n a p b) c")]),
        waits=[("d_w3", p.val("d_w3"))], sig="cc")

    ar.reset(m0)
    xb4 = [ar.alloc([128, D], F32) for _ in range(2)]
    yb4 = [ar.alloc([128, D], BF16) for _ in range(2)]
    scr4 = ar.alloc([128, D], F32)
    x1b = ar.alloc([128, D], F32)
    lnb2 = ar.alloc([128, D], BF16)
    tst2 = ar.alloc([128, 32 * 128], BF16)

    p.wait("sync", [("cc", RSY)])
    for i in range(8):
        y_t = yb4[i % 2]
        lw = [("p4_q", p.val("p4_q") - 2)] if i >= 2 else []
        lv = p.dma("sync", y_t, b_ys[i * 128:(i + 1) * 128, :],
                   waits=lw, sig="p4_l")
        p.step("vector", lambda e, y_t=y_t, i=i: e.tensor_reduce(
            out=mean8[:, i:i + 1],
            in_=y_t.rearrange("p (a c) -> p a c", a=1),
            axis=AX.X, op=ALU.add), waits=[("p4_l", lv)], sig="p4_q")
        p.step("scalar", lambda e, y_t=y_t, i=i: e.activation(
            out=scr4, in_=y_t, func=AF.Square,
            accum_out=msq8[:, i:i + 1]), waits=[("p4_l", lv)], sig="p4_q")

    p.wait("vector", [("p4_q", p.val("p4_q"))])
    p.fstep("vector", lambda e: e.tensor_scalar_mul(mean8, mean8, 1.0 / D))
    p.fstep("vector", lambda e: e.tensor_scalar_mul(msq8, msq8, 1.0 / D))
    p.fstep("vector", lambda e: e.tensor_tensor(
        out=sm2[:, 0:8], in0=mean8, in1=mean8, op=ALU.mult))
    p.fstep("vector", lambda e: e.tensor_tensor(
        out=msq8, in0=msq8, in1=sm2[:, 0:8], op=ALU.subtract))
    vv = p.step("vector", lambda e: e.tensor_scalar_add(msq8, msq8, EPS),
                sig="p4_v")
    sv = p.step("scalar", lambda e: e.activation(
        out=str_, in_=msq8, func=AF.Sqrt), waits=[("p4_v", vv)], sig="p4_a")
    p.fstep("vector", lambda e: e.reciprocal(str_, str_), waits=[("p4_a", sv)])
    p.fstep("vector", lambda e: e.tensor_scalar_mul(stm, mean8, -1.0))

    qv4 = p.val("p4_q")
    for i in range(8):
        x_t, y_t = xb4[i % 2], yb4[i % 2]
        lw = [("p4_x", p.val("p4_x") - 1)] if i >= 2 else []
        p.dma("sync", x_t, x_in[i * 128:(i + 1) * 128, :],
              waits=[("p4_q", qv4)] + lw, sig="p4_l2")
        lv2 = p.dma("sync", y_t, b_ys[i * 128:(i + 1) * 128, :], sig="p4_l2")
        p.wait("vector", [("p4_l2", lv2)])
        p.step("vector", lambda e, y=y_t, i=i: e.tensor_scalar(
            out=scr4, in0=y, scalar1=stm[:, i:i + 1], scalar2=str_[:, i:i + 1],
            op0=ALU.add, op1=ALU.mult))
        x1v = p.step("vector", lambda e, x=x_t: e.tensor_tensor(
            out=x1b, in0=scr4, in1=x, op=ALU.add),
            waits=[("p4_s", p.val("p4_s") - 16)], sig="p4_x")
        p.dma("sync", s_x1[i * 128:(i + 1) * 128, :], x1b,
              waits=[("p4_x", x1v)], sig="p4_s")
        ln_prep(x1b)
        lnv = p.step("vector", lambda e: e.tensor_scalar(
            out=lnb2, in0=x1b, scalar1=sm2[:, 4:5], scalar2=sm2[:, 5:6],
            op0=ALU.add, op1=ALU.mult),
            waits=[("p4_t", p.val("p4_t"))], sig="ln_v")
        for k in range(32):
            bnk = bankb(k % 2)
            tw = [("ln_v", lnv)]
            if k >= 2:
                tw.append(("p4_e", p.val("p4_e") - 1))
            else:
                tw.append(("p4_s", p.val("p4_s")))
            tv = p.step("tensor", lambda e, k=k, bnk=bnk: e.transpose(
                bnk, lnb2[:, k * 128:(k + 1) * 128], ident),
                waits=tw, sig="p4_t")
            p.step("vector", lambda e, k=k, bnk=bnk: e.tensor_copy(
                tst2[:, k * 128:(k + 1) * 128], bnk),
                waits=[("p4_t", tv)], sig="p4_e")
        p.dma("sync", s_ln2.rearrange("(k q) r -> q k r", q=128)
              [:, :, i * 128:(i + 1) * 128],
              tst2.rearrange("p (k r) -> p k r", r=128),
              waits=[("p4_e", p.val("p4_e"))], sig="p4_s")

    ar.reset(m0)
    h_sb = ar.alloc([128, 88 * TCH], BF16)
    ln2c = ar.alloc([128, 32 * TCH], BF16)
    w1c = [ar.alloc([128, 32 * 128], BF16) for _ in range(2)]
    w2c = [ar.alloc([128, 32 * 128], BF16) for _ in range(2)]
    w3bb = [ar.alloc([128, (HBAND // 128) * 512], BF16) for _ in range(2)]
    sgb = ar.alloc([128, TCH], F32)
    sgb2 = ar.alloc([128, TCH], F32)
    y2st = [ar.alloc([128, 512], F32) for _ in range(2)]
    y2sq = sgb

    p.step("vector", lambda e: e.memset(mean8, 0.0))
    p.step("vector", lambda e: e.memset(msq8, 0.0), sig="f_z")
    p.wait("scalar", [("f_z", p.val("f_z"))])
    p.wait("sync", [("p4_s", p.val("p4_s")), ("cc", agw1),
                    ("d_w3", p.val("d_w3"))])
    for ci in range(2):
        lnw = [("f_g", p.val("f_g"))] if ci else []
        lv = p.dma("sync", ln2c.rearrange("p (k c) -> p k c", c=TCH),
                   s_ln2.rearrange("(k p) c -> p k c", p=128)
                   [:, :, ci * TCH:(ci + 1) * TCH], waits=lnw, sig="f_l")
        p.wait("tensor", [("f_l", lv), ("cc", agw2)])
        hs_guard = p.val("f_dm")
        for ht in range(88):
            bandi, hl = ht // 11, (ht % 11) * 128
            wa, wb_ = w1c[ht % 2], w2c[ht % 2]
            lw = [("f_su", p.val("f_su") - 1)] if (ci or ht >= 2) else []
            p.dma("sync", wa.rearrange("p (k c) -> p k c", c=128),
                  g_w1[bandi][:, ht % 11, :, :], waits=lw, sig="f_wl")
            w2w = [("cc", agw2)] if (ci == 0 and ht == 0) else []
            wlv = p.dma("sync", wb_.rearrange("p (k c) -> p k c", c=128),
                        g_w2[bandi][:, ht % 11, :, :], waits=w2w, sig="f_wl")
            bg, bu = (0, 1) if ht % 2 == 0 else (6, 7)
            sgx = sgb if ht % 2 == 0 else sgb2
            gw = [("f_wl", wlv), ("f_su", p.val("f_su") - 1)]
            p.wait("tensor", gw)
            for Dc in range(32):
                p.step("tensor", lambda e, Dc=Dc, wa=wa, bg=bg: e.matmul(
                    bank(bg), wa[:, Dc * 128:(Dc + 1) * 128],
                    ln2c[:, Dc * TCH:(Dc + 1) * TCH],
                    start=(Dc == 0), stop=(Dc == 31)))
                p.step("tensor", lambda e, Dc=Dc, wb_=wb_, bu=bu: e.matmul(
                    bank(bu), wb_[:, Dc * 128:(Dc + 1) * 128],
                    ln2c[:, Dc * TCH:(Dc + 1) * TCH],
                    start=(Dc == 0), stop=(Dc == 31)),
                    sig="f_g" if Dc == 31 else None)
            gv = p.val("f_g")
            sv = p.step("scalar", lambda e, bg=bg, sgx=sgx: e.activation(
                out=sgx, in_=bank(bg), func=AF.Silu), waits=[("f_g", gv)],
                sig="f_sl")
            hw = [("f_sl", sv)]
            if ci and ht == 0:
                hw.append(("f_dm", hs_guard))
            p.step("vector", lambda e, ht=ht, bu=bu, sgx=sgx: e.tensor_tensor(
                out=h_sb[:, ht * TCH:(ht + 1) * TCH], in0=sgx, in1=bank(bu),
                op=ALU.mult), waits=hw, sig="f_su")
        huv = p.val("f_su")
        p.wait("tensor", [("f_su", huv)])
        gidx = 0
        for ds in range(8):
            for bandi in range(8):
                wbb = w3bb[gidx % 2]
                lw = [("f_dm", p.val("f_dm") - 11)] if (ci or gidx >= 2) \
                    else [("cc", agw3)]
                wlv = p.dma("sync", wbb.rearrange("p (a c) -> p a c", c=512),
                            g_w3[bandi, ds], waits=lw, sig="f_wl")
                p.wait("tensor", [("f_wl", wlv)])
                if bandi == 0:
                    p.wait("tensor", [("f_de", p.val("f_de"))])
                for htl in range(11):
                    ht = bandi * 11 + htl
                    for ti in range(4):
                        p.step("tensor", lambda e, ht=ht, ti=ti, wbb=wbb,
                               htl=htl: e.matmul(
                            bank(2 + ti),
                            h_sb[:, ht * TCH + ti * 128:
                                 ht * TCH + (ti + 1) * 128],
                            wbb[:, htl * 512:(htl + 1) * 512],
                            start=(ht == 0), stop=(ht == 87)),
                            sig="f_dm" if ti == 3 else None)
                gidx += 1
            dmv = p.val("f_dm")
            for ti in range(4):
                tt2 = ci * 4 + ti
                ys = y2st[ti % 2]
                ew = [("f_dm", dmv)]
                ew.append(("f_st", p.val("f_st") - 16))
                ev = p.step("vector", lambda e, ti=ti, ys=ys: e.tensor_copy(
                    ys, bank(2 + ti)), waits=ew, sig="f_de")
                p.fstep("vector", lambda e, ys=ys: e.tensor_reduce(
                    out=sm2[:, 6:7], in_=ys.rearrange("p (a c) -> p a c", a=1),
                    axis=AX.X, op=ALU.add))
                p.fstep("vector", lambda e, tt2=tt2: e.tensor_tensor(
                    out=mean8[:, tt2:tt2 + 1], in0=mean8[:, tt2:tt2 + 1],
                    in1=sm2[:, 6:7], op=ALU.add))
                sv = p.step("scalar", lambda e, ys=ys: e.activation(
                    out=y2sq, in_=ys, func=AF.Square, accum_out=sm2[:, 7:8]),
                    waits=[("f_de", ev), ("f_v2", p.val("f_v2"))], sig="f_a")
                p.step("vector", lambda e, tt2=tt2: e.tensor_tensor(
                    out=msq8[:, tt2:tt2 + 1], in0=msq8[:, tt2:tt2 + 1],
                    in1=sm2[:, 7:8], op=ALU.add), waits=[("f_a", sv)],
                    sig="f_v2")
                p.wait("vector", [("f_v2", p.val("f_v2"))])
                p.dma("sync", s_y2[tt2 * 128:(tt2 + 1) * 128,
                                   ds * 512:(ds + 1) * 512], ys,
                      waits=[("f_de", ev)], sig="f_st")

    p.wait("vector", [("f_v2", p.val("f_v2"))])
    p.fstep("vector", lambda e: e.tensor_scalar_mul(mean8, mean8, 1.0 / D))
    p.fstep("vector", lambda e: e.tensor_scalar_mul(msq8, msq8, 1.0 / D))
    p.fstep("vector", lambda e: e.tensor_tensor(
        out=sm2[:, 0:8], in0=mean8, in1=mean8, op=ALU.mult))
    p.fstep("vector", lambda e: e.tensor_tensor(
        out=msq8, in0=msq8, in1=sm2[:, 0:8], op=ALU.subtract))
    vv = p.step("vector", lambda e: e.tensor_scalar_add(msq8, msq8, EPS),
                sig="f_v")
    sv = p.step("scalar", lambda e: e.activation(
        out=str_, in_=msq8, func=AF.Sqrt), waits=[("f_v", vv)], sig="f_a2")
    p.fstep("vector", lambda e: e.reciprocal(str_, str_), waits=[("f_a2", sv)])
    p.fstep("vector", lambda e: e.tensor_scalar_mul(stm, mean8, -1.0))

    ar.reset(m0)
    xb5 = [ar.alloc([128, D], F32) for _ in range(2)]
    yb5 = [ar.alloc([128, D], F32) for _ in range(2)]
    scr5 = ar.alloc([128, D], F32)
    fout = ar.alloc([128, D], F32)

    fv = p.val("f_st")
    for i in range(8):
        x_t, y_t = xb5[i % 2], yb5[i % 2]
        lw = [("fin_o", p.val("fin_o") - 1)] if i >= 2 else []
        p.dma("sync", x_t, s_x1[i * 128:(i + 1) * 128, :],
              waits=[("f_st", fv)] + lw, sig="fin_l")
        lv2 = p.dma("sync", y_t, s_y2[i * 128:(i + 1) * 128, :], sig="fin_l")
        p.wait("vector", [("fin_l", lv2)])
        p.step("vector", lambda e, y=y_t, i=i: e.tensor_scalar(
            out=scr5, in0=y, scalar1=stm[:, i:i + 1], scalar2=str_[:, i:i + 1],
            op0=ALU.add, op1=ALU.mult))
        ov = p.step("vector", lambda e, x=x_t: e.tensor_tensor(
            out=fout, in0=scr5, in1=x, op=ALU.add),
            waits=[("fin_s", p.val("fin_s"))], sig="fin_o")
        p.dma("sync", out_d[i * 128:(i + 1) * 128, :], fout,
              waits=[("fin_o", ov)], sig="fin_s")

    if DEBUG:
        dbg = {
            "dbg_oT": b_oT, "dbg_y": b_ys, "dbg_x1": s_x1,
            "dbg_ln2": s_ln2, "dbg_y2": s_y2,
        }
        for nm, src_t in dbg.items():
            od2 = nc.dram_tensor(nm, list(src_t.tensor.shape),
                                 src_t.dtype, kind="ExternalOutput").ap()
            p.dma("sync", od2, src_t, waits=[("fin_s", p.val("fin_s"))],
                  sig="dbg_s")
        dsb = nc.dram_tensor("dbg_sb", [128, 64], F32,
                             kind="ExternalOutput").ap()
        dsb_st = ar.alloc([128, 64], F32)
        p.step("vector", lambda e: e.tensor_copy(dsb_st[:, 0:1], bias_sb),
               waits=[("fin_o", p.val("fin_o"))])
        p.step("vector", lambda e: e.tensor_copy(dsb_st[:, 1:17], sm2))
        p.step("vector", lambda e: e.tensor_copy(dsb_st[:, 17:25], mean8))
        p.step("vector", lambda e: e.tensor_copy(dsb_st[:, 25:33], msq8))
        p.step("vector", lambda e: e.tensor_copy(dsb_st[:, 33:41], stm))
        dv_ = p.step("vector", lambda e: e.tensor_copy(dsb_st[:, 41:49], str_),
                     sig="dbg_v")
        p.step("vector", lambda e: e.tensor_copy(
            dsb_st[0:1, 48:64], nrm), sig="dbg_v")
        p.dma("sync", dsb, dsb_st, waits=[("dbg_v", p.val("dbg_v"))],
              sig="dbg_s")
        p.wait("sync", [("dbg_s", p.val("dbg_s"))])
    p.wait("sync", [("fin_s", p.val("fin_s"))])
    return nc, p


_CACHE = {}


def _get_built():
    if "nc" not in _CACHE:
        nc, p = build_program()
        with nc.Block() as blk:
            for name in Prog.ENGS:
                fns = p.q[name]

                def body(e, fns=fns):
                    for f in fns:
                        f(e)
                getattr(blk, name)(body)
        _CACHE["nc"] = nc
    return _CACHE["nc"]


def kernel(L0_attn_x, L0_attn_Wq, L0_attn_Wk, L0_attn_Wv, L0_attn_Wo,
           L0_ffn_w1, L0_ffn_w2, L0_ffn_w3):
    x = np.ascontiguousarray(np.asarray(L0_attn_x, np.float32).reshape(NT, D))
    WqT = np.ascontiguousarray(np.asarray(L0_attn_Wq, np.float32).T)
    WkT = np.ascontiguousarray(np.asarray(L0_attn_Wk, np.float32).T)
    WvT = np.ascontiguousarray(np.asarray(L0_attn_Wv, np.float32).T)
    WoT = np.ascontiguousarray(np.asarray(L0_attn_Wo, np.float32).T)
    w1p = np.zeros((HPAD, D), np.float32)
    w1p[:H] = np.asarray(L0_ffn_w1, np.float32)
    w2p = np.zeros((HPAD, D), np.float32)
    w2p[:H] = np.asarray(L0_ffn_w2, np.float32)
    w3p = np.zeros((D, HPAD), np.float32)
    w3p[:, :H] = np.asarray(L0_ffn_w3, np.float32)

    cos_t, sinS_t = _rope_tables()
    kk = np.arange(128)[:, None]
    qq = np.arange(512)[None, :]
    mask_t = np.stack([np.where(i * 128 + kk <= qq, 0.0, MVAL)
                       for i in range(4)]).astype(ml_dtypes.bfloat16)
    ident_t = np.eye(128, dtype=ml_dtypes.bfloat16)
    nf4b_t = np.tile(-NF4_B, (128, 1)).astype(np.float32)

    nc = _get_built()
    in_maps = []
    for c in range(NC):
        in_maps.append(dict(
            x_c=np.ascontiguousarray(x[c * TLOC:(c + 1) * TLOC]),
            wqt_c=np.ascontiguousarray(WqT[:, c * DQ:(c + 1) * DQ]),
            wkt_c=np.ascontiguousarray(WkT[:, c * HD:(c + 1) * HD]),
            wvt_c=np.ascontiguousarray(WvT[:, c * HD:(c + 1) * HD]),
            wot_c=np.ascontiguousarray(WoT[c * DQ:(c + 1) * DQ, :]),
            w1_c=np.ascontiguousarray(w1p[c * HBAND:(c + 1) * HBAND]),
            w2_c=np.ascontiguousarray(w2p[c * HBAND:(c + 1) * HBAND]),
            w3_c=np.ascontiguousarray(w3p[:, c * HBAND:(c + 1) * HBAND]),
            cos_t=cos_t, sinS_t=sinS_t, mask_t=mask_t, ident_t=ident_t,
            nf4b_t=nf4b_t,
        ))
    res = run_bass_kernel_spmd(nc, in_maps, core_ids=list(range(NC)))
    out = np.concatenate([res.results[c]["out_c"] for c in range(NC)], axis=0)
    if DEBUG:
        return out.reshape(B, T, D), res
    return out.reshape(B, T, D)
